# version 20
# baseline (speedup 1.0000x reference)
import contextlib
import numpy as np
import ml_dtypes
import concourse.bass as bass
import concourse.mybir as mybir
from concourse.bass_utils import run_bass_kernel_spmd

F32 = mybir.dt.float32
BF16 = mybir.dt.bfloat16
AF = mybir.ActivationFunctionType
ALU = mybir.AluOpType
AX = mybir.AxisListType

N_CORES = 8


class Cfg:
    def __init__(self, T=2048, D=2048, DFF=8192):
        assert T % 512 == 0 and D % 512 == 0 and DFF % 512 == 0
        self.T, self.D, self.DFF = T, D, DFF
        self.NT = T // 128
        self.ND = D // 128
        self.NE = 3 * self.ND
        self.NO = DFF // 128
        self.NPC = (D // 4) // 128
        self.NQ = 4
        assert self.NO % self.NQ == 0
        self.NOQ = self.NO // self.NQ
        self.NTB = T // 512
        self.NDB = D // 512


class St:
    def __init__(self, nc, stack):
        self.nc = nc
        self.sems = {}
        self.cnt = {}
        self.waited = {}
        for name in ("sQ", "sG", "sP", "sV", "sA", "sW", "sS", "sU"):
            self.sems[name] = stack.enter_context(nc.semaphore(name))
            self.cnt[name] = 0

    def inc(self, name, inst):
        if name in ("sQ", "sG", "sS"):
            inst.then_inc(self.sems[name], 16)
        else:
            inst.then_inc(self.sems[name], 1)
        self.cnt[name] += 1

    def snap(self):
        return dict(self.cnt)

    def wait(self, eng, engname, name, val):
        if val <= 0:
            return
        key = (engname, name)
        if self.waited.get(key, 0) >= val:
            return
        eng.wait_ge(self.sems[name], val)
        self.waited[key] = val

    def fence(self, eng, engname, inst):
        self.inc("sW", inst)
        self.wait(eng, engname, "sW", self.cnt["sW"])

    def check(self, base, **expect):
        for k, v in expect.items():
            got = self.cnt[k] - base[k]
            assert got == v, f"sem {k}: planned {v}, emitted {got}"


def _preload_tables(scalar, nc, *funcs):
    from concourse.hw_specs import get_activation_tables
    for idx, (_name, fs) in enumerate(get_activation_tables(nc.m.arch).items()):
        if all(f in fs for f in funcs):
            return scalar.add_instruction(mybir.InstLoadActFuncSet(
                name=nc.get_next_instruction_name(), act_func_set_id=idx,
                ins=[], outs=[]))
    raise ValueError(f"no activation table set contains {funcs}")


def build_program(c: Cfg):
    nc = bass.Bass("TRN2", target_bir_lowering=False, detect_race_conditions=False)
    T, D, DFF = c.T, c.D, c.DFF
    NT, ND, NE, NO, NPC = c.NT, c.ND, c.NE, c.NO, c.NPC
    NQ, NOQ, NTB, NDB = c.NQ, c.NOQ, c.NTB, c.NDB

    x_h = nc.dram_tensor("x", [T, D], F32, kind="ExternalInput").ap()
    wqkv_h = nc.dram_tensor("w_qkv_t", [NE, 128, D], F32, kind="ExternalInput").ap()
    bqkv_h = nc.dram_tensor("bqkv_t", [128, NE], F32, kind="ExternalInput").ap()
    wout_h = nc.dram_tensor("w_out_t", [D, D], F32, kind="ExternalInput").ap()
    bout_h = nc.dram_tensor("b_out_row", [1, D], F32, kind="ExternalInput").ap()
    wg_h = nc.dram_tensor("w_gate_t", [NO, 128, D], F32, kind="ExternalInput").ap()
    wu_h = nc.dram_tensor("w_up_t", [NO, 128, D], F32, kind="ExternalInput").ap()
    wd_h = nc.dram_tensor("w_down_t", [NO, 128, D], F32, kind="ExternalInput").ap()
    cos_h = nc.dram_tensor("cos_t", [NPC, 128, T], F32, kind="ExternalInput").ap()
    sin_h = nc.dram_tensor("sin_t", [NPC, 128, T], F32, kind="ExternalInput").ap()
    eyef_h = nc.dram_tensor("eyef", [128, 128], F32, kind="ExternalInput").ap()

    out_h = nc.dram_tensor("out", [T, D], F32, kind="ExternalOutput").ap()
    qpark_h = nc.dram_tensor("qpark", [ND, 128, T], BF16, kind="Internal").ap()
    ypark_h = nc.dram_tensor("ypark", [ND, 128, T], BF16, kind="Internal").ap()
    hscr_h = nc.dram_tensor("hscr", [NO, 128, T], F32, kind="Internal").ap()
    hq_hbm = nc.dram_tensor("hq_hbm", [NO, 128, T], BF16, kind="Internal").ap()

    ctx = contextlib.ExitStack()
    with ctx:
        S = St(nc, ctx)
        al = ctx.enter_context

        eyef = al(nc.sbuf_tensor("eyef_s", [128, 128], F32))
        ones = al(nc.sbuf_tensor("ones_s", [128, 128], F32))
        bq_s = al(nc.sbuf_tensor("bq_s", [128, NE], F32))
        sx_sc = al(nc.sbuf_tensor("sx_sc", [128, NT], F32))
        stat_st = al(nc.sbuf_tensor("stat_st", [NT, 128], F32))
        sw_b = al(nc.sbuf_tensor("sw_b", [128, 3], F32))
        rw_b = al(nc.sbuf_tensor("rw_b", [128, 3], F32))
        A_c = al(nc.sbuf_tensor("A_c", [128, ND], F32))
        cst = al(nc.sbuf_tensor("cst", [128, 4], F32))
        sh_sc = al(nc.sbuf_tensor("sh_sc", [128, NT], F32))
        rh_tok = al(nc.sbuf_tensor("rh_tok", [128, NT], F32))
        vd_tok = al(nc.sbuf_tensor("vd_tok", [128, NT], F32))
        accabs = al(nc.sbuf_tensor("accabs", [128, 3], F32))
        rtmp_all = al(nc.sbuf_tensor("rtmp_all", [128, 3 * c.NO], F32))

        xnT_stack = contextlib.ExitStack()
        xnT = xnT_stack.enter_context(nc.sbuf_tensor("xnT", [128, ND * T], BF16))

        with nc.Block("ph0") as blk:
            @blk.sync
            def _(sync):
                S.inc("sQ", sync.dma_start(out=eyef[:], in_=eyef_h))
                S.inc("sQ", sync.dma_start(out=bq_s[:], in_=bqkv_h))

            @blk.vector
            def _(vector):
                vector.memset(ones[:], 1.0)
                vector.memset(cst[:, 0:1], 1e-5)
                vector.memset(cst[:, 1:2], 0.5)
                vector.memset(cst[:, 2:3], -1.0)
                vector.memset(cst[:, 3:4], -0.5)

            @blk.scalar
            def _(scalar):
                _preload_tables(scalar, nc, AF.Ln, AF.Exp, AF.Identity, AF.Square,
                                AF.Sign)

        NG = ND // 4
        b = S.snap()
        qn = lambda tg, k: 16 * (b["sQ"] + (2 + NG) * tg + k)
        bG0 = b["sG"]
        NSL = 3 * NO
        SPT = (NSL + NT - 1) // NT
        vn = lambda tg, k: b["sV"] + 6 * tg + k
        an = lambda tg, k: b["sA"] + (3 + 2 * NG) * tg + k
        pn = lambda tg, k: b["sP"] + 2 * NG * tg + k
        with contextlib.ExitStack() as ph:
            xbuf = [ph.enter_context(nc.sbuf_tensor(f"xb{i}", [128, D], F32)) for i in range(2)]
            accb = [ph.enter_context(nc.sbuf_tensor(f"accb{i}", [128, D], F32)) for i in range(2)]
            xnf = ph.enter_context(nc.sbuf_tensor("xnf", [128, D], F32))
            tqb = ph.enter_context(nc.sbuf_tensor("tqb", [128, D], F32))
            mqb = ph.enter_context(nc.sbuf_tensor("mqb", [128, D], F32))
            qbf = ph.enter_context(nc.sbuf_tensor("qbf", [128, D], F32))
            c32i = ph.enter_context(nc.sbuf_tensor("c32i", [128, D], mybir.dt.int32))
            wpre = [ph.enter_context(nc.sbuf_tensor(f"wpre{i}", [128, D], F32))
                    for i in range(2)]
            rtmp = ph.enter_context(nc.sbuf_tensor("rtmp", [128, 1], F32))
            qst = [ph.enter_context(nc.sbuf_tensor(f"qst{i}", [128, 512], BF16)) for i in range(2)]
            s1c = ph.enter_context(nc.sbuf_tensor("s1c", [128, 1], F32))
            mc = ph.enter_context(nc.sbuf_tensor("mc", [128, 1], F32))
            vcc = ph.enter_context(nc.sbuf_tensor("vcc", [128, 1], F32))
            lnc = ph.enter_context(nc.sbuf_tensor("lnc", [128, 1], F32))
            rc = ph.enter_context(nc.sbuf_tensor("rc", [128, 1], F32))
            axc = ph.enter_context(nc.sbuf_tensor("axc", [128, 1], F32))
            rxc = ph.enter_context(nc.sbuf_tensor("rxc", [128, 1], F32))
            psT = [ph.enter_context(nc.psum_tensor(f"psT{i}", [128, 512], F32)) for i in range(4)]

            with nc.Block("ph2") as blk:
                @blk.gpsimd
                def _(g):
                    srcs3 = [wg_h, wu_h, wd_h]
                    for j in range(NSL):
                        mm, sl = divmod(j, NO)
                        if j >= 2:
                            S.wait(g, "gp", "sU", j - 1)
                        S.inc("sG", g.dma_start(out=wpre[j % 2][:], in_=srcs3[mm][sl]))

                @blk.sync
                def _(sync):
                    for tg in range(NT):
                        S.wait(sync, "sy", "sV", vn(tg - 2, 4))
                        S.inc("sQ", sync.dma_start(out=xbuf[tg % 2][:],
                                                   in_=x_h[tg * 128:(tg + 1) * 128, :]))
                        S.wait(sync, "sy", "sV", vn(tg, 4))
                        S.inc("sQ", sync.dma_start(out=out_h[tg * 128:(tg + 1) * 128, :],
                                                   in_=accb[tg % 2][:]))
                        for g in range(NG):
                            S.wait(sync, "sy", "sA", an(tg, 2 * g + 5))
                            dst = qpark_h.rearrange("d p t -> p d t")[
                                :, g * 4:(g + 1) * 4, tg * 128:(tg + 1) * 128]
                            S.inc("sQ", sync.dma_start(
                                out=dst,
                                in_=qst[g % 2][:].rearrange("p (d t) -> p d t", d=4)))

                @blk.vector
                def _(vector):
                    for tg in range(NT):
                        S.wait(vector, "ve", "sQ", qn(tg, 1))
                        xb = xbuf[tg % 2]
                        S.fence(vector, "ve", vector.reduce_sum(s1c[:], xb[:], axis=AX.X))
                        S.fence(vector, "ve", vector.tensor_scalar(mc[:], s1c[:], 1.0 / D,
                                                                   None, ALU.mult))
                        S.wait(vector, "ve", "sP", pn(tg, 0))
                        S.inc("sV", vector.tensor_scalar(xnf[:], xb[:], mc[:], None,
                                                         ALU.subtract))
                        S.wait(vector, "ve", "sA", an(tg, 1))
                        S.fence(vector, "ve", vector.reduce_sum(s1c[:], tqb[:], axis=AX.X))
                        S.inc("sV", vector.tensor_scalar(vcc[:], s1c[:], 1.0 / D, None,
                                                         ALU.mult))
                        S.wait(vector, "ve", "sA", an(tg, 2))
                        S.inc("sV", vector.tensor_scalar(xnf[:], xnf[:], rc[:], None,
                                                         ALU.mult))
                        S.wait(vector, "ve", "sQ", qn(tg - 2, 2) if tg >= 2 else 0)
                        S.inc("sV", vector.tensor_tensor(accb[tg % 2][:], xb[:], xnf[:],
                                                         ALU.add))
                        S.fence(vector, "ve", vector.reduce_max(axc[:], xnf[:], axis=AX.X,
                                                                 apply_absolute_value=True))
                        S.inc("sV", vector.tensor_scalar(sx_sc[:, tg:tg + 1], axc[:], 1e-5,
                                                         1.0 / 127.0, ALU.max,
                                                         ALU.mult))
                        S.wait(vector, "ve", "sA", an(tg, 3))
                        vector.tensor_scalar(tqb[:], xnf[:], rxc[:], 256.0, ALU.mult, ALU.add)
                        vector.tensor_copy(c32i[:], tqb[:])
                        vector.tensor_copy(mqb[:], c32i[:])
                        vector.tensor_tensor(tqb[:], tqb[:], mqb[:], ALU.subtract)
                        vector.tensor_scalar(tqb[:], tqb[:], 0.5, None, ALU.is_ge)
                        S.inc("sV", vector.scalar_tensor_tensor(
                            qbf[:], tqb[:], -256.0, mqb[:], ALU.add, ALU.add))

                @blk.scalar
                def _(scalar):
                    for tg in range(NT):
                        S.wait(scalar, "sc", "sV", vn(tg, 1))
                        S.inc("sA", scalar.activation(tqb[:], xnf[:], AF.Square))
                        S.wait(scalar, "sc", "sV", vn(tg, 2))
                        scalar.activation(lnc[:], vcc[:], AF.Ln, bias=cst[:, 0:1])
                        S.inc("sA", scalar.activation(rc[:], lnc[:], AF.Exp,
                                                      scale=cst[:, 3:4]))
                        S.wait(scalar, "sc", "sV", vn(tg, 5))
                        scalar.activation(lnc[:], sx_sc[:, tg:tg + 1], AF.Ln)
                        S.inc("sA", scalar.activation(rxc[:], lnc[:], AF.Exp,
                                                      scale=cst[:, 2:3]))
                        rj = tg * SPT
                        rend = min((tg + 1) * SPT, NSL)
                        for g in range(NG):
                            S.wait(scalar, "sc", "sP", pn(tg, 2 * g + 1))
                            dst = xnT[:].rearrange("p (d t) -> p d t", d=ND)[
                                :, g * 4:(g + 1) * 4, tg * 128:(tg + 1) * 128]
                            S.inc("sA", scalar.activation(
                                dst,
                                psT[(2 * (tg * NG + g)) % 4][:].rearrange("p (d t) -> p d t",
                                                                          d=4),
                                AF.Identity))
                            S.wait(scalar, "sc", "sP", pn(tg, 2 * g + 2))
                            kq = tg * NG + g
                            if kq >= 2:
                                tgp, gp = divmod(kq - 2, NG)
                                S.wait(scalar, "sc", "sQ", qn(tgp, 2 + gp + 1))
                            S.inc("sA", scalar.activation(
                                qst[g % 2][:], psT[(2 * (tg * NG + g) + 1) % 4][:],
                                AF.Identity))
                            while rj < min(rend, tg * SPT + (g + 1) * SPT // NG):
                                S.wait(scalar, "sc", "sG", 16 * (bG0 + rj + 1))
                                S.inc("sU", scalar.activation(
                                    wpre[rj % 2][:], wpre[rj % 2][:], AF.Abs,
                                    accum_out=rtmp_all[:, rj:rj + 1]))
                                rj += 1
                        while rj < rend:
                            S.wait(scalar, "sc", "sG", 16 * (bG0 + rj + 1))
                            S.inc("sU", scalar.activation(
                                wpre[rj % 2][:], wpre[rj % 2][:], AF.Abs,
                                accum_out=rtmp_all[:, rj:rj + 1]))
                            rj += 1

                @blk.tensor
                def _(tensor):
                    for tg in range(NT):
                        S.wait(tensor, "te", "sV", vn(tg, 3))
                        for g in range(NG):
                            idx = 2 * (tg * NG + g)
                            if idx >= 4:
                                j = idx - 4
                                tgp, r = divmod(j, 2 * NG)
                                S.wait(tensor, "te", "sA", an(tgp, r + 4))
                            last = None
                            for k in range(4):
                                dc = g * 4 + k
                                last = tensor.transpose(
                                    psT[idx % 4][:, k * 128:(k + 1) * 128],
                                    xnf[:, dc * 128:(dc + 1) * 128], eyef[:])
                            S.inc("sP", last)
                            S.wait(tensor, "te", "sV", vn(tg, 6))
                            idx2 = idx + 1
                            if idx2 >= 4:
                                j = idx2 - 4
                                tgp, r = divmod(j, 2 * NG)
                                S.wait(tensor, "te", "sA", an(tgp, r + 4))
                            last = None
                            for k in range(4):
                                dc = g * 4 + k
                                last = tensor.transpose(
                                    psT[idx2 % 4][:, k * 128:(k + 1) * 128],
                                    qbf[:, dc * 128:(dc + 1) * 128], eyef[:])
                            S.inc("sP", last)
        S.check(b, sQ=(2 + NG) * NT, sV=6 * NT, sA=(3 + 2 * NG) * NT,
                sP=2 * NG * NT, sG=NSL, sW=4 * NT, sU=NSL)

        b = S.snap()
        with contextlib.ExitStack() as ph:
            ps_s = ph.enter_context(nc.psum_tensor("ps_s", [NT, 128], F32))
            with nc.Block("ph2b") as blk:
                @blk.tensor
                def _(tensor):
                    S.inc("sP", tensor.transpose(ps_s[:], sx_sc[:], eyef[:]))

                @blk.vector
                def _(vector):
                    S.wait(vector, "ve", "sP", b["sP"] + 1)
                    S.inc("sV", vector.tensor_copy(stat_st[0:NT, :], ps_s[:]))

        S.check(b, sP=1, sV=1)
        wm_out_base = S.cnt["sQ"]
        wm_statst = S.cnt["sV"]

        b = S.snap()
        with contextlib.ExitStack() as ph:
            sums_b = ph.enter_context(nc.sbuf_tensor("sums_b", [128, 3], F32))
            ps3 = ph.enter_context(nc.psum_tensor("ps3", [128, 3], F32))
            with nc.Block("ph3") as blk:
                @blk.tensor
                def _(tensor):
                    S.wait(tensor, "te", "sV", b["sV"] + 1)
                    S.inc("sP", tensor.matmul(ps3[:], ones[:], accabs[:], start=True,
                                              stop=True))

                @blk.vector
                def _(vector):
                    for mm in range(3):
                        r = vector.reduce_sum(accabs[:, mm:mm + 1],
                                              rtmp_all[:, mm * NO:(mm + 1) * NO],
                                              axis=AX.X)
                    S.inc("sV", r)
                    S.wait(vector, "ve", "sP", b["sP"] + 1)
                    S.fence(vector, "ve", vector.tensor_copy(sums_b[:], ps3[:]))
                    S.inc("sV", vector.tensor_scalar(sw_b[:], sums_b[:],
                                                     1.0 / float(DFF * D), 1e-5,
                                                     ALU.mult, ALU.add))

                @blk.scalar
                def _(scalar):
                    S.wait(scalar, "sc", "sV", b["sV"] + 2)
                    scalar.activation(sums_b[:], sw_b[:], AF.Ln)
                    S.inc("sA", scalar.activation(rw_b[:], sums_b[:], AF.Exp,
                                                  scale=cst[:, 2:3]))
        S.check(b, sV=2, sP=1, sA=1)

        def rot_order():
            order = []
            for i in range(NPC):
                order.append(("ev", i))
                order.append(("od", i))
            for j in range(ND - 2 * NPC):
                order.append(("pa", j))
            return order

        def emit_qkv_mms(tensor, ps, wslab):
            last = None
            for dc in range(ND):
                for tb in range(NTB):
                    last = tensor.matmul(
                        ps[:, tb * 512:(tb + 1) * 512],
                        wslab[:, dc * 128:(dc + 1) * 128],
                        xnT[:, dc * T + tb * 512: dc * T + tb * 512 + 512],
                        start=(dc == 0), stop=(dc == ND - 1))
            return last

        KK_stack = contextlib.ExitStack()
        KK = KK_stack.enter_context(nc.sbuf_tensor("KK", [128, ND * T], BF16))
        rbb_stack = contextlib.ExitStack()
        rbb = rbb_stack.enter_context(nc.sbuf_tensor("rbb", [128, T], F32))

        b = S.snap()
        with contextlib.ExitStack() as ph:
            wsl = [ph.enter_context(nc.sbuf_tensor(f"wsl4_{i}", [128, D], BF16)) for i in range(2)]
            kev = ph.enter_context(nc.sbuf_tensor("kev", [128, T], F32))
            kod = ph.enter_context(nc.sbuf_tensor("kod", [128, T], F32))
            cosb = ph.enter_context(nc.sbuf_tensor("cosb", [128, T], F32))
            sinb = ph.enter_context(nc.sbuf_tensor("sinb", [128, T], F32))
            sqtmp = ph.enter_context(nc.sbuf_tensor("sqtmp", [128, T], F32))
            e1 = ph.enter_context(nc.sbuf_tensor("e1", [128, T], F32))
            ksqa = ph.enter_context(nc.sbuf_tensor("ksqa", [128, T], F32))
            psK = [ph.enter_context(nc.psum_tensor(f"psK{i}", [128, T], F32)) for i in range(2)]
            order = rot_order()
            APC = {"ev": 1, "od": 1, "pa": 2}
            VPC = {"ev": 0, "od": 2, "pa": 1}
            acum = [0]
            vcum = [0]
            for kind, _ in order:
                acum.append(acum[-1] + APC[kind])
                vcum.append(vcum[-1] + VPC[kind])

            with nc.Block("ph4") as blk:
                @blk.gpsimd
                def _(g):
                    for ci in range(ND):
                        S.wait(g, "gp", "sP", (b["sP"] + ci - 1) if ci >= 2 else 0)
                        S.inc("sG", g.dma_start(out=wsl[ci % 2][:], in_=wqkv_h[ci]))

                @blk.sync
                def _(sync):
                    for i in range(NPC):
                        if i >= 1:
                            S.wait(sync, "sy", "sV", b["sV"] + 1 + vcum[2 * i])
                        S.inc("sQ", sync.dma_start(out=cosb[:], in_=cos_h[i]))
                        S.inc("sQ", sync.dma_start(out=sinb[:], in_=sin_h[i]))

                @blk.tensor
                def _(tensor):
                    for ci in range(ND):
                        S.wait(tensor, "te", "sG", 16 * (b["sG"] + ci + 1))
                        if ci >= 2:
                            S.wait(tensor, "te", "sA", b["sA"] + acum[ci - 1])
                        S.inc("sP", emit_qkv_mms(tensor, psK[ci % 2], wsl[ci % 2]))
                    S.wait(tensor, "te", "sV", b["sV"] + 1 + vcum[ND])
                    S.wait(tensor, "te", "sA", b["sA"] + acum[ND])
                    last = None
                    for tb in range(NTB):
                        last = tensor.matmul(psK[0][:, tb * 512:(tb + 1) * 512], ones[:],
                                             ksqa[:, tb * 512:(tb + 1) * 512],
                                             start=True, stop=True)
                    S.inc("sP", last)

                @blk.scalar
                def _(scalar):
                    for ci, (kind, i) in enumerate(order):
                        S.wait(scalar, "sc", "sP", b["sP"] + ci + 1)
                        bias = bq_s[:, ci:ci + 1]
                        S.wait(scalar, "sc", "sV", b["sV"] + 1 + vcum[ci])
                        if kind == "ev":
                            S.inc("sA", scalar.activation(kev[:], psK[ci % 2][:], AF.Identity,
                                                          bias=bias))
                        elif kind == "od":
                            S.inc("sA", scalar.activation(kod[:], psK[ci % 2][:], AF.Identity,
                                                          bias=bias))
                        else:
                            slot = 2 * NPC + i
                            S.inc("sA", scalar.activation(KK[:, slot * T:(slot + 1) * T],
                                                          psK[ci % 2][:], AF.Identity,
                                                          bias=bias))
                            S.inc("sA", scalar.activation(sqtmp[:], psK[ci % 2][:], AF.Square,
                                                          bias=bias))
                    S.wait(scalar, "sc", "sP", b["sP"] + ND + 1)
                    S.wait(scalar, "sc", "sV", b["sV"] + 1 + vcum[ND])
                    scalar.activation(e1[:], psK[0][:], AF.Ln)
                    S.inc("sA", scalar.activation(rbb[:], e1[:], AF.Exp,
                                                  scale=cst[:, 3:4]))

                @blk.vector
                def _(vector):
                    S.inc("sV", vector.memset(ksqa[:], 0.0))
                    for ci, (kind, i) in enumerate(order):
                        if kind == "od":
                            S.wait(vector, "ve", "sA", b["sA"] + acum[ci + 1])
                            S.wait(vector, "ve", "sQ", 16 * (b["sQ"] + 2 * (i + 1)))
                            ev_s, od_s = i, NPC + i
                            vector.tensor_tensor(e1[:], kev[:], cosb[:], ALU.mult)
                            vector.tensor_tensor(sqtmp[:], kod[:], sinb[:], ALU.mult)
                            vector.tensor_tensor(e1[:], e1[:], sqtmp[:], ALU.subtract)
                            vector.tensor_tensor(sqtmp[:], e1[:], e1[:], ALU.mult)
                            vector.tensor_tensor(ksqa[:], ksqa[:], sqtmp[:], ALU.add)
                            S.inc("sV", vector.tensor_copy(KK[:, ev_s * T:(ev_s + 1) * T],
                                                           e1[:]))
                            vector.tensor_tensor(e1[:], kod[:], cosb[:], ALU.mult)
                            vector.tensor_tensor(sqtmp[:], kev[:], sinb[:], ALU.mult)
                            vector.tensor_tensor(e1[:], e1[:], sqtmp[:], ALU.add)
                            vector.tensor_tensor(sqtmp[:], e1[:], e1[:], ALU.mult)
                            vector.tensor_tensor(ksqa[:], ksqa[:], sqtmp[:], ALU.add)
                            S.inc("sV", vector.tensor_copy(KK[:, od_s * T:(od_s + 1) * T],
                                                           e1[:]))
                        elif kind == "pa":
                            S.wait(vector, "ve", "sA", b["sA"] + acum[ci + 1])
                            S.inc("sV", vector.tensor_tensor(ksqa[:], ksqa[:], sqtmp[:],
                                                             ALU.add))
                    S.wait(vector, "ve", "sA", b["sA"] + acum[ND] + 1)
                    S.inc("sV", vector.tensor_scalar(rbb[:], rbb[:], 1e12, None, ALU.min))
        S.check(b, sG=ND, sP=ND + 1, sA=acum[ND] + 1, sV=1 + vcum[ND] + 1, sQ=2 * NPC)

        b = S.snap()
        with contextlib.ExitStack() as ph:
            wsl = [ph.enter_context(nc.sbuf_tensor(f"wsl5_{i}", [128, D], BF16)) for i in range(2)]
            vch = [ph.enter_context(nc.sbuf_tensor(f"vch{i}", [128, T], F32)) for i in range(2)]
            t1b = ph.enter_context(nc.sbuf_tensor("t1b", [128, T], F32))
            psV = [ph.enter_context(nc.psum_tensor(f"psV{i}", [128, T], F32)) for i in range(2)]
            with nc.Block("ph5") as blk:
                @blk.gpsimd
                def _(g):
                    for ci in range(ND):
                        S.wait(g, "gp", "sP", (b["sP"] + ci - 1) if ci >= 2 else 0)
                        S.inc("sG", g.dma_start(out=wsl[ci % 2][:], in_=wqkv_h[ND + ci]))

                @blk.tensor
                def _(tensor):
                    for ci in range(ND):
                        S.wait(tensor, "te", "sG", 16 * (b["sG"] + ci + 1))
                        if ci >= 2:
                            S.wait(tensor, "te", "sA", b["sA"] + ci - 1)
                        S.inc("sP", emit_qkv_mms(tensor, psV[ci % 2], wsl[ci % 2]))

                @blk.scalar
                def _(scalar):
                    for ci in range(ND):
                        S.wait(scalar, "sc", "sP", b["sP"] + ci + 1)
                        if ci >= 2:
                            S.wait(scalar, "sc", "sV", b["sV"] + 2 * (ci - 1) - 1)
                        S.inc("sA", scalar.activation(vch[ci % 2][:], psV[ci % 2][:], AF.Identity,
                                                      bias=bq_s[:, ND + ci:ND + ci + 1]))

                @blk.vector
                def _(vector):
                    for ci in range(ND):
                        S.wait(vector, "ve", "sA", b["sA"] + ci + 1)
                        S.inc("sV", vector.tensor_tensor(t1b[:], vch[ci % 2][:],
                                                         KK[:, ci * T:(ci + 1) * T], ALU.mult))
                        vector.tensor_tensor(t1b[:], t1b[:], rbb[:], ALU.mult)
                        S.inc("sV", vector.reduce_sum(A_c[:, ci:ci + 1], t1b[:], axis=AX.X))
        S.check(b, sG=ND, sP=ND, sA=ND, sV=2 * ND)
        rbb_stack.close()
        KK_stack.close()

        b = S.snap()
        with contextlib.ExitStack() as ph:
            wsl = [ph.enter_context(nc.sbuf_tensor(f"wsl6_{i}", [128, D], BF16)) for i in range(2)]
            kev = ph.enter_context(nc.sbuf_tensor("qev", [128, T], F32))
            kod = ph.enter_context(nc.sbuf_tensor("qod", [128, T], F32))
            cosb = ph.enter_context(nc.sbuf_tensor("cosb6", [128, T], F32))
            sinb = ph.enter_context(nc.sbuf_tensor("sinb6", [128, T], F32))
            e1 = ph.enter_context(nc.sbuf_tensor("e16", [128, T], F32))
            e2 = ph.enter_context(nc.sbuf_tensor("e26", [128, T], F32))
            ystg = [ph.enter_context(nc.sbuf_tensor(f"ystg{i}", [128, T], BF16))
                    for i in range(2)]
            psQ = [ph.enter_context(nc.psum_tensor(f"psQ{i}", [128, T], F32)) for i in range(2)]
            order = rot_order()
            VPC = {"ev": 0, "od": 2, "pa": 1}
            vcum = [0]
            stage_cum = [0]
            for kind, _ in order:
                vcum.append(vcum[-1] + VPC[kind])
                stage_cum.append(stage_cum[-1] + (2 if kind == "od" else (1 if kind == "pa" else 0)))
            NSTG = stage_cum[-1]
            assert NSTG == ND
            stage_slots = []
            for kind, i in order:
                if kind == "od":
                    stage_slots += [i, NPC + i]
                elif kind == "pa":
                    stage_slots += [2 * NPC + i]

            with nc.Block("ph6") as blk:
                @blk.gpsimd
                def _(g):
                    for ci in range(ND):
                        S.wait(g, "gp", "sP", (b["sP"] + ci - 1) if ci >= 2 else 0)
                        S.inc("sG", g.dma_start(out=wsl[ci % 2][:], in_=wqkv_h[2 * ND + ci]))

                park_done = {}
                _cnt = b["sQ"]
                for ci, (kind, i) in enumerate(order):
                    if kind == "od":
                        _cnt += 2
                        park_done[stage_cum[ci]] = None
                        park_done[stage_cum[ci] + 1] = None
                        _c2 = _cnt
                    if kind in ("od", "pa"):
                        for s in range(stage_cum[ci], stage_cum[ci + 1]):
                            _cnt += 1
                            park_done[s] = _cnt

                @blk.sync
                def _(sync):
                    for ci, (kind, i) in enumerate(order):
                        if kind == "od":
                            if i >= 1:
                                S.wait(sync, "sy", "sV", b["sV"] + vcum[2 * i])
                            S.inc("sQ", sync.dma_start(out=cosb[:], in_=cos_h[i]))
                            S.inc("sQ", sync.dma_start(out=sinb[:], in_=sin_h[i]))
                        for s in range(stage_cum[ci], stage_cum[ci + 1]):
                            S.wait(sync, "sy", "sV", b["sV"] + s + 1)
                            S.inc("sQ", sync.dma_start(out=ypark_h[stage_slots[s]],
                                                       in_=ystg[s % 2][:]))

                @blk.tensor
                def _(tensor):
                    for ci in range(ND):
                        S.wait(tensor, "te", "sG", 16 * (b["sG"] + ci + 1))
                        if ci >= 2:
                            S.wait(tensor, "te", "sA", b["sA"] + ci - 1)
                        S.inc("sP", emit_qkv_mms(tensor, psQ[ci % 2], wsl[ci % 2]))

                @blk.scalar
                def _(scalar):
                    for ci, (kind, i) in enumerate(order):
                        S.wait(scalar, "sc", "sP", b["sP"] + ci + 1)
                        bias = bq_s[:, 2 * ND + ci:2 * ND + ci + 1]
                        if kind == "ev":
                            S.wait(scalar, "sc", "sV", b["sV"] + vcum[ci])
                            S.inc("sA", scalar.activation(kev[:], psQ[ci % 2][:], AF.Identity,
                                                          bias=bias))
                        elif kind == "od":
                            S.inc("sA", scalar.activation(kod[:], psQ[ci % 2][:], AF.Identity,
                                                          bias=bias))
                        else:
                            S.wait(scalar, "sc", "sV", b["sV"] + vcum[ci])
                            S.inc("sA", scalar.activation(e1[:], psQ[ci % 2][:], AF.Identity,
                                                          bias=bias))

                @blk.vector
                def _(vector):
                    for ci, (kind, i) in enumerate(order):
                        if kind == "ev":
                            continue
                        S.wait(vector, "ve", "sA", b["sA"] + ci + 1)
                        if kind == "od":
                            S.wait(vector, "ve", "sQ", 16 * (b["sQ"] + 2 * (i + 1)))
                            s_ev = stage_cum[ci]
                            s_od = s_ev + 1
                            slot_ev, slot_od = i, NPC + i
                            for s_cur in (s_ev, s_od):
                                if s_cur >= 2:
                                    S.wait(vector, "ve", "sQ", 16 * park_done[s_cur - 2])
                            vector.tensor_tensor(e1[:], kev[:], cosb[:], ALU.mult)
                            vector.tensor_tensor(e2[:], kod[:], sinb[:], ALU.mult)
                            vector.tensor_tensor(e1[:], e1[:], e2[:], ALU.subtract)
                            S.inc("sV", vector.tensor_scalar(
                                ystg[s_ev % 2][:], e1[:], A_c[:, slot_ev:slot_ev + 1],
                                None, ALU.mult))
                            vector.tensor_tensor(e1[:], kod[:], cosb[:], ALU.mult)
                            vector.tensor_tensor(e2[:], kev[:], sinb[:], ALU.mult)
                            vector.tensor_tensor(e1[:], e1[:], e2[:], ALU.add)
                            S.inc("sV", vector.tensor_scalar(
                                ystg[s_od % 2][:], e1[:], A_c[:, slot_od:slot_od + 1],
                                None, ALU.mult))
                        else:
                            s_cur = stage_cum[ci]
                            slot = 2 * NPC + i
                            if s_cur >= 2:
                                S.wait(vector, "ve", "sQ", 16 * park_done[s_cur - 2])
                            S.inc("sV", vector.tensor_scalar(
                                ystg[s_cur % 2][:], e1[:], A_c[:, slot:slot + 1],
                                None, ALU.mult))
        S.check(b, sG=ND, sP=ND, sA=ND, sV=ND, sQ=2 * NPC + ND)
        xnT_stack.close()

        b = S.snap()
        with contextlib.ExitStack() as ph:
            wout_sb = ph.enter_context(nc.sbuf_tensor("wout_sb", [128, ND * D], BF16))
            boutr = ph.enter_context(nc.sbuf_tensor("boutr", [1, D], F32))
            yblk = [ph.enter_context(nc.sbuf_tensor(f"yblk{i}", [128, ND * 128], BF16))
                    for i in range(2)]
            outsb = [ph.enter_context(nc.sbuf_tensor(f"outsb{i}", [128, D], F32))
                     for i in range(2)]
            psY = [ph.enter_context(nc.psum_tensor(f"psY{i}", [128, D], F32)) for i in range(2)]
            with nc.Block("ph7") as blk:
                @blk.sync
                def _(sync):
                    S.inc("sQ", sync.dma_start(out=boutr[:], in_=bout_h))
                    for tg in range(NT):
                        S.wait(sync, "sy", "sP", (b["sP"] + tg - 1) if tg >= 2 else 0)
                        src_ap = ypark_h.rearrange("d p t -> p d t")[
                            :, :, tg * 128:(tg + 1) * 128]
                        S.inc("sQ", sync.dma_start(
                            out=yblk[tg % 2][:].rearrange("p (d t) -> p d t", d=ND),
                            in_=src_ap))

                @blk.gpsimd
                def _(g):
                    for dc in range(ND):
                        S.inc("sG", g.dma_start(out=wout_sb[:, dc * D:(dc + 1) * D],
                                                in_=wout_h[dc * 128:(dc + 1) * 128, :]))
                    S.wait(g, "gp", "sQ", 16 * wm_out_base)
                    for tg in range(NT):
                        S.wait(g, "gp", "sA", b["sA"] + tg + 1)
                        S.inc("sG", g.dma_start(out=out_h[tg * 128:(tg + 1) * 128, :],
                                                in_=outsb[tg % 2][:], accum_op=ALU.add))

                @blk.tensor
                def _(tensor):
                    S.wait(tensor, "te", "sG", 16 * (b["sG"] + ND))
                    S.wait(tensor, "te", "sQ", 16 * (b["sQ"] + 1))
                    for tg in range(NT):
                        S.wait(tensor, "te", "sQ", 16 * (b["sQ"] + 1 + tg + 1))
                        if tg >= 2:
                            S.wait(tensor, "te", "sA", b["sA"] + tg - 1)
                        last = None
                        for db in range(NDB):
                            for dc in range(ND):
                                tensor.matmul(
                                    psY[tg % 2][:, db * 512:(db + 1) * 512],
                                    yblk[tg % 2][:, dc * 128:(dc + 1) * 128],
                                    wout_sb[:, dc * D + db * 512: dc * D + db * 512 + 512],
                                    start=(dc == 0), stop=False)
                            last = tensor.matmul(
                                psY[tg % 2][:, db * 512:(db + 1) * 512],
                                ones[0:1, :], boutr[:, db * 512:(db + 1) * 512],
                                start=False, stop=True)
                        S.inc("sP", last)

                @blk.scalar
                def _(scalar):
                    for tg in range(NT):
                        S.wait(scalar, "sc", "sP", b["sP"] + tg + 1)
                        if tg >= 2:
                            S.wait(scalar, "sc", "sG", 16 * (b["sG"] + ND + tg - 1))
                        S.inc("sA", scalar.activation(outsb[tg % 2][:], psY[tg % 2][:],
                                                      AF.Identity))
        S.check(b, sQ=1 + NT, sG=ND + NT, sP=NT, sA=NT)

        rhb_stack = contextlib.ExitStack()
        rhb = rhb_stack.enter_context(nc.sbuf_tensor("rhb", [128, T], F32))
        gu_stack = contextlib.ExitStack()
        vgb = gu_stack.enter_context(nc.sbuf_tensor("vgb", [128, T], F32))
        vub = gu_stack.enter_context(nc.sbuf_tensor("vub", [128, T], F32))
        amaxfm = gu_stack.enter_context(nc.sbuf_tensor("amaxfm", [128, T], F32))
        amaxN = gu_stack.enter_context(nc.sbuf_tensor("amaxN", [128, T], F32))
        b = S.snap()
        with contextlib.ExitStack() as ph:
            vrow = ph.enter_context(nc.sbuf_tensor("vrow", [1, T], F32))
            sx_row = ph.enter_context(nc.sbuf_tensor("sx_row", [1, T], F32))
            psBC8 = ph.enter_context(nc.psum_tensor("psBC8", [128, T], F32))
            with nc.Block("ph7b") as blk:
                @blk.sync
                def _(sync):
                    S.inc("sQ", sync.dma_start(out=sx_row[:], in_=stat_st[0:NT, :]))

                @blk.vector
                def _(vector):
                    S.wait(vector, "ve", "sQ", 16 * (b["sQ"] + 1))
                    S.inc("sV", vector.tensor_scalar(vrow[:], sx_row[:], sw_b[0:1, 0:1],
                                                     None, ALU.mult))
                    S.wait(vector, "ve", "sP", b["sP"] + 1)
                    S.inc("sV", vector.tensor_copy(vgb[:], psBC8[:]))
                    S.inc("sV", vector.tensor_scalar(vrow[:], sx_row[:], sw_b[0:1, 1:2],
                                                     None, ALU.mult))
                    S.wait(vector, "ve", "sP", b["sP"] + 2)
                    S.inc("sV", vector.tensor_copy(vub[:], psBC8[:]))

                @blk.tensor
                def _(tensor):
                    for j in range(2):
                        S.wait(tensor, "te", "sV", b["sV"] + 1 + 2 * j)
                        last = None
                        for tb in range(NTB):
                            last = tensor.matmul(psBC8[:, tb * 512:(tb + 1) * 512],
                                                 ones[0:1, :],
                                                 vrow[:, tb * 512:(tb + 1) * 512],
                                                 start=True, stop=True)
                        S.inc("sP", last)
        S.check(b, sV=4, sP=2, sQ=1)

        qT_stack = contextlib.ExitStack()
        qT = qT_stack.enter_context(nc.sbuf_tensor("qT", [128, ND * T], BF16))
        b = S.snap()
        TH = T // 2
        NHB = max(1, TH // 512)
        HBW = min(TH, 512)
        vpos_tern = {}
        vpos = {}
        apos_tern = {}
        apos_sig = {}
        qpos = {}
        _v = b["sV"] + 1
        vpos_tern[0] = (_v + 1, _v + 2); _v += 2
        _a = b["sA"]
        apos_tern[0] = (_a + 1, _a + 2, _a + 3, _a + 4); _a += 4
        _q = b["sQ"] + 1
        qpos[("wg", 0)] = _q + 1
        qpos[("wu", 0)] = _q + 2
        _q += 2
        for oc in range(NO):
            if oc + 1 < NO:
                qpos[("wg", oc + 1)] = _q + 1
                qpos[("wu", oc + 1)] = _q + 2
                _q += 2
            qpos[("h", oc)] = _q + 1
            _q += 1
            if oc + 1 < NO:
                apos_tern[oc + 1] = (_a + 1, _a + 2, _a + 3, _a + 4); _a += 4
            apos_sig[(oc, 0)] = _a + 1
            apos_sig[(oc, 1)] = _a + 2
            _a += 2
            if oc + 1 < NO:
                vpos_tern[oc + 1] = (_v + 1, _v + 2); _v += 2
            for j, k in enumerate(("t1h0", "t2h0", "h0d", "t1h1", "t2h1", "h1d")):
                vpos[(oc, k)] = _v + 1 + j
            _v += 6
        pg8 = lambda oc, k: b["sP"] + 4 * oc + k
        with contextlib.ExitStack() as ph:
            wgs = [ph.enter_context(nc.sbuf_tensor(f"wgs{i}", [128, D], F32)) for i in range(2)]
            wus = [ph.enter_context(nc.sbuf_tensor(f"wus{i}", [128, D], F32)) for i in range(2)]
            teg = [ph.enter_context(nc.sbuf_tensor(f"teg{i}", [128, D], BF16)) for i in range(2)]
            teu = [ph.enter_context(nc.sbuf_tensor(f"teu{i}", [128, D], BF16)) for i in range(2)]
            zg0 = ph.enter_context(nc.sbuf_tensor("zg0", [128, D], F32))
            zg1 = ph.enter_context(nc.sbuf_tensor("zg1", [128, D], F32))
            hg = ph.enter_context(nc.sbuf_tensor("hg", [128, TH], F32))
            hs = ph.enter_context(nc.sbuf_tensor("hs", [128, TH], F32))
            hu = ph.enter_context(nc.sbuf_tensor("hu", [128, TH], F32))
            hsl = [ph.enter_context(nc.sbuf_tensor(f"hsl{i}", [128, T], F32)) for i in range(2)]
            psG = [ph.enter_context(nc.psum_tensor(f"psG{i}", [128, TH], F32)) for i in range(2)]
            psU = [ph.enter_context(nc.psum_tensor(f"psU{i}", [128, TH], F32)) for i in range(2)]

            def tern_acts(scalar, oc):
                S.wait(scalar, "sc", "sQ", 16 * qpos[("wg", oc)])
                if oc >= 1:
                    S.wait(scalar, "sc", "sV", vpos_tern[oc - 1][1])
                S.inc("sA", scalar.activation(zg0[:], wgs[oc % 2][:], AF.Sign,
                                              scale=rw_b[:, 0:1]))
                S.inc("sA", scalar.activation(zg1[:], wgs[oc % 2][:], AF.Square,
                                              scale=rw_b[:, 0:1]))
                S.wait(scalar, "sc", "sQ", 16 * qpos[("wu", oc)])
                S.wait(scalar, "sc", "sV", vpos_tern[oc][0])
                S.inc("sA", scalar.activation(zg0[:], wus[oc % 2][:], AF.Sign,
                                              scale=rw_b[:, 1:2]))
                S.inc("sA", scalar.activation(zg1[:], wus[oc % 2][:], AF.Square,
                                              scale=rw_b[:, 1:2]))

            def tern_dve(vector, oc):
                S.wait(vector, "ve", "sA", apos_tern[oc][1])
                if oc >= 2:
                    S.wait(vector, "ve", "sP", pg8(oc - 2, 4))
                vector.tensor_scalar(zg1[:], zg1[:], 0.25, None, ALU.is_ge)
                S.inc("sV", vector.tensor_tensor(teg[oc % 2][:], zg0[:], zg1[:], ALU.mult))
                S.wait(vector, "ve", "sA", apos_tern[oc][3])
                vector.tensor_scalar(zg1[:], zg1[:], 0.25, None, ALU.is_ge)
                S.inc("sV", vector.tensor_tensor(teu[oc % 2][:], zg0[:], zg1[:], ALU.mult))

            with nc.Block("ph8") as blk:
                @blk.sync
                def _(sync):
                    S.inc("sQ", sync.dma_start(
                        out=qT[:].rearrange("p (d t) -> p d t", d=ND),
                        in_=qpark_h.rearrange("d p t -> p d t")))
                    S.inc("sQ", sync.dma_start(out=wgs[0][:], in_=wg_h[0]))
                    S.inc("sQ", sync.dma_start(out=wus[0][:], in_=wu_h[0]))
                    for oc in range(NO):
                        if oc + 1 < NO:
                            S.wait(sync, "sy", "sA", apos_tern[oc - 1][1] if oc >= 1 else 0)
                            S.inc("sQ", sync.dma_start(out=wgs[(oc + 1) % 2][:],
                                                       in_=wg_h[oc + 1]))
                            S.wait(sync, "sy", "sA", apos_tern[oc - 1][3] if oc >= 1 else 0)
                            S.inc("sQ", sync.dma_start(out=wus[(oc + 1) % 2][:],
                                                       in_=wu_h[oc + 1]))
                        S.wait(sync, "sy", "sV", vpos[(oc, "h1d")])
                        S.inc("sQ", sync.dma_start(out=hscr_h[oc], in_=hsl[oc % 2][:]))

                @blk.tensor
                def _(tensor):
                    S.wait(tensor, "te", "sQ", 16 * (b["sQ"] + 1))
                    for oc in range(NO):
                        for (m, hf) in ((0, 0), (1, 0), (0, 1), (1, 1)):
                            tern = teg[oc % 2] if m == 0 else teu[oc % 2]
                            ps = (psG if m == 0 else psU)[hf]
                            S.wait(tensor, "te", "sV", vpos_tern[oc][m])
                            if oc >= 1:
                                rel = {(0, 0): "t1h0", (1, 0): "t2h0",
                                       (0, 1): "t1h1", (1, 1): "t2h1"}[(m, hf)]
                                S.wait(tensor, "te", "sV", vpos[(oc - 1, rel)])
                            last = None
                            for dc in range(ND):
                                for hb in range(NHB):
                                    off = dc * T + hf * TH + hb * HBW
                                    last = tensor.matmul(
                                        ps[:, hb * HBW:(hb + 1) * HBW],
                                        tern[:, dc * 128:(dc + 1) * 128],
                                        qT[:, off:off + HBW],
                                        start=(dc == 0), stop=(dc == ND - 1))
                            S.inc("sP", last)

                @blk.scalar
                def _(scalar):
                    _preload_tables(scalar, nc, AF.Sigmoid, AF.Identity, AF.Square, AF.Sign)
                    tern_acts(scalar, 0)
                    for oc in range(NO):
                        if oc + 1 < NO:
                            tern_acts(scalar, oc + 1)
                        S.wait(scalar, "sc", "sV", vpos[(oc, "t1h0")])
                        S.inc("sA", scalar.activation(hs[:], hg[:], AF.Sigmoid))
                        S.wait(scalar, "sc", "sV", vpos[(oc, "t1h1")])
                        S.inc("sA", scalar.activation(hs[:], hg[:], AF.Sigmoid))

                @blk.vector
                def _(vector):
                    vector.memset(amaxfm[:], 0.0)
                    S.inc("sV", vector.memset(amaxN[:], 0.0))
                    tern_dve(vector, 0)
                    for oc in range(NO):
                        if oc + 1 < NO:
                            tern_dve(vector, oc + 1)
                        for hf in range(2):
                            S.wait(vector, "ve", "sP", pg8(oc, 2 * hf + 1))
                            S.inc("sV", vector.tensor_tensor(
                                hg[:], psG[hf][:], vgb[:, hf * TH:(hf + 1) * TH], ALU.mult))
                            S.wait(vector, "ve", "sP", pg8(oc, 2 * hf + 2))
                            S.inc("sV", vector.tensor_tensor(
                                hu[:], psU[hf][:], vub[:, hf * TH:(hf + 1) * TH], ALU.mult))
                            S.wait(vector, "ve", "sA", apos_sig[(oc, hf)])
                            if oc >= 2:
                                S.wait(vector, "ve", "sQ", 16 * qpos[("h", oc - 2)])
                            sl = slice(hf * TH, (hf + 1) * TH)
                            vector.tensor_tensor(hsl[oc % 2][:, sl], hs[:], hu[:], ALU.mult)
                            vector.tensor_max(amaxfm[:, sl], amaxfm[:, sl],
                                              hsl[oc % 2][:, sl])
                            S.inc("sV", vector.tensor_tensor(
                                amaxN[:, sl], amaxN[:, sl], hsl[oc % 2][:, sl], ALU.min))
                    vector.tensor_scalar(amaxN[:], amaxN[:], -1.0, None, ALU.mult)
                    S.inc("sV", vector.tensor_max(amaxfm[:], amaxfm[:], amaxN[:]))
        S.check(b, sQ=1 + 3 * NO, sP=4 * NO, sA=6 * NO, sV=2 + 8 * NO)
        qT_stack.close()

        b = S.snap()
        with contextlib.ExitStack() as ph:
            ps9 = [ph.enter_context(nc.psum_tensor(f"ps9{i}", [128, 128], F32)) for i in range(2)]
            ps9b = ph.enter_context(nc.psum_tensor("ps9b", [NT, 128], F32))
            rh_row = ph.enter_context(nc.sbuf_tensor("rh_row", [1, T], F32))
            lh9 = ph.enter_context(nc.sbuf_tensor("lh9", [128, NT], F32))
            psBC = ph.enter_context(nc.psum_tensor("psBC", [128, T], F32))
            with nc.Block("ph9") as blk:
                @blk.tensor
                def _(tensor):
                    for tg in range(NT):
                        if tg >= 2:
                            S.wait(tensor, "te", "sV", b["sV"] + tg - 1)
                        S.inc("sP", tensor.transpose(ps9[tg % 2][:],
                                                     amaxfm[:, tg * 128:(tg + 1) * 128],
                                                     eyef[:]))
                    S.wait(tensor, "te", "sA", b["sA"] + 1)
                    S.inc("sP", tensor.transpose(ps9b[:], rh_tok[:], eyef[:]))
                    S.wait(tensor, "te", "sQ", 16 * (b["sQ"] + 1))
                    last = None
                    for tb in range(NTB):
                        last = tensor.matmul(psBC[:, tb * 512:(tb + 1) * 512], ones[0:1, :],
                                             rh_row[:, tb * 512:(tb + 1) * 512],
                                             start=True, stop=True)
                    S.inc("sP", last)

                @blk.scalar
                def _(scalar):
                    _preload_tables(scalar, nc, AF.Ln, AF.Exp, AF.Identity, AF.Square,
                                    AF.Sign)
                    S.wait(scalar, "sc", "sV", b["sV"] + NT + 1)
                    scalar.activation(lh9[:], sh_sc[:], AF.Ln)
                    S.inc("sA", scalar.activation(rh_tok[:], lh9[:], AF.Exp,
                                                  scale=cst[:, 2:3]))

                @blk.vector
                def _(vector):
                    for tg in range(NT):
                        S.wait(vector, "ve", "sP", b["sP"] + tg + 1)
                        S.inc("sV", vector.reduce_max(sh_sc[:, tg:tg + 1],
                                                      ps9[tg % 2][:], axis=AX.X))
                    S.fence(vector, "ve", vector.engine_nop())
                    if True:
                        pass
                    S.fence(vector, "ve", vector.tensor_scalar(
                        sh_sc[:], sh_sc[:], 1e-5, 1.0 / 127.0, ALU.max, ALU.mult))
                    S.inc("sV", vector.tensor_scalar(vd_tok[:], sh_sc[:], sw_b[:, 2:3], None,
                                                     ALU.mult))
                    S.wait(vector, "ve", "sP", b["sP"] + NT + 1)
                    S.inc("sV", vector.tensor_copy(stat_st[0:NT, :], ps9b[:]))
                    S.wait(vector, "ve", "sP", b["sP"] + NT + 2)
                    S.inc("sV", vector.tensor_copy(rhb[:], psBC[:]))

                @blk.sync
                def _(sync):
                    S.wait(sync, "sy", "sV", b["sV"] + NT + 2)
                    S.inc("sQ", sync.dma_start(out=rh_row[:], in_=stat_st[0:NT, :]))
        S.check(b, sP=NT + 2, sV=NT + 3, sQ=1, sA=1)
        gu_stack.close()

        b = S.snap()
        with contextlib.ExitStack() as ph:
            wdt = [[ph.enter_context(nc.sbuf_tensor(f"wdt{i}_{ol}", [128, D], BF16))
                    for ol in range(NOQ)] for i in range(2)]
            hread = [ph.enter_context(nc.sbuf_tensor("hread0", [128, T], F32))] * 2
            wds = [ph.enter_context(nc.sbuf_tensor("wds0", [128, D], F32))] * 2
            hqs = [ph.enter_context(nc.sbuf_tensor("hqs0", [128, T], BF16))] * 2
            hqb = [ph.enter_context(nc.sbuf_tensor(f"hqb{i}", [128, NOQ * 128], BF16))
                   for i in range(2)]
            BD = max(D, T)
            b1 = ph.enter_context(nc.sbuf_tensor("b1d", [128, BD], BF16))
            b2 = ph.enter_context(nc.sbuf_tensor("b2d", [128, BD], F32))
            q1 = ph.enter_context(nc.sbuf_tensor("q1d", [128, T // 2], F32))
            q2 = ph.enter_context(nc.sbuf_tensor("q2d", [128, T // 2], F32))
            hq32 = ph.enter_context(nc.sbuf_tensor("hq32", [128, T // 2],
                                                   mybir.dt.int32))
            outd = [ph.enter_context(nc.sbuf_tensor(f"outd{i}", [128, D], F32))
                    for i in range(2)]
            psD = [ph.enter_context(nc.psum_tensor(f"psD{i}", [128, D], F32)) for i in range(2)]

            pairs = [(q_, tg) for q_ in range(NQ) for tg in range(NT)]
            qv = {}
            _q = b["sQ"]
            for j in range(NO):
                _q += 1; qv[("h", j)] = _q
                if j >= 1:
                    _q += 1; qv[("st", j - 1)] = _q
            _q += 1; qv[("st", NO - 1)] = _q
            vv = {}
            _v = b["sV"]
            for j in range(NO):
                _v += 1; vv[("tern", j)] = _v
                _v += 1; vv[("quant", j)] = _v
            sched = []
            _cur = {"bi": 0, "ei": 0}

            def _drain_step(jq, max_emit):
                emitted = 0
                while emitted < max_emit:
                    b_el = (_cur["bi"] < len(pairs)
                            and pairs[_cur["bi"]][0] < jq)
                    e_el = (_cur["ei"] < len(pairs)
                            and pairs[_cur["ei"]][0] < jq)
                    if b_el and _cur["bi"] < _cur["ei"] + 3:
                        sched.append(("blk", _cur["bi"]))
                        _cur["bi"] += 1
                    elif e_el:
                        sched.append(("ev", _cur["ei"]))
                        _cur["ei"] += 1
                    else:
                        break
                    emitted += 1

            sched.append(("wd", 0))
            for j in range(NO):
                _drain_step(j // NOQ - 1, 10 ** 9)
                sched.append(("act", j))
                if j + 1 < NO:
                    sched.append(("wd", j + 1))
                _drain_step(j // NOQ, 2)
            _drain_step(NQ, 10 ** 9)
            av = {}
            _a = b["sA"]
            sblk = {}
            wdpos = {}
            _s = b["sS"]
            for kind, idx in sched:
                if kind == "act":
                    _a += 1; av[("sg", idx)] = _a
                    _a += 1; av[("sq", idx)] = _a
                elif kind == "ev":
                    _a += 1; av[("ev",) + pairs[idx]] = _a
                elif kind == "wd":
                    _s += 1; wdpos[idx] = _s
                else:
                    _s += 1; sblk[idx] = _s
            pv = {}
            _p = b["sP"]
            for bi, pr in enumerate(pairs):
                _p += 1; pv[pr] = _p
            gacc = {}
            _g = b["sG"]
            for bi in range(len(pairs)):
                _g += 1; gacc[bi] = _g

            with nc.Block("ph10") as blk:
                @blk.sync
                def _(sync):
                    for j in range(NO):
                        if j >= 1:
                            S.wait(sync, "sy", "sV", vv[("quant", j - 1)])
                        S.inc("sQ", sync.dma_start(out=hread[j % 2][:], in_=hscr_h[j]))
                        if j >= 1:
                            S.inc("sQ", sync.dma_start(out=hq_hbm[j - 1],
                                                       in_=hqs[(j - 1) % 2][:]))
                    S.wait(sync, "sy", "sV", vv[("quant", NO - 1)])
                    S.inc("sQ", sync.dma_start(out=hq_hbm[NO - 1],
                                               in_=hqs[(NO - 1) % 2][:]))

                @blk.gpsimd
                def _(g):
                    for bi, (q2, tg2) in enumerate(pairs):
                        S.wait(g, "gp", "sA", av[("ev", q2, tg2)])
                        if q2 >= 1:
                            S.wait(g, "gp", "sG", 16 * gacc[(q2 - 1) * NT + tg2])
                        elif bi == 0:
                            S.wait(g, "gp", "sG", 16 * b["sG"])
                        S.inc("sG", g.dma_start(out=out_h[tg2 * 128:(tg2 + 1) * 128, :],
                                                in_=outd[tg2 % 2][:], accum_op=ALU.add))
                    g.wait_ge(S.sems["sG"], 16 * S.cnt["sG"])

                @blk.scalar
                def _(scalar):
                    ei = 0
                    bi = 0

                    def emit_blk():
                        nonlocal bi
                        q3, tg3 = pairs[bi]
                        S.wait(scalar, "sc", "sQ",
                               16 * qv[("st", (q3 + 1) * NOQ - 1)])
                        if bi >= 2:
                            S.wait(scalar, "sc", "sP", pv[pairs[bi - 2]])
                        src_ap = hq_hbm.rearrange("o p t -> p o t")[
                            :, q3 * NOQ:(q3 + 1) * NOQ, tg3 * 128:(tg3 + 1) * 128]
                        S.inc("sS", scalar.dma_start(
                            out=hqb[bi % 2][:].rearrange("p (o t) -> p o t", o=NOQ),
                            in_=src_ap))
                        bi += 1

                    def emit_ev():
                        nonlocal ei
                        q2, tg2 = pairs[ei]
                        S.wait(scalar, "sc", "sP", pv[(q2, tg2)])
                        gi2 = q2 * NT + tg2
                        if gi2 >= 2:
                            S.wait(scalar, "sc", "sG", 16 * gacc[gi2 - 2])
                        S.inc("sA", scalar.activation(
                            outd[tg2 % 2][:], psD[tg2 % 2][:], AF.Identity,
                            scale=vd_tok[:, tg2:tg2 + 1]))
                        ei += 1

                    for kind, idx in sched:
                        if kind == "wd":
                            jj = idx
                            if jj >= 1:
                                S.wait(scalar, "sc", "sA", av[("sq", jj - 1)])
                            S.inc("sS", scalar.dma_start(out=wds[jj % 2][:],
                                                         in_=wd_h[jj]))
                        elif kind == "act":
                            j = idx
                            S.wait(scalar, "sc", "sS", 16 * wdpos[j])
                            if j >= 1:
                                S.wait(scalar, "sc", "sV",
                                       vv[("tern", j - 1)])
                            S.inc("sA", scalar.activation(b1[:, 0:D], wds[j % 2][:],
                                                          AF.Sign,
                                                          scale=rw_b[:, 2:3]))
                            S.inc("sA", scalar.activation(b2[:, 0:D], wds[j % 2][:],
                                                          AF.Square,
                                                          scale=rw_b[:, 2:3]))
                        elif kind == "blk":
                            assert idx == bi
                            emit_blk()
                        else:
                            assert idx == ei
                            emit_ev()

                @blk.vector
                def _(vector):
                    for j in range(NO):
                        q_ = j // NOQ
                        ol = j % NOQ
                        S.wait(vector, "ve", "sA", av[("sq", j)])
                        if q_ >= 2:
                            S.wait(vector, "ve", "sP", pv[(q_ - 2, NT - 1)])
                        vector.tensor_scalar(b2[:, 0:D], b2[:, 0:D], 0.25, None, ALU.is_ge)
                        S.inc("sV", vector.tensor_tensor(wdt[q_ % 2][ol][:], b1[:, 0:D],
                                                         b2[:, 0:D], ALU.mult))
                        S.wait(vector, "ve", "sQ", 16 * qv[("h", j)])
                        if j >= 1:
                            S.wait(vector, "ve", "sQ",
                                   16 * qv[("st", j - 1)])
                        TH2 = T // 2
                        last_q = None
                        for hh in range(2):
                            sl = slice(hh * TH2, (hh + 1) * TH2)
                            vector.tensor_tensor(q1[:], hread[j % 2][:, sl],
                                                 rhb[:, sl], ALU.mult)
                            vector.tensor_scalar(q1[:], q1[:], 256.0, None, ALU.add)
                            vector.tensor_copy(hq32[:], q1[:])
                            vector.tensor_copy(q2[:], hq32[:])
                            vector.tensor_tensor(q1[:], q1[:], q2[:], ALU.subtract)
                            vector.tensor_scalar(q1[:], q1[:], 0.5, None, ALU.is_ge)
                            last_q = vector.scalar_tensor_tensor(
                                hqs[j % 2][:, sl], q1[:], -256.0, q2[:],
                                ALU.add, ALU.add)
                        S.inc("sV", last_q)

                @blk.tensor
                def _(tensor):
                    for bi, (q_, tg) in enumerate(pairs):
                        S.wait(tensor, "te", "sV", vv[("tern", (q_ + 1) * NOQ - 1)])
                        S.wait(tensor, "te", "sS", 16 * sblk[bi])
                        gi2 = q_ * NT + tg
                        if gi2 >= 2:
                            S.wait(tensor, "te", "sA", av[("ev", *pairs[gi2 - 2])])
                        last = None
                        for ol in range(NOQ):
                            for db in range(NDB):
                                last = tensor.matmul(
                                    psD[tg % 2][:, db * 512:(db + 1) * 512],
                                    hqb[bi % 2][:, ol * 128:(ol + 1) * 128],
                                    wdt[q_ % 2][ol][:, db * 512:(db + 1) * 512],
                                    start=(ol == 0), stop=(ol == NOQ - 1))
                        S.inc("sP", last)
        S.check(b, sQ=2 * NO, sA=2 * NO + NQ * NT, sV=2 * NO, sP=NQ * NT,
                sG=NQ * NT, sS=NQ * NT + NO)
        rhb_stack.close()

    return nc


def _tile_slabs(w_t, nslab):
    rows, K = w_t.shape
    assert rows == nslab * 128 and K % 128 == 0
    nd = K // 128
    a = w_t.reshape(nslab, 128, nd, 128)
    return np.ascontiguousarray(a.transpose(0, 3, 2, 1).reshape(nslab, 128, K))


def _rope_tables(T, rot):
    try:
        import jax
        import jax.numpy as jnp
        with jax.default_device(jax.devices("cpu")[0]):
            inv = 1.0 / 10000.0 ** (jnp.arange(0, rot, 2, dtype=jnp.float32) / rot)
            freqs = jnp.arange(T, dtype=jnp.float32)[:, None] * inv[None, :]
            c = np.asarray(jnp.cos(freqs), dtype=np.float32)
            s = np.asarray(jnp.sin(freqs), dtype=np.float32)
        return c, s
    except Exception:
        a = np.arange(0, rot, 2, dtype=np.float32) / np.float32(rot)
        inv = (np.float32(1.0) / (np.float32(10000.0) ** a)).astype(np.float32)
        freqs = np.arange(T, dtype=np.float32)[:, None] * inv[None, :]
        return (np.cos(freqs).astype(np.float32), np.sin(freqs).astype(np.float32))


def _perm(D):
    rot = D // 2
    return np.concatenate([np.arange(0, rot, 2), np.arange(1, rot, 2),
                           np.arange(rot, D)])


def host_prep(c, inputs):
    D, T, DFF = c.D, c.T, c.DFF
    ND, NPC, NO = c.ND, c.NPC, c.NO
    W_in = np.asarray(inputs["L0_attn_W_in"], dtype=np.float32)
    b_in = np.asarray(inputs["L0_attn_b_in"], dtype=np.float32)
    W_out = np.asarray(inputs["L0_attn_W_out"], dtype=np.float32)
    b_out = np.asarray(inputs["L0_attn_b_out"], dtype=np.float32)
    w_gate = np.asarray(inputs["L0_ffn_w_gate"], dtype=np.float32)
    w_up = np.asarray(inputs["L0_ffn_w_up"], dtype=np.float32)
    w_down = np.asarray(inputs["L0_ffn_w_down"], dtype=np.float32)

    perm = _perm(D)
    Wq = W_in[0:D][perm]
    Wk = W_in[D:2 * D][perm]
    Wv = W_in[2 * D:3 * D][perm]
    bq = b_in[0:D][perm]
    bk = b_in[D:2 * D][perm]
    bv = b_in[2 * D:3 * D][perm]

    def kvq_rows(Wm):
        chunks = Wm.reshape(ND, 128, D)
        order = []
        for i in range(NPC):
            order.append(i)
            order.append(NPC + i)
        order += list(range(2 * NPC, ND))
        return chunks[order].reshape(D, D)

    def kvq_bias(bm):
        chunks = bm.reshape(ND, 128)
        order = []
        for i in range(NPC):
            order.append(i)
            order.append(NPC + i)
        order += list(range(2 * NPC, ND))
        return chunks[order].reshape(D)

    rows = np.concatenate([kvq_rows(Wk), Wv, kvq_rows(Wq)], axis=0)
    bias = np.concatenate([kvq_bias(bk), bv, kvq_bias(bq)])

    w_qkv_t = _tile_slabs(rows, 3 * ND)
    bqkv_t = np.ascontiguousarray(bias.reshape(3 * ND, 128).T)

    w_out_t = np.ascontiguousarray(W_out[:, perm].T)
    b_out_row = np.ascontiguousarray(b_out.reshape(1, D))

    w_gate_t = _tile_slabs(w_gate, NO)
    w_up_t = _tile_slabs(w_up, NO)
    w_down_t = np.ascontiguousarray(w_down.T.reshape(NO, 128, D))

    rot = D // 2
    cos32, sin32 = _rope_tables(T, rot)
    cos_t = np.ascontiguousarray(cos32.T.reshape(NPC, 128, T))
    sin_t = np.ascontiguousarray(sin32.T.reshape(NPC, 128, T))

    eyef = np.eye(128, dtype=np.float32)

    shared = dict(w_qkv_t=w_qkv_t, bqkv_t=bqkv_t, w_out_t=w_out_t,
                  b_out_row=b_out_row, w_gate_t=w_gate_t, w_up_t=w_up_t,
                  w_down_t=w_down_t, cos_t=cos_t, sin_t=sin_t, eyef=eyef)
    return shared


def kernel(L0_attn_X, L0_attn_W_in, L0_attn_b_in, L0_attn_W_out, L0_attn_b_out,
           L0_ffn_w_gate, L0_ffn_w_up, L0_ffn_w_down):
    X = np.ascontiguousarray(np.asarray(L0_attn_X, dtype=np.float32))
    B, T, D = X.shape
    DFF = L0_ffn_w_gate.shape[0]
    c = Cfg(T=T, D=D, DFF=DFF)
    shared = host_prep(c, dict(
        L0_attn_W_in=L0_attn_W_in, L0_attn_b_in=L0_attn_b_in,
        L0_attn_W_out=L0_attn_W_out, L0_attn_b_out=L0_attn_b_out,
        L0_ffn_w_gate=L0_ffn_w_gate, L0_ffn_w_up=L0_ffn_w_up,
        L0_ffn_w_down=L0_ffn_w_down))
    nc = build_program(c)
    in_maps = [dict(shared, x=np.ascontiguousarray(X[b_])) for b_ in range(B)]
    res = run_bass_kernel_spmd(nc, in_maps, core_ids=list(range(N_CORES)))
    return np.stack([np.asarray(res.results[b_]["out"]) for b_ in range(B)], axis=0)


# version 21
# speedup vs baseline: 1.0068x; 1.0068x over previous
import contextlib
import numpy as np
import ml_dtypes
import concourse.bass as bass
import concourse.mybir as mybir
from concourse.bass_utils import run_bass_kernel_spmd

F32 = mybir.dt.float32
BF16 = mybir.dt.bfloat16
AF = mybir.ActivationFunctionType
ALU = mybir.AluOpType
AX = mybir.AxisListType

N_CORES = 8


class Cfg:
    def __init__(self, T=2048, D=2048, DFF=8192):
        assert T % 512 == 0 and D % 512 == 0 and DFF % 512 == 0
        self.T, self.D, self.DFF = T, D, DFF
        self.NT = T // 128
        self.ND = D // 128
        self.NE = 3 * self.ND
        self.NO = DFF // 128
        self.NPC = (D // 4) // 128
        self.NQ = 4
        assert self.NO % self.NQ == 0
        self.NOQ = self.NO // self.NQ
        self.NTB = T // 512
        self.NDB = D // 512


class St:
    def __init__(self, nc, stack):
        self.nc = nc
        self.sems = {}
        self.cnt = {}
        self.waited = {}
        for name in ("sQ", "sG", "sP", "sV", "sA", "sW", "sS", "sU"):
            self.sems[name] = stack.enter_context(nc.semaphore(name))
            self.cnt[name] = 0

    def inc(self, name, inst):
        if name in ("sQ", "sG", "sS"):
            inst.then_inc(self.sems[name], 16)
        else:
            inst.then_inc(self.sems[name], 1)
        self.cnt[name] += 1

    def snap(self):
        return dict(self.cnt)

    def wait(self, eng, engname, name, val):
        if val <= 0:
            return
        key = (engname, name)
        if self.waited.get(key, 0) >= val:
            return
        eng.wait_ge(self.sems[name], val)
        self.waited[key] = val

    def fence(self, eng, engname, inst):
        self.inc("sW", inst)
        self.wait(eng, engname, "sW", self.cnt["sW"])

    def check(self, base, **expect):
        for k, v in expect.items():
            got = self.cnt[k] - base[k]
            assert got == v, f"sem {k}: planned {v}, emitted {got}"


def _preload_tables(scalar, nc, *funcs):
    from concourse.hw_specs import get_activation_tables
    for idx, (_name, fs) in enumerate(get_activation_tables(nc.m.arch).items()):
        if all(f in fs for f in funcs):
            return scalar.add_instruction(mybir.InstLoadActFuncSet(
                name=nc.get_next_instruction_name(), act_func_set_id=idx,
                ins=[], outs=[]))
    raise ValueError(f"no activation table set contains {funcs}")


def build_program(c: Cfg):
    nc = bass.Bass("TRN2", target_bir_lowering=False, detect_race_conditions=False)
    T, D, DFF = c.T, c.D, c.DFF
    NT, ND, NE, NO, NPC = c.NT, c.ND, c.NE, c.NO, c.NPC
    NQ, NOQ, NTB, NDB = c.NQ, c.NOQ, c.NTB, c.NDB

    x_h = nc.dram_tensor("x", [T, D], F32, kind="ExternalInput").ap()
    wqkv_h = nc.dram_tensor("w_qkv_t", [NE, 128, D], F32, kind="ExternalInput").ap()
    bqkv_h = nc.dram_tensor("bqkv_t", [128, NE], F32, kind="ExternalInput").ap()
    wout_h = nc.dram_tensor("w_out_t", [D, D], F32, kind="ExternalInput").ap()
    bout_h = nc.dram_tensor("b_out_row", [1, D], F32, kind="ExternalInput").ap()
    wg_h = nc.dram_tensor("w_gate_t", [NO, 128, D], F32, kind="ExternalInput").ap()
    wu_h = nc.dram_tensor("w_up_t", [NO, 128, D], F32, kind="ExternalInput").ap()
    wd_h = nc.dram_tensor("w_down_t", [NO, 128, D], F32, kind="ExternalInput").ap()
    cos_h = nc.dram_tensor("cos_t", [NPC, 128, T], F32, kind="ExternalInput").ap()
    sin_h = nc.dram_tensor("sin_t", [NPC, 128, T], F32, kind="ExternalInput").ap()
    eyef_h = nc.dram_tensor("eyef", [128, 128], F32, kind="ExternalInput").ap()

    out_h = nc.dram_tensor("out", [T, D], F32, kind="ExternalOutput").ap()
    qpark_h = nc.dram_tensor("qpark", [ND, 128, T], BF16, kind="Internal").ap()
    ypark_h = nc.dram_tensor("ypark", [ND, 128, T], BF16, kind="Internal").ap()
    hscr_h = nc.dram_tensor("hscr", [NO, 128, T], F32, kind="Internal").ap()
    hq_hbm = nc.dram_tensor("hq_hbm", [NO, 128, T], BF16, kind="Internal").ap()

    ctx = contextlib.ExitStack()
    with ctx:
        S = St(nc, ctx)
        al = ctx.enter_context

        eyef = al(nc.sbuf_tensor("eyef_s", [128, 128], F32))
        ones = al(nc.sbuf_tensor("ones_s", [128, 128], F32))
        bq_s = al(nc.sbuf_tensor("bq_s", [128, NE], F32))
        sx_sc = al(nc.sbuf_tensor("sx_sc", [128, NT], F32))
        stat_st = al(nc.sbuf_tensor("stat_st", [NT, 128], F32))
        sw_b = al(nc.sbuf_tensor("sw_b", [128, 3], F32))
        rw_b = al(nc.sbuf_tensor("rw_b", [128, 3], F32))
        A_c = al(nc.sbuf_tensor("A_c", [128, ND], F32))
        cst = al(nc.sbuf_tensor("cst", [128, 4], F32))
        sh_sc = al(nc.sbuf_tensor("sh_sc", [128, NT], F32))
        rh_tok = al(nc.sbuf_tensor("rh_tok", [128, NT], F32))
        vd_tok = al(nc.sbuf_tensor("vd_tok", [128, NT], F32))
        accabs = al(nc.sbuf_tensor("accabs", [128, 3], F32))
        rtmp_all = al(nc.sbuf_tensor("rtmp_all", [128, 3 * c.NO], F32))

        xnT_stack = contextlib.ExitStack()
        xnT = xnT_stack.enter_context(nc.sbuf_tensor("xnT", [128, ND * T], BF16))

        with nc.Block("ph0") as blk:
            @blk.sync
            def _(sync):
                S.inc("sQ", sync.dma_start(out=eyef[:], in_=eyef_h))
                S.inc("sQ", sync.dma_start(out=bq_s[:], in_=bqkv_h))

            @blk.vector
            def _(vector):
                vector.memset(ones[:], 1.0)
                vector.memset(cst[:, 0:1], 1e-5)
                vector.memset(cst[:, 1:2], 0.5)
                vector.memset(cst[:, 2:3], -1.0)
                vector.memset(cst[:, 3:4], -0.5)

            @blk.scalar
            def _(scalar):
                _preload_tables(scalar, nc, AF.Ln, AF.Exp, AF.Identity, AF.Square,
                                AF.Sign)

        NG = ND // 4
        b = S.snap()
        qpos = {}
        _q = b["sQ"]
        _q += 1; qpos[("x", 0)] = _q
        _q += 1; qpos[("x", 1)] = _q
        for _t in range(c.NT):
            _q += 1; qpos[("acc", _t)] = _q
            if _t + 2 < c.NT:
                _q += 1; qpos[("x", _t + 2)] = _q
            for _g in range(ND // 4):
                _q += 1; qpos[("park", _t, _g)] = _q
        bG0 = b["sG"]
        NSL = 3 * NO
        SPT = (NSL + NT - 1) // NT
        pn = lambda tg, k: b["sP"] + 2 * NG * tg + k
        dve_order = [("A", 0)]
        for s_ in range(NT):
            if s_ + 1 < NT:
                dve_order.append(("A", s_ + 1))
            dve_order.append(("B", s_))
            if s_ >= 1:
                dve_order.append(("D", s_ - 1))
            dve_order.append(("C", s_))
        dve_order.append(("D", NT - 1))
        VINC = {"A": ["v1"], "B": ["v2"], "C": ["v3", "v4", "v5"], "D": ["v6"]}
        vpos = {}
        _v = b["sV"]
        for st_, t_ in dve_order:
            for nm_ in VINC[st_]:
                _v += 1
                vpos[(nm_, t_)] = _v
        apos = {}
        _a = b["sA"]
        for t_ in range(NT + 1):
            if t_ < NT:
                for nm_ in ("a1", "a2", "a3"):
                    _a += 1
                    apos[(nm_, t_)] = _a
            if t_ >= 1:
                for g_ in range(NG):
                    _a += 1
                    apos[("ev_x", t_ - 1, g_)] = _a
                for g_ in range(NG):
                    _a += 1
                    apos[("ev_q", t_ - 1, g_)] = _a
        with contextlib.ExitStack() as ph:
            xbuf = [ph.enter_context(nc.sbuf_tensor(f"xb{i}", [128, D], F32)) for i in range(2)]
            accb = [ph.enter_context(nc.sbuf_tensor(f"accb{i}", [128, D], F32)) for i in range(2)]
            xnf = [ph.enter_context(nc.sbuf_tensor(f"xnf{i}", [128, D], F32))
                   for i in range(3)]
            tqb = [ph.enter_context(nc.sbuf_tensor(f"tqb{i}", [128, D], F32))
                   for i in range(2)]
            mqb = ph.enter_context(nc.sbuf_tensor("mqb", [128, D], F32))
            qbf = [ph.enter_context(nc.sbuf_tensor(f"qbf{i}", [128, D], F32))
                   for i in range(2)]
            c32i = ph.enter_context(nc.sbuf_tensor("c32i", [128, D], mybir.dt.int32))
            wpre = [ph.enter_context(nc.sbuf_tensor(f"wpre{i}", [128, D], F32))
                    for i in range(2)]
            qst = [ph.enter_context(nc.sbuf_tensor(f"qst{i}", [128, 512], BF16)) for i in range(2)]
            s1c = ph.enter_context(nc.sbuf_tensor("s1c", [128, 1], F32))
            mc = ph.enter_context(nc.sbuf_tensor("mc", [128, 1], F32))
            vcc = ph.enter_context(nc.sbuf_tensor("vcc", [128, 1], F32))
            lnc = ph.enter_context(nc.sbuf_tensor("lnc", [128, 1], F32))
            rc = ph.enter_context(nc.sbuf_tensor("rc", [128, 1], F32))
            axc = ph.enter_context(nc.sbuf_tensor("axc", [128, 1], F32))
            rxc = ph.enter_context(nc.sbuf_tensor("rxc", [128, 1], F32))
            psT = [ph.enter_context(nc.psum_tensor(f"psT{i}", [128, 512], F32)) for i in range(4)]

            with nc.Block("ph2") as blk:
                @blk.gpsimd
                def _(g):
                    srcs3 = [wg_h, wu_h, wd_h]
                    for j in range(NSL):
                        mm, sl = divmod(j, NO)
                        if j >= 2:
                            S.wait(g, "gp", "sU", j - 1)
                        S.inc("sG", g.dma_start(out=wpre[j % 2][:], in_=srcs3[mm][sl]))

                @blk.sync
                def _(sync):
                    def xload(tt):
                        S.inc("sQ", sync.dma_start(
                            out=xbuf[tt % 2][:],
                            in_=x_h[tt * 128:(tt + 1) * 128, :]))

                    xload(0)
                    if NT > 1:
                        xload(1)
                    for tg in range(NT):
                        S.wait(sync, "sy", "sV", vpos[("v4", tg)])
                        S.inc("sQ", sync.dma_start(out=out_h[tg * 128:(tg + 1) * 128, :],
                                                   in_=accb[tg % 2][:]))
                        if tg + 2 < NT:
                            xload(tg + 2)
                        for g in range(NG):
                            S.wait(sync, "sy", "sA", apos[("ev_q", tg, g)])
                            dst = qpark_h.rearrange("d p t -> p d t")[
                                :, g * 4:(g + 1) * 4, tg * 128:(tg + 1) * 128]
                            S.inc("sQ", sync.dma_start(
                                out=dst,
                                in_=qst[g % 2][:].rearrange("p (d t) -> p d t", d=4)))

                @blk.vector
                def _(vector):
                    for st_, t in dve_order:
                        if st_ == "A":
                            S.wait(vector, "ve", "sQ", 16 * qpos[("x", t)])
                            if t >= 3:
                                S.wait(vector, "ve", "sP", pn(t - 3, NG))
                            xb = xbuf[t % 2]
                            S.fence(vector, "ve", vector.reduce_sum(s1c[:], xb[:],
                                                                    axis=AX.X))
                            S.fence(vector, "ve", vector.tensor_scalar(
                                mc[:], s1c[:], 1.0 / D, None, ALU.mult))
                            S.inc("sV", vector.tensor_scalar(
                                xnf[t % 3][:], xb[:], mc[:], None, ALU.subtract))
                        elif st_ == "B":
                            S.wait(vector, "ve", "sA", apos[("a1", t)])
                            S.fence(vector, "ve", vector.reduce_sum(
                                s1c[:], tqb[t % 2][:], axis=AX.X))
                            S.inc("sV", vector.tensor_scalar(vcc[:], s1c[:], 1.0 / D,
                                                             None, ALU.mult))
                        elif st_ == "C":
                            S.wait(vector, "ve", "sA", apos[("a2", t)])
                            S.inc("sV", vector.tensor_scalar(
                                xnf[t % 3][:], xnf[t % 3][:], rc[:], None, ALU.mult))
                            if t >= 2:
                                S.wait(vector, "ve", "sQ",
                                       16 * qpos[("acc", t - 2)])
                            S.inc("sV", vector.tensor_tensor(
                                accb[t % 2][:], xbuf[t % 2][:], xnf[t % 3][:], ALU.add))
                            S.fence(vector, "ve", vector.reduce_max(
                                axc[:], xnf[t % 3][:], axis=AX.X,
                                apply_absolute_value=True))
                            S.inc("sV", vector.tensor_scalar(
                                sx_sc[:, t:t + 1], axc[:], 1e-5, 1.0 / 127.0,
                                ALU.max, ALU.mult))
                        else:
                            S.wait(vector, "ve", "sA", apos[("a3", t)])
                            if t >= 2:
                                S.wait(vector, "ve", "sP", pn(t - 1, 0))
                            zz = tqb[t % 2]
                            vector.tensor_scalar(zz[:], xnf[t % 3][:], rxc[:], 256.0,
                                                 ALU.mult, ALU.add)
                            vector.tensor_copy(c32i[:], zz[:])
                            vector.tensor_copy(mqb[:], c32i[:])
                            vector.tensor_tensor(zz[:], zz[:], mqb[:], ALU.subtract)
                            vector.tensor_scalar(zz[:], zz[:], 0.5, None, ALU.is_ge)
                            S.inc("sV", vector.scalar_tensor_tensor(
                                qbf[t % 2][:], zz[:], -256.0, mqb[:], ALU.add,
                                ALU.add))

                @blk.scalar
                def _(scalar):
                    def emit_evacs(te):
                        for g in range(NG):
                            S.wait(scalar, "sc", "sP", pn(te, g + 1))
                            dst = xnT[:].rearrange("p (d t) -> p d t", d=ND)[
                                :, g * 4:(g + 1) * 4, te * 128:(te + 1) * 128]
                            gs = 2 * NG * te + g
                            S.inc("sA", scalar.activation(
                                dst,
                                psT[gs % 4][:].rearrange("p (d t) -> p d t", d=4),
                                AF.Identity))
                        for g in range(NG):
                            S.wait(scalar, "sc", "sP", pn(te, NG + g + 1))
                            kq = te * NG + g
                            if kq >= 2:
                                tgp, gp = divmod(kq - 2, NG)
                                S.wait(scalar, "sc", "sQ",
                                       16 * qpos[("park", tgp, gp)])
                            gs = 2 * NG * te + NG + g
                            S.inc("sA", scalar.activation(
                                qst[g % 2][:], psT[gs % 4][:], AF.Identity))

                    def emit_reds(tb):
                        rj = tb * SPT
                        rend = min((tb + 1) * SPT, NSL)
                        while rj < rend:
                            S.wait(scalar, "sc", "sG", 16 * (bG0 + rj + 1))
                            S.inc("sU", scalar.activation(
                                wpre[rj % 2][:], wpre[rj % 2][:], AF.Abs,
                                accum_out=rtmp_all[:, rj:rj + 1]))
                            rj += 1

                    for tg in range(NT):
                        S.wait(scalar, "sc", "sV", vpos[("v1", tg)])
                        if tg >= 2:
                            S.wait(scalar, "sc", "sV", vpos[("v6", tg - 2)])
                        S.inc("sA", scalar.activation(tqb[tg % 2][:], xnf[tg % 3][:],
                                                      AF.Square))
                        S.wait(scalar, "sc", "sV", vpos[("v2", tg)])
                        scalar.activation(lnc[:], vcc[:], AF.Ln, bias=cst[:, 0:1])
                        S.inc("sA", scalar.activation(rc[:], lnc[:], AF.Exp,
                                                      scale=cst[:, 3:4]))
                        S.wait(scalar, "sc", "sV", vpos[("v5", tg)])
                        scalar.activation(lnc[:], sx_sc[:, tg:tg + 1], AF.Ln)
                        S.inc("sA", scalar.activation(rxc[:], lnc[:], AF.Exp,
                                                      scale=cst[:, 2:3]))
                        if tg >= 1:
                            emit_evacs(tg - 1)
                        emit_reds(tg)
                    emit_evacs(NT - 1)

                @blk.tensor
                def _(tensor):
                    def guard(gs):
                        if gs >= 4:
                            tp_, r = divmod(gs - 4, 2 * NG)
                            key = ("ev_x", tp_, r) if r < NG else ("ev_q", tp_, r - NG)
                            S.wait(tensor, "te", "sA", apos[key])

                    for tg in range(NT):
                        S.wait(tensor, "te", "sV", vpos[("v3", tg)])
                        for g in range(NG):
                            gs = 2 * NG * tg + g
                            guard(gs)
                            last = None
                            for k in range(4):
                                dc = g * 4 + k
                                last = tensor.transpose(
                                    psT[gs % 4][:, k * 128:(k + 1) * 128],
                                    xnf[tg % 3][:, dc * 128:(dc + 1) * 128], eyef[:])
                            S.inc("sP", last)
                        S.wait(tensor, "te", "sV", vpos[("v6", tg)])
                        for g in range(NG):
                            gs = 2 * NG * tg + NG + g
                            guard(gs)
                            last = None
                            for k in range(4):
                                dc = g * 4 + k
                                last = tensor.transpose(
                                    psT[gs % 4][:, k * 128:(k + 1) * 128],
                                    qbf[tg % 2][:, dc * 128:(dc + 1) * 128], eyef[:])
                            S.inc("sP", last)
        S.check(b, sQ=(2 + NG) * NT, sV=6 * NT, sA=(3 + 2 * NG) * NT,
                sP=2 * NG * NT, sG=NSL, sW=4 * NT, sU=NSL)

        b = S.snap()
        with contextlib.ExitStack() as ph:
            ps_s = ph.enter_context(nc.psum_tensor("ps_s", [NT, 128], F32))
            with nc.Block("ph2b") as blk:
                @blk.tensor
                def _(tensor):
                    S.inc("sP", tensor.transpose(ps_s[:], sx_sc[:], eyef[:]))

                @blk.vector
                def _(vector):
                    S.wait(vector, "ve", "sP", b["sP"] + 1)
                    S.inc("sV", vector.tensor_copy(stat_st[0:NT, :], ps_s[:]))

        S.check(b, sP=1, sV=1)
        wm_out_base = S.cnt["sQ"]
        wm_statst = S.cnt["sV"]

        b = S.snap()
        with contextlib.ExitStack() as ph:
            sums_b = ph.enter_context(nc.sbuf_tensor("sums_b", [128, 3], F32))
            ps3 = ph.enter_context(nc.psum_tensor("ps3", [128, 3], F32))
            with nc.Block("ph3") as blk:
                @blk.tensor
                def _(tensor):
                    S.wait(tensor, "te", "sV", b["sV"] + 1)
                    S.inc("sP", tensor.matmul(ps3[:], ones[:], accabs[:], start=True,
                                              stop=True))

                @blk.vector
                def _(vector):
                    for mm in range(3):
                        r = vector.reduce_sum(accabs[:, mm:mm + 1],
                                              rtmp_all[:, mm * NO:(mm + 1) * NO],
                                              axis=AX.X)
                    S.inc("sV", r)
                    S.wait(vector, "ve", "sP", b["sP"] + 1)
                    S.fence(vector, "ve", vector.tensor_copy(sums_b[:], ps3[:]))
                    S.inc("sV", vector.tensor_scalar(sw_b[:], sums_b[:],
                                                     1.0 / float(DFF * D), 1e-5,
                                                     ALU.mult, ALU.add))

                @blk.scalar
                def _(scalar):
                    S.wait(scalar, "sc", "sV", b["sV"] + 2)
                    scalar.activation(sums_b[:], sw_b[:], AF.Ln)
                    S.inc("sA", scalar.activation(rw_b[:], sums_b[:], AF.Exp,
                                                  scale=cst[:, 2:3]))
        S.check(b, sV=2, sP=1, sA=1)

        def rot_order():
            order = []
            for i in range(NPC):
                order.append(("ev", i))
                order.append(("od", i))
            for j in range(ND - 2 * NPC):
                order.append(("pa", j))
            return order

        def emit_qkv_mms(tensor, ps, wslab):
            last = None
            for dc in range(ND):
                for tb in range(NTB):
                    last = tensor.matmul(
                        ps[:, tb * 512:(tb + 1) * 512],
                        wslab[:, dc * 128:(dc + 1) * 128],
                        xnT[:, dc * T + tb * 512: dc * T + tb * 512 + 512],
                        start=(dc == 0), stop=(dc == ND - 1))
            return last

        KK_stack = contextlib.ExitStack()
        KK = KK_stack.enter_context(nc.sbuf_tensor("KK", [128, ND * T], BF16))
        rbb_stack = contextlib.ExitStack()
        rbb = rbb_stack.enter_context(nc.sbuf_tensor("rbb", [128, T], F32))

        b = S.snap()
        with contextlib.ExitStack() as ph:
            wsl = [ph.enter_context(nc.sbuf_tensor(f"wsl4_{i}", [128, D], BF16)) for i in range(2)]
            kev = ph.enter_context(nc.sbuf_tensor("kev", [128, T], F32))
            kod = ph.enter_context(nc.sbuf_tensor("kod", [128, T], F32))
            cosb = ph.enter_context(nc.sbuf_tensor("cosb", [128, T], F32))
            sinb = ph.enter_context(nc.sbuf_tensor("sinb", [128, T], F32))
            sqtmp = ph.enter_context(nc.sbuf_tensor("sqtmp", [128, T], F32))
            e1 = ph.enter_context(nc.sbuf_tensor("e1", [128, T], F32))
            ksqa = ph.enter_context(nc.sbuf_tensor("ksqa", [128, T], F32))
            psK = [ph.enter_context(nc.psum_tensor(f"psK{i}", [128, T], F32)) for i in range(2)]
            order = rot_order()
            APC = {"ev": 1, "od": 1, "pa": 2}
            VPC = {"ev": 0, "od": 2, "pa": 1}
            acum = [0]
            vcum = [0]
            for kind, _ in order:
                acum.append(acum[-1] + APC[kind])
                vcum.append(vcum[-1] + VPC[kind])

            with nc.Block("ph4") as blk:
                @blk.gpsimd
                def _(g):
                    for ci in range(ND):
                        S.wait(g, "gp", "sP", (b["sP"] + ci - 1) if ci >= 2 else 0)
                        S.inc("sG", g.dma_start(out=wsl[ci % 2][:], in_=wqkv_h[ci]))

                @blk.sync
                def _(sync):
                    for i in range(NPC):
                        if i >= 1:
                            S.wait(sync, "sy", "sV", b["sV"] + 1 + vcum[2 * i])
                        S.inc("sQ", sync.dma_start(out=cosb[:], in_=cos_h[i]))
                        S.inc("sQ", sync.dma_start(out=sinb[:], in_=sin_h[i]))

                @blk.tensor
                def _(tensor):
                    for ci in range(ND):
                        S.wait(tensor, "te", "sG", 16 * (b["sG"] + ci + 1))
                        if ci >= 2:
                            S.wait(tensor, "te", "sA", b["sA"] + acum[ci - 1])
                        S.inc("sP", emit_qkv_mms(tensor, psK[ci % 2], wsl[ci % 2]))
                    S.wait(tensor, "te", "sV", b["sV"] + 1 + vcum[ND])
                    S.wait(tensor, "te", "sA", b["sA"] + acum[ND])
                    last = None
                    for tb in range(NTB):
                        last = tensor.matmul(psK[0][:, tb * 512:(tb + 1) * 512], ones[:],
                                             ksqa[:, tb * 512:(tb + 1) * 512],
                                             start=True, stop=True)
                    S.inc("sP", last)

                @blk.scalar
                def _(scalar):
                    for ci, (kind, i) in enumerate(order):
                        S.wait(scalar, "sc", "sP", b["sP"] + ci + 1)
                        bias = bq_s[:, ci:ci + 1]
                        S.wait(scalar, "sc", "sV", b["sV"] + 1 + vcum[ci])
                        if kind == "ev":
                            S.inc("sA", scalar.activation(kev[:], psK[ci % 2][:], AF.Identity,
                                                          bias=bias))
                        elif kind == "od":
                            S.inc("sA", scalar.activation(kod[:], psK[ci % 2][:], AF.Identity,
                                                          bias=bias))
                        else:
                            slot = 2 * NPC + i
                            S.inc("sA", scalar.activation(KK[:, slot * T:(slot + 1) * T],
                                                          psK[ci % 2][:], AF.Identity,
                                                          bias=bias))
                            S.inc("sA", scalar.activation(sqtmp[:], psK[ci % 2][:], AF.Square,
                                                          bias=bias))
                    S.wait(scalar, "sc", "sP", b["sP"] + ND + 1)
                    S.wait(scalar, "sc", "sV", b["sV"] + 1 + vcum[ND])
                    scalar.activation(e1[:], psK[0][:], AF.Ln)
                    S.inc("sA", scalar.activation(rbb[:], e1[:], AF.Exp,
                                                  scale=cst[:, 3:4]))

                @blk.vector
                def _(vector):
                    S.inc("sV", vector.memset(ksqa[:], 0.0))
                    for ci, (kind, i) in enumerate(order):
                        if kind == "od":
                            S.wait(vector, "ve", "sA", b["sA"] + acum[ci + 1])
                            S.wait(vector, "ve", "sQ", 16 * (b["sQ"] + 2 * (i + 1)))
                            ev_s, od_s = i, NPC + i
                            vector.tensor_tensor(e1[:], kev[:], cosb[:], ALU.mult)
                            vector.tensor_tensor(sqtmp[:], kod[:], sinb[:], ALU.mult)
                            vector.tensor_tensor(e1[:], e1[:], sqtmp[:], ALU.subtract)
                            vector.tensor_tensor(sqtmp[:], e1[:], e1[:], ALU.mult)
                            vector.tensor_tensor(ksqa[:], ksqa[:], sqtmp[:], ALU.add)
                            S.inc("sV", vector.tensor_copy(KK[:, ev_s * T:(ev_s + 1) * T],
                                                           e1[:]))
                            vector.tensor_tensor(e1[:], kod[:], cosb[:], ALU.mult)
                            vector.tensor_tensor(sqtmp[:], kev[:], sinb[:], ALU.mult)
                            vector.tensor_tensor(e1[:], e1[:], sqtmp[:], ALU.add)
                            vector.tensor_tensor(sqtmp[:], e1[:], e1[:], ALU.mult)
                            vector.tensor_tensor(ksqa[:], ksqa[:], sqtmp[:], ALU.add)
                            S.inc("sV", vector.tensor_copy(KK[:, od_s * T:(od_s + 1) * T],
                                                           e1[:]))
                        elif kind == "pa":
                            S.wait(vector, "ve", "sA", b["sA"] + acum[ci + 1])
                            S.inc("sV", vector.tensor_tensor(ksqa[:], ksqa[:], sqtmp[:],
                                                             ALU.add))
                    S.wait(vector, "ve", "sA", b["sA"] + acum[ND] + 1)
                    S.inc("sV", vector.tensor_scalar(rbb[:], rbb[:], 1e12, None, ALU.min))
        S.check(b, sG=ND, sP=ND + 1, sA=acum[ND] + 1, sV=1 + vcum[ND] + 1, sQ=2 * NPC)

        b = S.snap()
        with contextlib.ExitStack() as ph:
            wsl = [ph.enter_context(nc.sbuf_tensor(f"wsl5_{i}", [128, D], BF16)) for i in range(2)]
            vch = [ph.enter_context(nc.sbuf_tensor(f"vch{i}", [128, T], F32)) for i in range(2)]
            t1b = ph.enter_context(nc.sbuf_tensor("t1b", [128, T], F32))
            psV = [ph.enter_context(nc.psum_tensor(f"psV{i}", [128, T], F32)) for i in range(2)]
            with nc.Block("ph5") as blk:
                @blk.gpsimd
                def _(g):
                    for ci in range(ND):
                        S.wait(g, "gp", "sP", (b["sP"] + ci - 1) if ci >= 2 else 0)
                        S.inc("sG", g.dma_start(out=wsl[ci % 2][:], in_=wqkv_h[ND + ci]))

                @blk.tensor
                def _(tensor):
                    for ci in range(ND):
                        S.wait(tensor, "te", "sG", 16 * (b["sG"] + ci + 1))
                        if ci >= 2:
                            S.wait(tensor, "te", "sA", b["sA"] + ci - 1)
                        S.inc("sP", emit_qkv_mms(tensor, psV[ci % 2], wsl[ci % 2]))

                @blk.scalar
                def _(scalar):
                    for ci in range(ND):
                        S.wait(scalar, "sc", "sP", b["sP"] + ci + 1)
                        if ci >= 2:
                            S.wait(scalar, "sc", "sV", b["sV"] + 2 * (ci - 1) - 1)
                        S.inc("sA", scalar.activation(vch[ci % 2][:], psV[ci % 2][:], AF.Identity,
                                                      bias=bq_s[:, ND + ci:ND + ci + 1]))

                @blk.vector
                def _(vector):
                    for ci in range(ND):
                        S.wait(vector, "ve", "sA", b["sA"] + ci + 1)
                        S.inc("sV", vector.tensor_tensor(t1b[:], vch[ci % 2][:],
                                                         KK[:, ci * T:(ci + 1) * T], ALU.mult))
                        vector.tensor_tensor(t1b[:], t1b[:], rbb[:], ALU.mult)
                        S.inc("sV", vector.reduce_sum(A_c[:, ci:ci + 1], t1b[:], axis=AX.X))
        S.check(b, sG=ND, sP=ND, sA=ND, sV=2 * ND)
        rbb_stack.close()
        KK_stack.close()

        b = S.snap()
        with contextlib.ExitStack() as ph:
            wsl = [ph.enter_context(nc.sbuf_tensor(f"wsl6_{i}", [128, D], BF16)) for i in range(2)]
            kev = ph.enter_context(nc.sbuf_tensor("qev", [128, T], F32))
            kod = ph.enter_context(nc.sbuf_tensor("qod", [128, T], F32))
            cosb = ph.enter_context(nc.sbuf_tensor("cosb6", [128, T], F32))
            sinb = ph.enter_context(nc.sbuf_tensor("sinb6", [128, T], F32))
            e1 = ph.enter_context(nc.sbuf_tensor("e16", [128, T], F32))
            e2 = ph.enter_context(nc.sbuf_tensor("e26", [128, T], F32))
            ystg = [ph.enter_context(nc.sbuf_tensor(f"ystg{i}", [128, T], BF16))
                    for i in range(2)]
            psQ = [ph.enter_context(nc.psum_tensor(f"psQ{i}", [128, T], F32)) for i in range(2)]
            order = rot_order()
            VPC = {"ev": 0, "od": 2, "pa": 1}
            vcum = [0]
            stage_cum = [0]
            for kind, _ in order:
                vcum.append(vcum[-1] + VPC[kind])
                stage_cum.append(stage_cum[-1] + (2 if kind == "od" else (1 if kind == "pa" else 0)))
            NSTG = stage_cum[-1]
            assert NSTG == ND
            stage_slots = []
            for kind, i in order:
                if kind == "od":
                    stage_slots += [i, NPC + i]
                elif kind == "pa":
                    stage_slots += [2 * NPC + i]

            with nc.Block("ph6") as blk:
                @blk.gpsimd
                def _(g):
                    for ci in range(ND):
                        S.wait(g, "gp", "sP", (b["sP"] + ci - 1) if ci >= 2 else 0)
                        S.inc("sG", g.dma_start(out=wsl[ci % 2][:], in_=wqkv_h[2 * ND + ci]))

                park_done = {}
                _cnt = b["sQ"]
                for ci, (kind, i) in enumerate(order):
                    if kind == "od":
                        _cnt += 2
                        park_done[stage_cum[ci]] = None
                        park_done[stage_cum[ci] + 1] = None
                        _c2 = _cnt
                    if kind in ("od", "pa"):
                        for s in range(stage_cum[ci], stage_cum[ci + 1]):
                            _cnt += 1
                            park_done[s] = _cnt

                @blk.sync
                def _(sync):
                    for ci, (kind, i) in enumerate(order):
                        if kind == "od":
                            if i >= 1:
                                S.wait(sync, "sy", "sV", b["sV"] + vcum[2 * i])
                            S.inc("sQ", sync.dma_start(out=cosb[:], in_=cos_h[i]))
                            S.inc("sQ", sync.dma_start(out=sinb[:], in_=sin_h[i]))
                        for s in range(stage_cum[ci], stage_cum[ci + 1]):
                            S.wait(sync, "sy", "sV", b["sV"] + s + 1)
                            S.inc("sQ", sync.dma_start(out=ypark_h[stage_slots[s]],
                                                       in_=ystg[s % 2][:]))

                @blk.tensor
                def _(tensor):
                    for ci in range(ND):
                        S.wait(tensor, "te", "sG", 16 * (b["sG"] + ci + 1))
                        if ci >= 2:
                            S.wait(tensor, "te", "sA", b["sA"] + ci - 1)
                        S.inc("sP", emit_qkv_mms(tensor, psQ[ci % 2], wsl[ci % 2]))

                @blk.scalar
                def _(scalar):
                    for ci, (kind, i) in enumerate(order):
                        S.wait(scalar, "sc", "sP", b["sP"] + ci + 1)
                        bias = bq_s[:, 2 * ND + ci:2 * ND + ci + 1]
                        if kind == "ev":
                            S.wait(scalar, "sc", "sV", b["sV"] + vcum[ci])
                            S.inc("sA", scalar.activation(kev[:], psQ[ci % 2][:], AF.Identity,
                                                          bias=bias))
                        elif kind == "od":
                            S.inc("sA", scalar.activation(kod[:], psQ[ci % 2][:], AF.Identity,
                                                          bias=bias))
                        else:
                            S.wait(scalar, "sc", "sV", b["sV"] + vcum[ci])
                            S.inc("sA", scalar.activation(e1[:], psQ[ci % 2][:], AF.Identity,
                                                          bias=bias))

                @blk.vector
                def _(vector):
                    for ci, (kind, i) in enumerate(order):
                        if kind == "ev":
                            continue
                        S.wait(vector, "ve", "sA", b["sA"] + ci + 1)
                        if kind == "od":
                            S.wait(vector, "ve", "sQ", 16 * (b["sQ"] + 2 * (i + 1)))
                            s_ev = stage_cum[ci]
                            s_od = s_ev + 1
                            slot_ev, slot_od = i, NPC + i
                            for s_cur in (s_ev, s_od):
                                if s_cur >= 2:
                                    S.wait(vector, "ve", "sQ", 16 * park_done[s_cur - 2])
                            vector.tensor_tensor(e1[:], kev[:], cosb[:], ALU.mult)
                            vector.tensor_tensor(e2[:], kod[:], sinb[:], ALU.mult)
                            vector.tensor_tensor(e1[:], e1[:], e2[:], ALU.subtract)
                            S.inc("sV", vector.tensor_scalar(
                                ystg[s_ev % 2][:], e1[:], A_c[:, slot_ev:slot_ev + 1],
                                None, ALU.mult))
                            vector.tensor_tensor(e1[:], kod[:], cosb[:], ALU.mult)
                            vector.tensor_tensor(e2[:], kev[:], sinb[:], ALU.mult)
                            vector.tensor_tensor(e1[:], e1[:], e2[:], ALU.add)
                            S.inc("sV", vector.tensor_scalar(
                                ystg[s_od % 2][:], e1[:], A_c[:, slot_od:slot_od + 1],
                                None, ALU.mult))
                        else:
                            s_cur = stage_cum[ci]
                            slot = 2 * NPC + i
                            if s_cur >= 2:
                                S.wait(vector, "ve", "sQ", 16 * park_done[s_cur - 2])
                            S.inc("sV", vector.tensor_scalar(
                                ystg[s_cur % 2][:], e1[:], A_c[:, slot:slot + 1],
                                None, ALU.mult))
        S.check(b, sG=ND, sP=ND, sA=ND, sV=ND, sQ=2 * NPC + ND)
        xnT_stack.close()

        b = S.snap()
        with contextlib.ExitStack() as ph:
            wout_sb = ph.enter_context(nc.sbuf_tensor("wout_sb", [128, ND * D], BF16))
            boutr = ph.enter_context(nc.sbuf_tensor("boutr", [1, D], F32))
            yblk = [ph.enter_context(nc.sbuf_tensor(f"yblk{i}", [128, ND * 128], BF16))
                    for i in range(2)]
            outsb = [ph.enter_context(nc.sbuf_tensor(f"outsb{i}", [128, D], F32))
                     for i in range(2)]
            psY = [ph.enter_context(nc.psum_tensor(f"psY{i}", [128, D], F32)) for i in range(2)]
            with nc.Block("ph7") as blk:
                @blk.sync
                def _(sync):
                    S.inc("sQ", sync.dma_start(out=boutr[:], in_=bout_h))
                    for tg in range(NT):
                        S.wait(sync, "sy", "sP", (b["sP"] + tg - 1) if tg >= 2 else 0)
                        src_ap = ypark_h.rearrange("d p t -> p d t")[
                            :, :, tg * 128:(tg + 1) * 128]
                        S.inc("sQ", sync.dma_start(
                            out=yblk[tg % 2][:].rearrange("p (d t) -> p d t", d=ND),
                            in_=src_ap))

                @blk.gpsimd
                def _(g):
                    for dc in range(ND):
                        S.inc("sG", g.dma_start(out=wout_sb[:, dc * D:(dc + 1) * D],
                                                in_=wout_h[dc * 128:(dc + 1) * 128, :]))
                    S.wait(g, "gp", "sQ", 16 * wm_out_base)
                    for tg in range(NT):
                        S.wait(g, "gp", "sA", b["sA"] + tg + 1)
                        S.inc("sG", g.dma_start(out=out_h[tg * 128:(tg + 1) * 128, :],
                                                in_=outsb[tg % 2][:], accum_op=ALU.add))

                @blk.tensor
                def _(tensor):
                    S.wait(tensor, "te", "sG", 16 * (b["sG"] + ND))
                    S.wait(tensor, "te", "sQ", 16 * (b["sQ"] + 1))
                    for tg in range(NT):
                        S.wait(tensor, "te", "sQ", 16 * (b["sQ"] + 1 + tg + 1))
                        if tg >= 2:
                            S.wait(tensor, "te", "sA", b["sA"] + tg - 1)
                        last = None
                        for db in range(NDB):
                            for dc in range(ND):
                                tensor.matmul(
                                    psY[tg % 2][:, db * 512:(db + 1) * 512],
                                    yblk[tg % 2][:, dc * 128:(dc + 1) * 128],
                                    wout_sb[:, dc * D + db * 512: dc * D + db * 512 + 512],
                                    start=(dc == 0), stop=False)
                            last = tensor.matmul(
                                psY[tg % 2][:, db * 512:(db + 1) * 512],
                                ones[0:1, :], boutr[:, db * 512:(db + 1) * 512],
                                start=False, stop=True)
                        S.inc("sP", last)

                @blk.scalar
                def _(scalar):
                    for tg in range(NT):
                        S.wait(scalar, "sc", "sP", b["sP"] + tg + 1)
                        if tg >= 2:
                            S.wait(scalar, "sc", "sG", 16 * (b["sG"] + ND + tg - 1))
                        S.inc("sA", scalar.activation(outsb[tg % 2][:], psY[tg % 2][:],
                                                      AF.Identity))
        S.check(b, sQ=1 + NT, sG=ND + NT, sP=NT, sA=NT)

        rhb_stack = contextlib.ExitStack()
        rhb = rhb_stack.enter_context(nc.sbuf_tensor("rhb", [128, T], F32))
        gu_stack = contextlib.ExitStack()
        vgb = gu_stack.enter_context(nc.sbuf_tensor("vgb", [128, T], F32))
        vub = gu_stack.enter_context(nc.sbuf_tensor("vub", [128, T], F32))
        amaxfm = gu_stack.enter_context(nc.sbuf_tensor("amaxfm", [128, T], F32))
        amaxN = gu_stack.enter_context(nc.sbuf_tensor("amaxN", [128, T], F32))
        b = S.snap()
        with contextlib.ExitStack() as ph:
            vrow = ph.enter_context(nc.sbuf_tensor("vrow", [1, T], F32))
            sx_row = ph.enter_context(nc.sbuf_tensor("sx_row", [1, T], F32))
            psBC8 = ph.enter_context(nc.psum_tensor("psBC8", [128, T], F32))
            with nc.Block("ph7b") as blk:
                @blk.sync
                def _(sync):
                    S.inc("sQ", sync.dma_start(out=sx_row[:], in_=stat_st[0:NT, :]))

                @blk.vector
                def _(vector):
                    S.wait(vector, "ve", "sQ", 16 * (b["sQ"] + 1))
                    S.inc("sV", vector.tensor_scalar(vrow[:], sx_row[:], sw_b[0:1, 0:1],
                                                     None, ALU.mult))
                    S.wait(vector, "ve", "sP", b["sP"] + 1)
                    S.inc("sV", vector.tensor_copy(vgb[:], psBC8[:]))
                    S.inc("sV", vector.tensor_scalar(vrow[:], sx_row[:], sw_b[0:1, 1:2],
                                                     None, ALU.mult))
                    S.wait(vector, "ve", "sP", b["sP"] + 2)
                    S.inc("sV", vector.tensor_copy(vub[:], psBC8[:]))

                @blk.tensor
                def _(tensor):
                    for j in range(2):
                        S.wait(tensor, "te", "sV", b["sV"] + 1 + 2 * j)
                        last = None
                        for tb in range(NTB):
                            last = tensor.matmul(psBC8[:, tb * 512:(tb + 1) * 512],
                                                 ones[0:1, :],
                                                 vrow[:, tb * 512:(tb + 1) * 512],
                                                 start=True, stop=True)
                        S.inc("sP", last)
        S.check(b, sV=4, sP=2, sQ=1)

        qT_stack = contextlib.ExitStack()
        qT = qT_stack.enter_context(nc.sbuf_tensor("qT", [128, ND * T], BF16))
        b = S.snap()
        TH = T // 2
        NHB = max(1, TH // 512)
        HBW = min(TH, 512)
        vpos_tern = {}
        vpos = {}
        apos_tern = {}
        apos_sig = {}
        qpos = {}
        _v = b["sV"] + 1
        vpos_tern[0] = (_v + 1, _v + 2); _v += 2
        _a = b["sA"]
        apos_tern[0] = (_a + 1, _a + 2, _a + 3, _a + 4); _a += 4
        _q = b["sQ"] + 1
        qpos[("wg", 0)] = _q + 1
        qpos[("wu", 0)] = _q + 2
        _q += 2
        for oc in range(NO):
            if oc + 1 < NO:
                qpos[("wg", oc + 1)] = _q + 1
                qpos[("wu", oc + 1)] = _q + 2
                _q += 2
            qpos[("h", oc)] = _q + 1
            _q += 1
            if oc + 1 < NO:
                apos_tern[oc + 1] = (_a + 1, _a + 2, _a + 3, _a + 4); _a += 4
            apos_sig[(oc, 0)] = _a + 1
            apos_sig[(oc, 1)] = _a + 2
            _a += 2
            if oc + 1 < NO:
                vpos_tern[oc + 1] = (_v + 1, _v + 2); _v += 2
            for j, k in enumerate(("t1h0", "t2h0", "h0d", "t1h1", "t2h1", "h1d")):
                vpos[(oc, k)] = _v + 1 + j
            _v += 6
        pg8 = lambda oc, k: b["sP"] + 4 * oc + k
        with contextlib.ExitStack() as ph:
            wgs = [ph.enter_context(nc.sbuf_tensor(f"wgs{i}", [128, D], F32)) for i in range(2)]
            wus = [ph.enter_context(nc.sbuf_tensor(f"wus{i}", [128, D], F32)) for i in range(2)]
            teg = [ph.enter_context(nc.sbuf_tensor(f"teg{i}", [128, D], BF16)) for i in range(2)]
            teu = [ph.enter_context(nc.sbuf_tensor(f"teu{i}", [128, D], BF16)) for i in range(2)]
            zg0 = ph.enter_context(nc.sbuf_tensor("zg0", [128, D], F32))
            zg1 = ph.enter_context(nc.sbuf_tensor("zg1", [128, D], F32))
            hg = ph.enter_context(nc.sbuf_tensor("hg", [128, TH], F32))
            hs = ph.enter_context(nc.sbuf_tensor("hs", [128, TH], F32))
            hu = ph.enter_context(nc.sbuf_tensor("hu", [128, TH], F32))
            hsl = [ph.enter_context(nc.sbuf_tensor(f"hsl{i}", [128, T], F32)) for i in range(2)]
            psG = [ph.enter_context(nc.psum_tensor(f"psG{i}", [128, TH], F32)) for i in range(2)]
            psU = [ph.enter_context(nc.psum_tensor(f"psU{i}", [128, TH], F32)) for i in range(2)]

            def tern_acts(scalar, oc):
                S.wait(scalar, "sc", "sQ", 16 * qpos[("wg", oc)])
                if oc >= 1:
                    S.wait(scalar, "sc", "sV", vpos_tern[oc - 1][1])
                S.inc("sA", scalar.activation(zg0[:], wgs[oc % 2][:], AF.Sign,
                                              scale=rw_b[:, 0:1]))
                S.inc("sA", scalar.activation(zg1[:], wgs[oc % 2][:], AF.Square,
                                              scale=rw_b[:, 0:1]))
                S.wait(scalar, "sc", "sQ", 16 * qpos[("wu", oc)])
                S.wait(scalar, "sc", "sV", vpos_tern[oc][0])
                S.inc("sA", scalar.activation(zg0[:], wus[oc % 2][:], AF.Sign,
                                              scale=rw_b[:, 1:2]))
                S.inc("sA", scalar.activation(zg1[:], wus[oc % 2][:], AF.Square,
                                              scale=rw_b[:, 1:2]))

            def tern_dve(vector, oc):
                S.wait(vector, "ve", "sA", apos_tern[oc][1])
                if oc >= 2:
                    S.wait(vector, "ve", "sP", pg8(oc - 2, 4))
                vector.tensor_scalar(zg1[:], zg1[:], 0.25, None, ALU.is_ge)
                S.inc("sV", vector.tensor_tensor(teg[oc % 2][:], zg0[:], zg1[:], ALU.mult))
                S.wait(vector, "ve", "sA", apos_tern[oc][3])
                vector.tensor_scalar(zg1[:], zg1[:], 0.25, None, ALU.is_ge)
                S.inc("sV", vector.tensor_tensor(teu[oc % 2][:], zg0[:], zg1[:], ALU.mult))

            with nc.Block("ph8") as blk:
                @blk.sync
                def _(sync):
                    S.inc("sQ", sync.dma_start(
                        out=qT[:].rearrange("p (d t) -> p d t", d=ND),
                        in_=qpark_h.rearrange("d p t -> p d t")))
                    S.inc("sQ", sync.dma_start(out=wgs[0][:], in_=wg_h[0]))
                    S.inc("sQ", sync.dma_start(out=wus[0][:], in_=wu_h[0]))
                    for oc in range(NO):
                        if oc + 1 < NO:
                            S.wait(sync, "sy", "sA", apos_tern[oc - 1][1] if oc >= 1 else 0)
                            S.inc("sQ", sync.dma_start(out=wgs[(oc + 1) % 2][:],
                                                       in_=wg_h[oc + 1]))
                            S.wait(sync, "sy", "sA", apos_tern[oc - 1][3] if oc >= 1 else 0)
                            S.inc("sQ", sync.dma_start(out=wus[(oc + 1) % 2][:],
                                                       in_=wu_h[oc + 1]))
                        S.wait(sync, "sy", "sV", vpos[(oc, "h1d")])
                        S.inc("sQ", sync.dma_start(out=hscr_h[oc], in_=hsl[oc % 2][:]))

                @blk.tensor
                def _(tensor):
                    S.wait(tensor, "te", "sQ", 16 * (b["sQ"] + 1))
                    for oc in range(NO):
                        for (m, hf) in ((0, 0), (1, 0), (0, 1), (1, 1)):
                            tern = teg[oc % 2] if m == 0 else teu[oc % 2]
                            ps = (psG if m == 0 else psU)[hf]
                            S.wait(tensor, "te", "sV", vpos_tern[oc][m])
                            if oc >= 1:
                                rel = {(0, 0): "t1h0", (1, 0): "t2h0",
                                       (0, 1): "t1h1", (1, 1): "t2h1"}[(m, hf)]
                                S.wait(tensor, "te", "sV", vpos[(oc - 1, rel)])
                            last = None
                            for dc in range(ND):
                                for hb in range(NHB):
                                    off = dc * T + hf * TH + hb * HBW
                                    last = tensor.matmul(
                                        ps[:, hb * HBW:(hb + 1) * HBW],
                                        tern[:, dc * 128:(dc + 1) * 128],
                                        qT[:, off:off + HBW],
                                        start=(dc == 0), stop=(dc == ND - 1))
                            S.inc("sP", last)

                @blk.scalar
                def _(scalar):
                    _preload_tables(scalar, nc, AF.Sigmoid, AF.Identity, AF.Square, AF.Sign)
                    tern_acts(scalar, 0)
                    for oc in range(NO):
                        if oc + 1 < NO:
                            tern_acts(scalar, oc + 1)
                        S.wait(scalar, "sc", "sV", vpos[(oc, "t1h0")])
                        S.inc("sA", scalar.activation(hs[:], hg[:], AF.Sigmoid))
                        S.wait(scalar, "sc", "sV", vpos[(oc, "t1h1")])
                        S.inc("sA", scalar.activation(hs[:], hg[:], AF.Sigmoid))

                @blk.vector
                def _(vector):
                    vector.memset(amaxfm[:], 0.0)
                    S.inc("sV", vector.memset(amaxN[:], 0.0))
                    tern_dve(vector, 0)
                    for oc in range(NO):
                        if oc + 1 < NO:
                            tern_dve(vector, oc + 1)
                        for hf in range(2):
                            S.wait(vector, "ve", "sP", pg8(oc, 2 * hf + 1))
                            S.inc("sV", vector.tensor_tensor(
                                hg[:], psG[hf][:], vgb[:, hf * TH:(hf + 1) * TH], ALU.mult))
                            S.wait(vector, "ve", "sP", pg8(oc, 2 * hf + 2))
                            S.inc("sV", vector.tensor_tensor(
                                hu[:], psU[hf][:], vub[:, hf * TH:(hf + 1) * TH], ALU.mult))
                            S.wait(vector, "ve", "sA", apos_sig[(oc, hf)])
                            if oc >= 2:
                                S.wait(vector, "ve", "sQ", 16 * qpos[("h", oc - 2)])
                            sl = slice(hf * TH, (hf + 1) * TH)
                            vector.tensor_tensor(hsl[oc % 2][:, sl], hs[:], hu[:], ALU.mult)
                            vector.tensor_max(amaxfm[:, sl], amaxfm[:, sl],
                                              hsl[oc % 2][:, sl])
                            S.inc("sV", vector.tensor_tensor(
                                amaxN[:, sl], amaxN[:, sl], hsl[oc % 2][:, sl], ALU.min))
                    vector.tensor_scalar(amaxN[:], amaxN[:], -1.0, None, ALU.mult)
                    S.inc("sV", vector.tensor_max(amaxfm[:], amaxfm[:], amaxN[:]))
        S.check(b, sQ=1 + 3 * NO, sP=4 * NO, sA=6 * NO, sV=2 + 8 * NO)
        qT_stack.close()

        b = S.snap()
        with contextlib.ExitStack() as ph:
            ps9 = [ph.enter_context(nc.psum_tensor(f"ps9{i}", [128, 128], F32)) for i in range(2)]
            ps9b = ph.enter_context(nc.psum_tensor("ps9b", [NT, 128], F32))
            rh_row = ph.enter_context(nc.sbuf_tensor("rh_row", [1, T], F32))
            lh9 = ph.enter_context(nc.sbuf_tensor("lh9", [128, NT], F32))
            psBC = ph.enter_context(nc.psum_tensor("psBC", [128, T], F32))
            with nc.Block("ph9") as blk:
                @blk.tensor
                def _(tensor):
                    for tg in range(NT):
                        if tg >= 2:
                            S.wait(tensor, "te", "sV", b["sV"] + tg - 1)
                        S.inc("sP", tensor.transpose(ps9[tg % 2][:],
                                                     amaxfm[:, tg * 128:(tg + 1) * 128],
                                                     eyef[:]))
                    S.wait(tensor, "te", "sA", b["sA"] + 1)
                    S.inc("sP", tensor.transpose(ps9b[:], rh_tok[:], eyef[:]))
                    S.wait(tensor, "te", "sQ", 16 * (b["sQ"] + 1))
                    last = None
                    for tb in range(NTB):
                        last = tensor.matmul(psBC[:, tb * 512:(tb + 1) * 512], ones[0:1, :],
                                             rh_row[:, tb * 512:(tb + 1) * 512],
                                             start=True, stop=True)
                    S.inc("sP", last)

                @blk.scalar
                def _(scalar):
                    _preload_tables(scalar, nc, AF.Ln, AF.Exp, AF.Identity, AF.Square,
                                    AF.Sign)
                    S.wait(scalar, "sc", "sV", b["sV"] + NT + 1)
                    scalar.activation(lh9[:], sh_sc[:], AF.Ln)
                    S.inc("sA", scalar.activation(rh_tok[:], lh9[:], AF.Exp,
                                                  scale=cst[:, 2:3]))

                @blk.vector
                def _(vector):
                    for tg in range(NT):
                        S.wait(vector, "ve", "sP", b["sP"] + tg + 1)
                        S.inc("sV", vector.reduce_max(sh_sc[:, tg:tg + 1],
                                                      ps9[tg % 2][:], axis=AX.X))
                    S.fence(vector, "ve", vector.engine_nop())
                    if True:
                        pass
                    S.fence(vector, "ve", vector.tensor_scalar(
                        sh_sc[:], sh_sc[:], 1e-5, 1.0 / 127.0, ALU.max, ALU.mult))
                    S.inc("sV", vector.tensor_scalar(vd_tok[:], sh_sc[:], sw_b[:, 2:3], None,
                                                     ALU.mult))
                    S.wait(vector, "ve", "sP", b["sP"] + NT + 1)
                    S.inc("sV", vector.tensor_copy(stat_st[0:NT, :], ps9b[:]))
                    S.wait(vector, "ve", "sP", b["sP"] + NT + 2)
                    S.inc("sV", vector.tensor_copy(rhb[:], psBC[:]))

                @blk.sync
                def _(sync):
                    S.wait(sync, "sy", "sV", b["sV"] + NT + 2)
                    S.inc("sQ", sync.dma_start(out=rh_row[:], in_=stat_st[0:NT, :]))
        S.check(b, sP=NT + 2, sV=NT + 3, sQ=1, sA=1)
        gu_stack.close()

        b = S.snap()
        with contextlib.ExitStack() as ph:
            wdt = [[ph.enter_context(nc.sbuf_tensor(f"wdt{i}_{ol}", [128, D], BF16))
                    for ol in range(NOQ)] for i in range(2)]
            hread = [ph.enter_context(nc.sbuf_tensor("hread0", [128, T], F32))] * 2
            wds = [ph.enter_context(nc.sbuf_tensor("wds0", [128, D], F32))] * 2
            hqs = [ph.enter_context(nc.sbuf_tensor("hqs0", [128, T], BF16))] * 2
            hqb = [ph.enter_context(nc.sbuf_tensor(f"hqb{i}", [128, NOQ * 128], BF16))
                   for i in range(2)]
            BD = max(D, T)
            b1 = ph.enter_context(nc.sbuf_tensor("b1d", [128, BD], BF16))
            b2 = ph.enter_context(nc.sbuf_tensor("b2d", [128, BD], F32))
            q1 = ph.enter_context(nc.sbuf_tensor("q1d", [128, T // 2], F32))
            q2 = ph.enter_context(nc.sbuf_tensor("q2d", [128, T // 2], F32))
            hq32 = ph.enter_context(nc.sbuf_tensor("hq32", [128, T // 2],
                                                   mybir.dt.int32))
            outd = [ph.enter_context(nc.sbuf_tensor(f"outd{i}", [128, D], F32))
                    for i in range(2)]
            psD = [ph.enter_context(nc.psum_tensor(f"psD{i}", [128, D], F32)) for i in range(2)]

            pairs = [(q_, tg) for q_ in range(NQ) for tg in range(NT)]
            qv = {}
            _q = b["sQ"]
            for j in range(NO):
                _q += 1; qv[("h", j)] = _q
                if j >= 1:
                    _q += 1; qv[("st", j - 1)] = _q
            _q += 1; qv[("st", NO - 1)] = _q
            vv = {}
            _v = b["sV"]
            for j in range(NO):
                _v += 1; vv[("tern", j)] = _v
                _v += 1; vv[("quant", j)] = _v
            sched = []
            _cur = {"bi": 0, "ei": 0}

            def _drain_step(jq, max_emit):
                emitted = 0
                while emitted < max_emit:
                    b_el = (_cur["bi"] < len(pairs)
                            and pairs[_cur["bi"]][0] < jq)
                    e_el = (_cur["ei"] < len(pairs)
                            and pairs[_cur["ei"]][0] < jq)
                    if b_el and _cur["bi"] < _cur["ei"] + 3:
                        sched.append(("blk", _cur["bi"]))
                        _cur["bi"] += 1
                    elif e_el:
                        sched.append(("ev", _cur["ei"]))
                        _cur["ei"] += 1
                    else:
                        break
                    emitted += 1

            sched.append(("wd", 0))
            for j in range(NO):
                _drain_step(j // NOQ - 1, 10 ** 9)
                sched.append(("act", j))
                if j + 1 < NO:
                    sched.append(("wd", j + 1))
                _drain_step(j // NOQ, 2)
            _drain_step(NQ, 10 ** 9)
            av = {}
            _a = b["sA"]
            sblk = {}
            wdpos = {}
            _s = b["sS"]
            for kind, idx in sched:
                if kind == "act":
                    _a += 1; av[("sg", idx)] = _a
                    _a += 1; av[("sq", idx)] = _a
                elif kind == "ev":
                    _a += 1; av[("ev",) + pairs[idx]] = _a
                elif kind == "wd":
                    _s += 1; wdpos[idx] = _s
                else:
                    _s += 1; sblk[idx] = _s
            pv = {}
            _p = b["sP"]
            for bi, pr in enumerate(pairs):
                _p += 1; pv[pr] = _p
            gacc = {}
            _g = b["sG"]
            for bi in range(len(pairs)):
                _g += 1; gacc[bi] = _g

            with nc.Block("ph10") as blk:
                @blk.sync
                def _(sync):
                    for j in range(NO):
                        if j >= 1:
                            S.wait(sync, "sy", "sV", vv[("quant", j - 1)])
                        S.inc("sQ", sync.dma_start(out=hread[j % 2][:], in_=hscr_h[j]))
                        if j >= 1:
                            S.inc("sQ", sync.dma_start(out=hq_hbm[j - 1],
                                                       in_=hqs[(j - 1) % 2][:]))
                    S.wait(sync, "sy", "sV", vv[("quant", NO - 1)])
                    S.inc("sQ", sync.dma_start(out=hq_hbm[NO - 1],
                                               in_=hqs[(NO - 1) % 2][:]))

                @blk.gpsimd
                def _(g):
                    for bi, (q2, tg2) in enumerate(pairs):
                        S.wait(g, "gp", "sA", av[("ev", q2, tg2)])
                        if q2 >= 1:
                            S.wait(g, "gp", "sG", 16 * gacc[(q2 - 1) * NT + tg2])
                        elif bi == 0:
                            S.wait(g, "gp", "sG", 16 * b["sG"])
                        S.inc("sG", g.dma_start(out=out_h[tg2 * 128:(tg2 + 1) * 128, :],
                                                in_=outd[tg2 % 2][:], accum_op=ALU.add))
                    g.wait_ge(S.sems["sG"], 16 * S.cnt["sG"])

                @blk.scalar
                def _(scalar):
                    ei = 0
                    bi = 0

                    def emit_blk():
                        nonlocal bi
                        q3, tg3 = pairs[bi]
                        S.wait(scalar, "sc", "sQ",
                               16 * qv[("st", (q3 + 1) * NOQ - 1)])
                        if bi >= 2:
                            S.wait(scalar, "sc", "sP", pv[pairs[bi - 2]])
                        src_ap = hq_hbm.rearrange("o p t -> p o t")[
                            :, q3 * NOQ:(q3 + 1) * NOQ, tg3 * 128:(tg3 + 1) * 128]
                        S.inc("sS", scalar.dma_start(
                            out=hqb[bi % 2][:].rearrange("p (o t) -> p o t", o=NOQ),
                            in_=src_ap))
                        bi += 1

                    def emit_ev():
                        nonlocal ei
                        q2, tg2 = pairs[ei]
                        S.wait(scalar, "sc", "sP", pv[(q2, tg2)])
                        gi2 = q2 * NT + tg2
                        if gi2 >= 2:
                            S.wait(scalar, "sc", "sG", 16 * gacc[gi2 - 2])
                        S.inc("sA", scalar.activation(
                            outd[tg2 % 2][:], psD[tg2 % 2][:], AF.Identity,
                            scale=vd_tok[:, tg2:tg2 + 1]))
                        ei += 1

                    for kind, idx in sched:
                        if kind == "wd":
                            jj = idx
                            if jj >= 1:
                                S.wait(scalar, "sc", "sA", av[("sq", jj - 1)])
                            S.inc("sS", scalar.dma_start(out=wds[jj % 2][:],
                                                         in_=wd_h[jj]))
                        elif kind == "act":
                            j = idx
                            S.wait(scalar, "sc", "sS", 16 * wdpos[j])
                            if j >= 1:
                                S.wait(scalar, "sc", "sV",
                                       vv[("tern", j - 1)])
                            S.inc("sA", scalar.activation(b1[:, 0:D], wds[j % 2][:],
                                                          AF.Sign,
                                                          scale=rw_b[:, 2:3]))
                            S.inc("sA", scalar.activation(b2[:, 0:D], wds[j % 2][:],
                                                          AF.Square,
                                                          scale=rw_b[:, 2:3]))
                        elif kind == "blk":
                            assert idx == bi
                            emit_blk()
                        else:
                            assert idx == ei
                            emit_ev()

                @blk.vector
                def _(vector):
                    for j in range(NO):
                        q_ = j // NOQ
                        ol = j % NOQ
                        S.wait(vector, "ve", "sA", av[("sq", j)])
                        if q_ >= 2:
                            S.wait(vector, "ve", "sP", pv[(q_ - 2, NT - 1)])
                        vector.tensor_scalar(b2[:, 0:D], b2[:, 0:D], 0.25, None, ALU.is_ge)
                        S.inc("sV", vector.tensor_tensor(wdt[q_ % 2][ol][:], b1[:, 0:D],
                                                         b2[:, 0:D], ALU.mult))
                        S.wait(vector, "ve", "sQ", 16 * qv[("h", j)])
                        if j >= 1:
                            S.wait(vector, "ve", "sQ",
                                   16 * qv[("st", j - 1)])
                        TH2 = T // 2
                        last_q = None
                        for hh in range(2):
                            sl = slice(hh * TH2, (hh + 1) * TH2)
                            vector.tensor_tensor(q1[:], hread[j % 2][:, sl],
                                                 rhb[:, sl], ALU.mult)
                            vector.tensor_scalar(q1[:], q1[:], 256.0, None, ALU.add)
                            vector.tensor_copy(hq32[:], q1[:])
                            vector.tensor_copy(q2[:], hq32[:])
                            vector.tensor_tensor(q1[:], q1[:], q2[:], ALU.subtract)
                            vector.tensor_scalar(q1[:], q1[:], 0.5, None, ALU.is_ge)
                            last_q = vector.scalar_tensor_tensor(
                                hqs[j % 2][:, sl], q1[:], -256.0, q2[:],
                                ALU.add, ALU.add)
                        S.inc("sV", last_q)

                @blk.tensor
                def _(tensor):
                    for bi, (q_, tg) in enumerate(pairs):
                        S.wait(tensor, "te", "sV", vv[("tern", (q_ + 1) * NOQ - 1)])
                        S.wait(tensor, "te", "sS", 16 * sblk[bi])
                        gi2 = q_ * NT + tg
                        if gi2 >= 2:
                            S.wait(tensor, "te", "sA", av[("ev", *pairs[gi2 - 2])])
                        last = None
                        for ol in range(NOQ):
                            for db in range(NDB):
                                last = tensor.matmul(
                                    psD[tg % 2][:, db * 512:(db + 1) * 512],
                                    hqb[bi % 2][:, ol * 128:(ol + 1) * 128],
                                    wdt[q_ % 2][ol][:, db * 512:(db + 1) * 512],
                                    start=(ol == 0), stop=(ol == NOQ - 1))
                        S.inc("sP", last)
        S.check(b, sQ=2 * NO, sA=2 * NO + NQ * NT, sV=2 * NO, sP=NQ * NT,
                sG=NQ * NT, sS=NQ * NT + NO)
        rhb_stack.close()

    return nc


def _tile_slabs(w_t, nslab):
    rows, K = w_t.shape
    assert rows == nslab * 128 and K % 128 == 0
    nd = K // 128
    a = w_t.reshape(nslab, 128, nd, 128)
    return np.ascontiguousarray(a.transpose(0, 3, 2, 1).reshape(nslab, 128, K))


def _rope_tables(T, rot):
    try:
        import jax
        import jax.numpy as jnp
        with jax.default_device(jax.devices("cpu")[0]):
            inv = 1.0 / 10000.0 ** (jnp.arange(0, rot, 2, dtype=jnp.float32) / rot)
            freqs = jnp.arange(T, dtype=jnp.float32)[:, None] * inv[None, :]
            c = np.asarray(jnp.cos(freqs), dtype=np.float32)
            s = np.asarray(jnp.sin(freqs), dtype=np.float32)
        return c, s
    except Exception:
        a = np.arange(0, rot, 2, dtype=np.float32) / np.float32(rot)
        inv = (np.float32(1.0) / (np.float32(10000.0) ** a)).astype(np.float32)
        freqs = np.arange(T, dtype=np.float32)[:, None] * inv[None, :]
        return (np.cos(freqs).astype(np.float32), np.sin(freqs).astype(np.float32))


def _perm(D):
    rot = D // 2
    return np.concatenate([np.arange(0, rot, 2), np.arange(1, rot, 2),
                           np.arange(rot, D)])


def host_prep(c, inputs):
    D, T, DFF = c.D, c.T, c.DFF
    ND, NPC, NO = c.ND, c.NPC, c.NO
    W_in = np.asarray(inputs["L0_attn_W_in"], dtype=np.float32)
    b_in = np.asarray(inputs["L0_attn_b_in"], dtype=np.float32)
    W_out = np.asarray(inputs["L0_attn_W_out"], dtype=np.float32)
    b_out = np.asarray(inputs["L0_attn_b_out"], dtype=np.float32)
    w_gate = np.asarray(inputs["L0_ffn_w_gate"], dtype=np.float32)
    w_up = np.asarray(inputs["L0_ffn_w_up"], dtype=np.float32)
    w_down = np.asarray(inputs["L0_ffn_w_down"], dtype=np.float32)

    perm = _perm(D)
    Wq = W_in[0:D][perm]
    Wk = W_in[D:2 * D][perm]
    Wv = W_in[2 * D:3 * D][perm]
    bq = b_in[0:D][perm]
    bk = b_in[D:2 * D][perm]
    bv = b_in[2 * D:3 * D][perm]

    def kvq_rows(Wm):
        chunks = Wm.reshape(ND, 128, D)
        order = []
        for i in range(NPC):
            order.append(i)
            order.append(NPC + i)
        order += list(range(2 * NPC, ND))
        return chunks[order].reshape(D, D)

    def kvq_bias(bm):
        chunks = bm.reshape(ND, 128)
        order = []
        for i in range(NPC):
            order.append(i)
            order.append(NPC + i)
        order += list(range(2 * NPC, ND))
        return chunks[order].reshape(D)

    rows = np.concatenate([kvq_rows(Wk), Wv, kvq_rows(Wq)], axis=0)
    bias = np.concatenate([kvq_bias(bk), bv, kvq_bias(bq)])

    w_qkv_t = _tile_slabs(rows, 3 * ND)
    bqkv_t = np.ascontiguousarray(bias.reshape(3 * ND, 128).T)

    w_out_t = np.ascontiguousarray(W_out[:, perm].T)
    b_out_row = np.ascontiguousarray(b_out.reshape(1, D))

    w_gate_t = _tile_slabs(w_gate, NO)
    w_up_t = _tile_slabs(w_up, NO)
    w_down_t = np.ascontiguousarray(w_down.T.reshape(NO, 128, D))

    rot = D // 2
    cos32, sin32 = _rope_tables(T, rot)
    cos_t = np.ascontiguousarray(cos32.T.reshape(NPC, 128, T))
    sin_t = np.ascontiguousarray(sin32.T.reshape(NPC, 128, T))

    eyef = np.eye(128, dtype=np.float32)

    shared = dict(w_qkv_t=w_qkv_t, bqkv_t=bqkv_t, w_out_t=w_out_t,
                  b_out_row=b_out_row, w_gate_t=w_gate_t, w_up_t=w_up_t,
                  w_down_t=w_down_t, cos_t=cos_t, sin_t=sin_t, eyef=eyef)
    return shared


def kernel(L0_attn_X, L0_attn_W_in, L0_attn_b_in, L0_attn_W_out, L0_attn_b_out,
           L0_ffn_w_gate, L0_ffn_w_up, L0_ffn_w_down):
    X = np.ascontiguousarray(np.asarray(L0_attn_X, dtype=np.float32))
    B, T, D = X.shape
    DFF = L0_ffn_w_gate.shape[0]
    c = Cfg(T=T, D=D, DFF=DFF)
    shared = host_prep(c, dict(
        L0_attn_W_in=L0_attn_W_in, L0_attn_b_in=L0_attn_b_in,
        L0_attn_W_out=L0_attn_W_out, L0_attn_b_out=L0_attn_b_out,
        L0_ffn_w_gate=L0_ffn_w_gate, L0_ffn_w_up=L0_ffn_w_up,
        L0_ffn_w_down=L0_ffn_w_down))
    nc = build_program(c)
    in_maps = [dict(shared, x=np.ascontiguousarray(X[b_])) for b_ in range(B)]
    res = run_bass_kernel_spmd(nc, in_maps, core_ids=list(range(N_CORES)))
    return np.stack([np.asarray(res.results[b_]["out"]) for b_ in range(B)], axis=0)


# version 22
# speedup vs baseline: 1.0330x; 1.0260x over previous
import contextlib
import numpy as np
import ml_dtypes
import concourse.bass as bass
import concourse.mybir as mybir
from concourse.bass_utils import run_bass_kernel_spmd

F32 = mybir.dt.float32
BF16 = mybir.dt.bfloat16
AF = mybir.ActivationFunctionType
ALU = mybir.AluOpType
AX = mybir.AxisListType

N_CORES = 8


class Cfg:
    def __init__(self, T=2048, D=2048, DFF=8192):
        assert T % 512 == 0 and D % 512 == 0 and DFF % 512 == 0
        self.T, self.D, self.DFF = T, D, DFF
        self.NT = T // 128
        self.ND = D // 128
        self.NE = 3 * self.ND
        self.NO = DFF // 128
        self.NPC = (D // 4) // 128
        self.NQ = 4
        assert self.NO % self.NQ == 0
        self.NOQ = self.NO // self.NQ
        self.NTB = T // 512
        self.NDB = D // 512


class St:
    def __init__(self, nc, stack):
        self.nc = nc
        self.sems = {}
        self.cnt = {}
        self.waited = {}
        for name in ("sQ", "sG", "sP", "sV", "sA", "sW", "sS", "sU"):
            self.sems[name] = stack.enter_context(nc.semaphore(name))
            self.cnt[name] = 0

    def inc(self, name, inst):
        if name in ("sQ", "sG", "sS"):
            inst.then_inc(self.sems[name], 16)
        else:
            inst.then_inc(self.sems[name], 1)
        self.cnt[name] += 1

    def snap(self):
        return dict(self.cnt)

    def wait(self, eng, engname, name, val):
        if val <= 0:
            return
        key = (engname, name)
        if self.waited.get(key, 0) >= val:
            return
        eng.wait_ge(self.sems[name], val)
        self.waited[key] = val

    def fence(self, eng, engname, inst):
        self.inc("sW", inst)
        self.wait(eng, engname, "sW", self.cnt["sW"])

    def check(self, base, **expect):
        for k, v in expect.items():
            got = self.cnt[k] - base[k]
            assert got == v, f"sem {k}: planned {v}, emitted {got}"


def _preload_tables(scalar, nc, *funcs):
    from concourse.hw_specs import get_activation_tables
    for idx, (_name, fs) in enumerate(get_activation_tables(nc.m.arch).items()):
        if all(f in fs for f in funcs):
            return scalar.add_instruction(mybir.InstLoadActFuncSet(
                name=nc.get_next_instruction_name(), act_func_set_id=idx,
                ins=[], outs=[]))
    raise ValueError(f"no activation table set contains {funcs}")


def build_program(c: Cfg):
    nc = bass.Bass("TRN2", target_bir_lowering=False, detect_race_conditions=False)
    T, D, DFF = c.T, c.D, c.DFF
    NT, ND, NE, NO, NPC = c.NT, c.ND, c.NE, c.NO, c.NPC
    NQ, NOQ, NTB, NDB = c.NQ, c.NOQ, c.NTB, c.NDB

    x_h = nc.dram_tensor("x", [T, D], F32, kind="ExternalInput").ap()
    wqkv_h = nc.dram_tensor("w_qkv_t", [NE, 128, D], F32, kind="ExternalInput").ap()
    bqkv_h = nc.dram_tensor("bqkv_t", [128, NE], F32, kind="ExternalInput").ap()
    wout_h = nc.dram_tensor("w_out_t", [D, D], F32, kind="ExternalInput").ap()
    bout_h = nc.dram_tensor("b_out_row", [1, D], F32, kind="ExternalInput").ap()
    wg_h = nc.dram_tensor("w_gate_t", [NO, 128, D], F32, kind="ExternalInput").ap()
    wu_h = nc.dram_tensor("w_up_t", [NO, 128, D], F32, kind="ExternalInput").ap()
    wd_h = nc.dram_tensor("w_down_t", [NO, 128, D], F32, kind="ExternalInput").ap()
    cos_h = nc.dram_tensor("cos_t", [NPC, 128, T], F32, kind="ExternalInput").ap()
    sin_h = nc.dram_tensor("sin_t", [NPC, 128, T], F32, kind="ExternalInput").ap()
    eyef_h = nc.dram_tensor("eyef", [128, 128], F32, kind="ExternalInput").ap()

    out_h = nc.dram_tensor("out", [T, D], F32, kind="ExternalOutput").ap()
    qpark_h = nc.dram_tensor("qpark", [ND, 128, T], BF16, kind="Internal").ap()
    ypark_h = nc.dram_tensor("ypark", [ND, 128, T], BF16, kind="Internal").ap()
    hscr_h = nc.dram_tensor("hscr", [NO, 128, T], F32, kind="Internal").ap()
    hq_hbm = nc.dram_tensor("hq_hbm", [NO, 128, T], BF16, kind="Internal").ap()

    ctx = contextlib.ExitStack()
    with ctx:
        S = St(nc, ctx)
        al = ctx.enter_context

        eyef = al(nc.sbuf_tensor("eyef_s", [128, 128], F32))
        ones = al(nc.sbuf_tensor("ones_s", [128, 128], F32))
        bq_s = al(nc.sbuf_tensor("bq_s", [128, NE], F32))
        sx_sc = al(nc.sbuf_tensor("sx_sc", [128, NT], F32))
        stat_st = al(nc.sbuf_tensor("stat_st", [NT, 128], F32))
        sw_b = al(nc.sbuf_tensor("sw_b", [128, 3], F32))
        rw_b = al(nc.sbuf_tensor("rw_b", [128, 3], F32))
        A_c = al(nc.sbuf_tensor("A_c", [128, ND], F32))
        cst = al(nc.sbuf_tensor("cst", [128, 4], F32))
        sh_sc = al(nc.sbuf_tensor("sh_sc", [128, NT], F32))
        rh_tok = al(nc.sbuf_tensor("rh_tok", [128, NT], F32))
        vd_tok = al(nc.sbuf_tensor("vd_tok", [128, NT], F32))
        accabs = al(nc.sbuf_tensor("accabs", [128, 3], F32))
        rtmp_all = al(nc.sbuf_tensor("rtmp_all", [128, 3 * c.NO], F32))

        xnT_stack = contextlib.ExitStack()
        xnT = xnT_stack.enter_context(nc.sbuf_tensor("xnT", [128, ND * T], BF16))

        with nc.Block("ph0") as blk:
            @blk.sync
            def _(sync):
                S.inc("sQ", sync.dma_start(out=eyef[:], in_=eyef_h))
                S.inc("sQ", sync.dma_start(out=bq_s[:], in_=bqkv_h))

            @blk.vector
            def _(vector):
                vector.memset(ones[:], 1.0)
                vector.memset(cst[:, 0:1], 1e-5)
                vector.memset(cst[:, 1:2], 0.5)
                vector.memset(cst[:, 2:3], -1.0)
                vector.memset(cst[:, 3:4], -0.5)

            @blk.scalar
            def _(scalar):
                _preload_tables(scalar, nc, AF.Ln, AF.Exp, AF.Identity, AF.Square,
                                AF.Sign)

        NG = ND // 4
        b = S.snap()
        qpos = {}
        _q = b["sQ"]
        _q += 1; qpos[("x", 0)] = _q
        _q += 1; qpos[("x", 1)] = _q
        for _t in range(c.NT):
            _q += 1; qpos[("acc", _t)] = _q
            if _t + 2 < c.NT:
                _q += 1; qpos[("x", _t + 2)] = _q
            for _g in range(ND // 4):
                _q += 1; qpos[("park", _t, _g)] = _q
        bG0 = b["sG"]
        NSL = 3 * NO
        SPT = (NSL + NT - 1) // NT
        pn = lambda tg, k: b["sP"] + 2 * NG * tg + k
        dve_order = [("A", 0)]
        for s_ in range(NT):
            if s_ + 1 < NT:
                dve_order.append(("A", s_ + 1))
            dve_order.append(("B", s_))
            if s_ >= 1:
                dve_order.append(("D", s_ - 1))
            dve_order.append(("C", s_))
        dve_order.append(("D", NT - 1))
        VINC = {"A": ["v1"], "B": ["v2"], "C": ["v3", "v4", "v5"], "D": ["v6"]}
        vpos = {}
        _v = b["sV"]
        for st_, t_ in dve_order:
            for nm_ in VINC[st_]:
                _v += 1
                vpos[(nm_, t_)] = _v
        apos = {}
        _a = b["sA"]
        _a += 1; apos[("a1", 0)] = _a
        for t_ in range(NT + 1):
            if t_ < NT:
                if t_ + 1 < NT:
                    _a += 1; apos[("a1", t_ + 1)] = _a
                _a += 1; apos[("a2", t_)] = _a
                _a += 1; apos[("a3", t_)] = _a
            if t_ >= 1:
                for g_ in range(NG):
                    _a += 1
                    apos[("ev_x", t_ - 1, g_)] = _a
                for g_ in range(NG):
                    _a += 1
                    apos[("ev_q", t_ - 1, g_)] = _a
        with contextlib.ExitStack() as ph:
            xbuf = [ph.enter_context(nc.sbuf_tensor(f"xb{i}", [128, D], F32)) for i in range(2)]
            accb = [ph.enter_context(nc.sbuf_tensor(f"accb{i}", [128, D], F32)) for i in range(2)]
            xnf = [ph.enter_context(nc.sbuf_tensor(f"xnf{i}", [128, D], F32))
                   for i in range(3)]
            tqb = [ph.enter_context(nc.sbuf_tensor(f"tqb{i}", [128, D], F32))
                   for i in range(2)]
            mqb = ph.enter_context(nc.sbuf_tensor("mqb", [128, D], F32))
            qbf = [ph.enter_context(nc.sbuf_tensor(f"qbf{i}", [128, D], F32))
                   for i in range(2)]
            c32i = ph.enter_context(nc.sbuf_tensor("c32i", [128, D], mybir.dt.int32))
            wpre = [ph.enter_context(nc.sbuf_tensor(f"wpre{i}", [128, D], F32))
                    for i in range(2)]
            qst = [ph.enter_context(nc.sbuf_tensor(f"qst{i}", [128, 512], BF16)) for i in range(2)]
            s1c = ph.enter_context(nc.sbuf_tensor("s1c", [128, 1], F32))
            mc = ph.enter_context(nc.sbuf_tensor("mc", [128, 1], F32))
            vcc = ph.enter_context(nc.sbuf_tensor("vcc", [128, 1], F32))
            lnc = ph.enter_context(nc.sbuf_tensor("lnc", [128, 1], F32))
            rc = ph.enter_context(nc.sbuf_tensor("rc", [128, 1], F32))
            axc = ph.enter_context(nc.sbuf_tensor("axc", [128, 1], F32))
            rxc = ph.enter_context(nc.sbuf_tensor("rxc", [128, 1], F32))
            psT = [ph.enter_context(nc.psum_tensor(f"psT{i}", [128, 512], F32)) for i in range(4)]

            with nc.Block("ph2") as blk:
                @blk.gpsimd
                def _(g):
                    srcs3 = [wg_h, wu_h, wd_h]
                    for j in range(NSL):
                        mm, sl = divmod(j, NO)
                        if j >= 2:
                            S.wait(g, "gp", "sU", j - 1)
                        S.inc("sG", g.dma_start(out=wpre[j % 2][:], in_=srcs3[mm][sl]))

                @blk.sync
                def _(sync):
                    def xload(tt):
                        S.inc("sQ", sync.dma_start(
                            out=xbuf[tt % 2][:],
                            in_=x_h[tt * 128:(tt + 1) * 128, :]))

                    xload(0)
                    if NT > 1:
                        xload(1)
                    for tg in range(NT):
                        S.wait(sync, "sy", "sV", vpos[("v4", tg)])
                        S.inc("sQ", sync.dma_start(out=out_h[tg * 128:(tg + 1) * 128, :],
                                                   in_=accb[tg % 2][:]))
                        if tg + 2 < NT:
                            xload(tg + 2)
                        for g in range(NG):
                            S.wait(sync, "sy", "sA", apos[("ev_q", tg, g)])
                            dst = qpark_h.rearrange("d p t -> p d t")[
                                :, g * 4:(g + 1) * 4, tg * 128:(tg + 1) * 128]
                            S.inc("sQ", sync.dma_start(
                                out=dst,
                                in_=qst[g % 2][:].rearrange("p (d t) -> p d t", d=4)))

                @blk.vector
                def _(vector):
                    for st_, t in dve_order:
                        if st_ == "A":
                            S.wait(vector, "ve", "sQ", 16 * qpos[("x", t)])
                            if t >= 3:
                                S.wait(vector, "ve", "sP", pn(t - 3, NG))
                            xb = xbuf[t % 2]
                            S.fence(vector, "ve", vector.reduce_sum(s1c[:], xb[:],
                                                                    axis=AX.X))
                            S.fence(vector, "ve", vector.tensor_scalar(
                                mc[:], s1c[:], 1.0 / D, None, ALU.mult))
                            S.inc("sV", vector.tensor_scalar(
                                xnf[t % 3][:], xb[:], mc[:], None, ALU.subtract))
                        elif st_ == "B":
                            S.wait(vector, "ve", "sA", apos[("a1", t)])
                            S.fence(vector, "ve", vector.reduce_sum(
                                s1c[:], tqb[t % 2][:], axis=AX.X))
                            S.inc("sV", vector.tensor_scalar(vcc[:], s1c[:], 1.0 / D,
                                                             None, ALU.mult))
                        elif st_ == "C":
                            S.wait(vector, "ve", "sA", apos[("a2", t)])
                            S.inc("sV", vector.tensor_scalar(
                                xnf[t % 3][:], xnf[t % 3][:], rc[:], None, ALU.mult))
                            if t >= 2:
                                S.wait(vector, "ve", "sQ",
                                       16 * qpos[("acc", t - 2)])
                            S.inc("sV", vector.tensor_tensor(
                                accb[t % 2][:], xbuf[t % 2][:], xnf[t % 3][:], ALU.add))
                            S.fence(vector, "ve", vector.reduce_max(
                                axc[:], xnf[t % 3][:], axis=AX.X,
                                apply_absolute_value=True))
                            S.inc("sV", vector.tensor_scalar(
                                sx_sc[:, t:t + 1], axc[:], 1e-5, 1.0 / 127.0,
                                ALU.max, ALU.mult))
                        else:
                            S.wait(vector, "ve", "sA", apos[("a3", t)])
                            if t >= 2:
                                S.wait(vector, "ve", "sP", pn(t - 1, 0))
                            zz = tqb[t % 2]
                            vector.tensor_scalar(zz[:], xnf[t % 3][:], rxc[:], 256.0,
                                                 ALU.mult, ALU.add)
                            vector.tensor_copy(c32i[:], zz[:])
                            vector.tensor_copy(mqb[:], c32i[:])
                            vector.tensor_tensor(zz[:], zz[:], mqb[:], ALU.subtract)
                            vector.tensor_scalar(zz[:], zz[:], 0.5, None, ALU.is_ge)
                            S.inc("sV", vector.scalar_tensor_tensor(
                                qbf[t % 2][:], zz[:], -256.0, mqb[:], ALU.add,
                                ALU.add))

                @blk.scalar
                def _(scalar):
                    def emit_evacs(te):
                        for g in range(NG):
                            S.wait(scalar, "sc", "sP", pn(te, g + 1))
                            dst = xnT[:].rearrange("p (d t) -> p d t", d=ND)[
                                :, g * 4:(g + 1) * 4, te * 128:(te + 1) * 128]
                            gs = 2 * NG * te + g
                            S.inc("sA", scalar.activation(
                                dst,
                                psT[gs % 4][:].rearrange("p (d t) -> p d t", d=4),
                                AF.Identity))
                        for g in range(NG):
                            S.wait(scalar, "sc", "sP", pn(te, NG + g + 1))
                            kq = te * NG + g
                            if kq >= 2:
                                tgp, gp = divmod(kq - 2, NG)
                                S.wait(scalar, "sc", "sQ",
                                       16 * qpos[("park", tgp, gp)])
                            gs = 2 * NG * te + NG + g
                            S.inc("sA", scalar.activation(
                                qst[g % 2][:], psT[gs % 4][:], AF.Identity))

                    def emit_reds(tb):
                        rj = tb * SPT
                        rend = min((tb + 1) * SPT, NSL)
                        while rj < rend:
                            S.wait(scalar, "sc", "sG", 16 * (bG0 + rj + 1))
                            S.inc("sU", scalar.activation(
                                wpre[rj % 2][:], wpre[rj % 2][:], AF.Abs,
                                accum_out=rtmp_all[:, rj:rj + 1]))
                            rj += 1

                    def emit_a1(ta):
                        S.wait(scalar, "sc", "sV", vpos[("v1", ta)])
                        if ta >= 2:
                            S.wait(scalar, "sc", "sV", vpos[("v6", ta - 2)])
                        S.inc("sA", scalar.activation(tqb[ta % 2][:], xnf[ta % 3][:],
                                                      AF.Square))

                    emit_a1(0)
                    for tg in range(NT):
                        if tg + 1 < NT:
                            emit_a1(tg + 1)
                        S.wait(scalar, "sc", "sV", vpos[("v2", tg)])
                        scalar.activation(lnc[:], vcc[:], AF.Ln, bias=cst[:, 0:1])
                        S.inc("sA", scalar.activation(rc[:], lnc[:], AF.Exp,
                                                      scale=cst[:, 3:4]))
                        S.wait(scalar, "sc", "sV", vpos[("v5", tg)])
                        scalar.activation(lnc[:], sx_sc[:, tg:tg + 1], AF.Ln)
                        S.inc("sA", scalar.activation(rxc[:], lnc[:], AF.Exp,
                                                      scale=cst[:, 2:3]))
                        if tg >= 1:
                            emit_evacs(tg - 1)
                        emit_reds(tg)
                    emit_evacs(NT - 1)

                @blk.tensor
                def _(tensor):
                    def guard(gs):
                        if gs >= 4:
                            tp_, r = divmod(gs - 4, 2 * NG)
                            key = ("ev_x", tp_, r) if r < NG else ("ev_q", tp_, r - NG)
                            S.wait(tensor, "te", "sA", apos[key])

                    for tg in range(NT):
                        S.wait(tensor, "te", "sV", vpos[("v3", tg)])
                        for g in range(NG):
                            gs = 2 * NG * tg + g
                            guard(gs)
                            last = None
                            for k in range(4):
                                dc = g * 4 + k
                                last = tensor.transpose(
                                    psT[gs % 4][:, k * 128:(k + 1) * 128],
                                    xnf[tg % 3][:, dc * 128:(dc + 1) * 128], eyef[:])
                            S.inc("sP", last)
                        S.wait(tensor, "te", "sV", vpos[("v6", tg)])
                        for g in range(NG):
                            gs = 2 * NG * tg + NG + g
                            guard(gs)
                            last = None
                            for k in range(4):
                                dc = g * 4 + k
                                last = tensor.transpose(
                                    psT[gs % 4][:, k * 128:(k + 1) * 128],
                                    qbf[tg % 2][:, dc * 128:(dc + 1) * 128], eyef[:])
                            S.inc("sP", last)
        S.check(b, sQ=(2 + NG) * NT, sV=6 * NT, sA=(3 + 2 * NG) * NT,
                sP=2 * NG * NT, sG=NSL, sW=4 * NT, sU=NSL)

        b = S.snap()
        with contextlib.ExitStack() as ph:
            ps_s = ph.enter_context(nc.psum_tensor("ps_s", [NT, 128], F32))
            with nc.Block("ph2b") as blk:
                @blk.tensor
                def _(tensor):
                    S.inc("sP", tensor.transpose(ps_s[:], sx_sc[:], eyef[:]))

                @blk.vector
                def _(vector):
                    S.wait(vector, "ve", "sP", b["sP"] + 1)
                    S.inc("sV", vector.tensor_copy(stat_st[0:NT, :], ps_s[:]))

        S.check(b, sP=1, sV=1)
        wm_out_base = S.cnt["sQ"]
        wm_statst = S.cnt["sV"]

        b = S.snap()
        with contextlib.ExitStack() as ph:
            sums_b = ph.enter_context(nc.sbuf_tensor("sums_b", [128, 3], F32))
            ps3 = ph.enter_context(nc.psum_tensor("ps3", [128, 3], F32))
            with nc.Block("ph3") as blk:
                @blk.tensor
                def _(tensor):
                    S.wait(tensor, "te", "sV", b["sV"] + 1)
                    S.inc("sP", tensor.matmul(ps3[:], ones[:], accabs[:], start=True,
                                              stop=True))

                @blk.vector
                def _(vector):
                    for mm in range(3):
                        r = vector.reduce_sum(accabs[:, mm:mm + 1],
                                              rtmp_all[:, mm * NO:(mm + 1) * NO],
                                              axis=AX.X)
                    S.inc("sV", r)
                    S.wait(vector, "ve", "sP", b["sP"] + 1)
                    S.fence(vector, "ve", vector.tensor_copy(sums_b[:], ps3[:]))
                    S.inc("sV", vector.tensor_scalar(sw_b[:], sums_b[:],
                                                     1.0 / float(DFF * D), 1e-5,
                                                     ALU.mult, ALU.add))

                @blk.scalar
                def _(scalar):
                    S.wait(scalar, "sc", "sV", b["sV"] + 2)
                    scalar.activation(sums_b[:], sw_b[:], AF.Ln)
                    S.inc("sA", scalar.activation(rw_b[:], sums_b[:], AF.Exp,
                                                  scale=cst[:, 2:3]))
        S.check(b, sV=2, sP=1, sA=1)

        def rot_order():
            order = []
            for i in range(NPC):
                order.append(("ev", i))
                order.append(("od", i))
            for j in range(ND - 2 * NPC):
                order.append(("pa", j))
            return order

        def emit_qkv_mms(tensor, ps, wslab):
            last = None
            for dc in range(ND):
                for tb in range(NTB):
                    last = tensor.matmul(
                        ps[:, tb * 512:(tb + 1) * 512],
                        wslab[:, dc * 128:(dc + 1) * 128],
                        xnT[:, dc * T + tb * 512: dc * T + tb * 512 + 512],
                        start=(dc == 0), stop=(dc == ND - 1))
            return last

        KK_stack = contextlib.ExitStack()
        KK = KK_stack.enter_context(nc.sbuf_tensor("KK", [128, ND * T], BF16))
        rbb_stack = contextlib.ExitStack()
        rbb = rbb_stack.enter_context(nc.sbuf_tensor("rbb", [128, T], F32))

        b = S.snap()
        with contextlib.ExitStack() as ph:
            wsl = [ph.enter_context(nc.sbuf_tensor(f"wsl4_{i}", [128, D], BF16)) for i in range(2)]
            kev = ph.enter_context(nc.sbuf_tensor("kev", [128, T], F32))
            kod = ph.enter_context(nc.sbuf_tensor("kod", [128, T], F32))
            cosb = ph.enter_context(nc.sbuf_tensor("cosb", [128, T], F32))
            sinb = ph.enter_context(nc.sbuf_tensor("sinb", [128, T], F32))
            sqtmp = ph.enter_context(nc.sbuf_tensor("sqtmp", [128, T], F32))
            e1 = ph.enter_context(nc.sbuf_tensor("e1", [128, T], F32))
            ksqa = ph.enter_context(nc.sbuf_tensor("ksqa", [128, T], F32))
            psK = [ph.enter_context(nc.psum_tensor(f"psK{i}", [128, T], F32)) for i in range(2)]
            order = rot_order()
            APC = {"ev": 1, "od": 1, "pa": 2}
            VPC = {"ev": 0, "od": 2, "pa": 1}
            acum = [0]
            vcum = [0]
            for kind, _ in order:
                acum.append(acum[-1] + APC[kind])
                vcum.append(vcum[-1] + VPC[kind])

            with nc.Block("ph4") as blk:
                @blk.gpsimd
                def _(g):
                    for ci in range(ND):
                        S.wait(g, "gp", "sP", (b["sP"] + ci - 1) if ci >= 2 else 0)
                        S.inc("sG", g.dma_start(out=wsl[ci % 2][:], in_=wqkv_h[ci]))

                @blk.sync
                def _(sync):
                    for i in range(NPC):
                        if i >= 1:
                            S.wait(sync, "sy", "sV", b["sV"] + 1 + vcum[2 * i])
                        S.inc("sQ", sync.dma_start(out=cosb[:], in_=cos_h[i]))
                        S.inc("sQ", sync.dma_start(out=sinb[:], in_=sin_h[i]))

                @blk.tensor
                def _(tensor):
                    for ci in range(ND):
                        S.wait(tensor, "te", "sG", 16 * (b["sG"] + ci + 1))
                        if ci >= 2:
                            S.wait(tensor, "te", "sA", b["sA"] + acum[ci - 1])
                        S.inc("sP", emit_qkv_mms(tensor, psK[ci % 2], wsl[ci % 2]))
                    S.wait(tensor, "te", "sV", b["sV"] + 1 + vcum[ND])
                    S.wait(tensor, "te", "sA", b["sA"] + acum[ND])
                    last = None
                    for tb in range(NTB):
                        last = tensor.matmul(psK[0][:, tb * 512:(tb + 1) * 512], ones[:],
                                             ksqa[:, tb * 512:(tb + 1) * 512],
                                             start=True, stop=True)
                    S.inc("sP", last)

                @blk.scalar
                def _(scalar):
                    for ci, (kind, i) in enumerate(order):
                        S.wait(scalar, "sc", "sP", b["sP"] + ci + 1)
                        bias = bq_s[:, ci:ci + 1]
                        S.wait(scalar, "sc", "sV", b["sV"] + 1 + vcum[ci])
                        if kind == "ev":
                            S.inc("sA", scalar.activation(kev[:], psK[ci % 2][:], AF.Identity,
                                                          bias=bias))
                        elif kind == "od":
                            S.inc("sA", scalar.activation(kod[:], psK[ci % 2][:], AF.Identity,
                                                          bias=bias))
                        else:
                            slot = 2 * NPC + i
                            S.inc("sA", scalar.activation(KK[:, slot * T:(slot + 1) * T],
                                                          psK[ci % 2][:], AF.Identity,
                                                          bias=bias))
                            S.inc("sA", scalar.activation(sqtmp[:], psK[ci % 2][:], AF.Square,
                                                          bias=bias))
                    S.wait(scalar, "sc", "sP", b["sP"] + ND + 1)
                    S.wait(scalar, "sc", "sV", b["sV"] + 1 + vcum[ND])
                    scalar.activation(e1[:], psK[0][:], AF.Ln)
                    S.inc("sA", scalar.activation(rbb[:], e1[:], AF.Exp,
                                                  scale=cst[:, 3:4]))

                @blk.vector
                def _(vector):
                    S.inc("sV", vector.memset(ksqa[:], 0.0))
                    for ci, (kind, i) in enumerate(order):
                        if kind == "od":
                            S.wait(vector, "ve", "sA", b["sA"] + acum[ci + 1])
                            S.wait(vector, "ve", "sQ", 16 * (b["sQ"] + 2 * (i + 1)))
                            ev_s, od_s = i, NPC + i
                            vector.tensor_tensor(e1[:], kev[:], cosb[:], ALU.mult)
                            vector.tensor_tensor(sqtmp[:], kod[:], sinb[:], ALU.mult)
                            vector.tensor_tensor(e1[:], e1[:], sqtmp[:], ALU.subtract)
                            vector.tensor_tensor(sqtmp[:], e1[:], e1[:], ALU.mult)
                            vector.tensor_tensor(ksqa[:], ksqa[:], sqtmp[:], ALU.add)
                            S.inc("sV", vector.tensor_copy(KK[:, ev_s * T:(ev_s + 1) * T],
                                                           e1[:]))
                            vector.tensor_tensor(e1[:], kod[:], cosb[:], ALU.mult)
                            vector.tensor_tensor(sqtmp[:], kev[:], sinb[:], ALU.mult)
                            vector.tensor_tensor(e1[:], e1[:], sqtmp[:], ALU.add)
                            vector.tensor_tensor(sqtmp[:], e1[:], e1[:], ALU.mult)
                            vector.tensor_tensor(ksqa[:], ksqa[:], sqtmp[:], ALU.add)
                            S.inc("sV", vector.tensor_copy(KK[:, od_s * T:(od_s + 1) * T],
                                                           e1[:]))
                        elif kind == "pa":
                            S.wait(vector, "ve", "sA", b["sA"] + acum[ci + 1])
                            S.inc("sV", vector.tensor_tensor(ksqa[:], ksqa[:], sqtmp[:],
                                                             ALU.add))
                    S.wait(vector, "ve", "sA", b["sA"] + acum[ND] + 1)
                    S.inc("sV", vector.tensor_scalar(rbb[:], rbb[:], 1e12, None, ALU.min))
        S.check(b, sG=ND, sP=ND + 1, sA=acum[ND] + 1, sV=1 + vcum[ND] + 1, sQ=2 * NPC)

        b = S.snap()
        with contextlib.ExitStack() as ph:
            wsl = [ph.enter_context(nc.sbuf_tensor(f"wsl5_{i}", [128, D], BF16)) for i in range(2)]
            vch = [ph.enter_context(nc.sbuf_tensor(f"vch{i}", [128, T], F32)) for i in range(2)]
            t1b = ph.enter_context(nc.sbuf_tensor("t1b", [128, T], F32))
            psV = [ph.enter_context(nc.psum_tensor(f"psV{i}", [128, T], F32)) for i in range(2)]
            with nc.Block("ph5") as blk:
                @blk.gpsimd
                def _(g):
                    for ci in range(ND):
                        S.wait(g, "gp", "sP", (b["sP"] + ci - 1) if ci >= 2 else 0)
                        S.inc("sG", g.dma_start(out=wsl[ci % 2][:], in_=wqkv_h[ND + ci]))

                @blk.tensor
                def _(tensor):
                    for ci in range(ND):
                        S.wait(tensor, "te", "sG", 16 * (b["sG"] + ci + 1))
                        if ci >= 2:
                            S.wait(tensor, "te", "sA", b["sA"] + ci - 1)
                        S.inc("sP", emit_qkv_mms(tensor, psV[ci % 2], wsl[ci % 2]))

                @blk.scalar
                def _(scalar):
                    for ci in range(ND):
                        S.wait(scalar, "sc", "sP", b["sP"] + ci + 1)
                        if ci >= 2:
                            S.wait(scalar, "sc", "sV", b["sV"] + 2 * (ci - 1) - 1)
                        S.inc("sA", scalar.activation(vch[ci % 2][:], psV[ci % 2][:], AF.Identity,
                                                      bias=bq_s[:, ND + ci:ND + ci + 1]))

                @blk.vector
                def _(vector):
                    for ci in range(ND):
                        S.wait(vector, "ve", "sA", b["sA"] + ci + 1)
                        S.inc("sV", vector.tensor_tensor(t1b[:], vch[ci % 2][:],
                                                         KK[:, ci * T:(ci + 1) * T], ALU.mult))
                        vector.tensor_tensor(t1b[:], t1b[:], rbb[:], ALU.mult)
                        S.inc("sV", vector.reduce_sum(A_c[:, ci:ci + 1], t1b[:], axis=AX.X))
        S.check(b, sG=ND, sP=ND, sA=ND, sV=2 * ND)
        rbb_stack.close()
        KK_stack.close()

        b = S.snap()
        with contextlib.ExitStack() as ph:
            wsl = [ph.enter_context(nc.sbuf_tensor(f"wsl6_{i}", [128, D], BF16)) for i in range(2)]
            kev = ph.enter_context(nc.sbuf_tensor("qev", [128, T], F32))
            kod = ph.enter_context(nc.sbuf_tensor("qod", [128, T], F32))
            cosb = ph.enter_context(nc.sbuf_tensor("cosb6", [128, T], F32))
            sinb = ph.enter_context(nc.sbuf_tensor("sinb6", [128, T], F32))
            e1 = ph.enter_context(nc.sbuf_tensor("e16", [128, T], F32))
            e2 = ph.enter_context(nc.sbuf_tensor("e26", [128, T], F32))
            ystg = [ph.enter_context(nc.sbuf_tensor(f"ystg{i}", [128, T], BF16))
                    for i in range(2)]
            psQ = [ph.enter_context(nc.psum_tensor(f"psQ{i}", [128, T], F32)) for i in range(2)]
            order = rot_order()
            VPC = {"ev": 0, "od": 2, "pa": 1}
            vcum = [0]
            stage_cum = [0]
            for kind, _ in order:
                vcum.append(vcum[-1] + VPC[kind])
                stage_cum.append(stage_cum[-1] + (2 if kind == "od" else (1 if kind == "pa" else 0)))
            NSTG = stage_cum[-1]
            assert NSTG == ND
            stage_slots = []
            for kind, i in order:
                if kind == "od":
                    stage_slots += [i, NPC + i]
                elif kind == "pa":
                    stage_slots += [2 * NPC + i]

            with nc.Block("ph6") as blk:
                @blk.gpsimd
                def _(g):
                    for ci in range(ND):
                        S.wait(g, "gp", "sP", (b["sP"] + ci - 1) if ci >= 2 else 0)
                        S.inc("sG", g.dma_start(out=wsl[ci % 2][:], in_=wqkv_h[2 * ND + ci]))

                park_done = {}
                _cnt = b["sQ"]
                for ci, (kind, i) in enumerate(order):
                    if kind == "od":
                        _cnt += 2
                        park_done[stage_cum[ci]] = None
                        park_done[stage_cum[ci] + 1] = None
                        _c2 = _cnt
                    if kind in ("od", "pa"):
                        for s in range(stage_cum[ci], stage_cum[ci + 1]):
                            _cnt += 1
                            park_done[s] = _cnt

                @blk.sync
                def _(sync):
                    for ci, (kind, i) in enumerate(order):
                        if kind == "od":
                            if i >= 1:
                                S.wait(sync, "sy", "sV", b["sV"] + vcum[2 * i])
                            S.inc("sQ", sync.dma_start(out=cosb[:], in_=cos_h[i]))
                            S.inc("sQ", sync.dma_start(out=sinb[:], in_=sin_h[i]))
                        for s in range(stage_cum[ci], stage_cum[ci + 1]):
                            S.wait(sync, "sy", "sV", b["sV"] + s + 1)
                            S.inc("sQ", sync.dma_start(out=ypark_h[stage_slots[s]],
                                                       in_=ystg[s % 2][:]))

                @blk.tensor
                def _(tensor):
                    for ci in range(ND):
                        S.wait(tensor, "te", "sG", 16 * (b["sG"] + ci + 1))
                        if ci >= 2:
                            S.wait(tensor, "te", "sA", b["sA"] + ci - 1)
                        S.inc("sP", emit_qkv_mms(tensor, psQ[ci % 2], wsl[ci % 2]))

                @blk.scalar
                def _(scalar):
                    for ci, (kind, i) in enumerate(order):
                        S.wait(scalar, "sc", "sP", b["sP"] + ci + 1)
                        bias = bq_s[:, 2 * ND + ci:2 * ND + ci + 1]
                        if kind == "ev":
                            S.wait(scalar, "sc", "sV", b["sV"] + vcum[ci])
                            S.inc("sA", scalar.activation(kev[:], psQ[ci % 2][:], AF.Identity,
                                                          bias=bias))
                        elif kind == "od":
                            S.inc("sA", scalar.activation(kod[:], psQ[ci % 2][:], AF.Identity,
                                                          bias=bias))
                        else:
                            S.wait(scalar, "sc", "sV", b["sV"] + vcum[ci])
                            S.inc("sA", scalar.activation(e1[:], psQ[ci % 2][:], AF.Identity,
                                                          bias=bias))

                @blk.vector
                def _(vector):
                    for ci, (kind, i) in enumerate(order):
                        if kind == "ev":
                            continue
                        S.wait(vector, "ve", "sA", b["sA"] + ci + 1)
                        if kind == "od":
                            S.wait(vector, "ve", "sQ", 16 * (b["sQ"] + 2 * (i + 1)))
                            s_ev = stage_cum[ci]
                            s_od = s_ev + 1
                            slot_ev, slot_od = i, NPC + i
                            for s_cur in (s_ev, s_od):
                                if s_cur >= 2:
                                    S.wait(vector, "ve", "sQ", 16 * park_done[s_cur - 2])
                            vector.tensor_tensor(e1[:], kev[:], cosb[:], ALU.mult)
                            vector.tensor_tensor(e2[:], kod[:], sinb[:], ALU.mult)
                            vector.tensor_tensor(e1[:], e1[:], e2[:], ALU.subtract)
                            S.inc("sV", vector.tensor_scalar(
                                ystg[s_ev % 2][:], e1[:], A_c[:, slot_ev:slot_ev + 1],
                                None, ALU.mult))
                            vector.tensor_tensor(e1[:], kod[:], cosb[:], ALU.mult)
                            vector.tensor_tensor(e2[:], kev[:], sinb[:], ALU.mult)
                            vector.tensor_tensor(e1[:], e1[:], e2[:], ALU.add)
                            S.inc("sV", vector.tensor_scalar(
                                ystg[s_od % 2][:], e1[:], A_c[:, slot_od:slot_od + 1],
                                None, ALU.mult))
                        else:
                            s_cur = stage_cum[ci]
                            slot = 2 * NPC + i
                            if s_cur >= 2:
                                S.wait(vector, "ve", "sQ", 16 * park_done[s_cur - 2])
                            S.inc("sV", vector.tensor_scalar(
                                ystg[s_cur % 2][:], e1[:], A_c[:, slot:slot + 1],
                                None, ALU.mult))
        S.check(b, sG=ND, sP=ND, sA=ND, sV=ND, sQ=2 * NPC + ND)
        xnT_stack.close()

        b = S.snap()
        with contextlib.ExitStack() as ph:
            wout_sb = ph.enter_context(nc.sbuf_tensor("wout_sb", [128, ND * D], BF16))
            boutr = ph.enter_context(nc.sbuf_tensor("boutr", [1, D], F32))
            yblk = [ph.enter_context(nc.sbuf_tensor(f"yblk{i}", [128, ND * 128], BF16))
                    for i in range(2)]
            outsb = [ph.enter_context(nc.sbuf_tensor(f"outsb{i}", [128, D], F32))
                     for i in range(2)]
            psY = [ph.enter_context(nc.psum_tensor(f"psY{i}", [128, D], F32)) for i in range(2)]
            with nc.Block("ph7") as blk:
                @blk.sync
                def _(sync):
                    S.inc("sQ", sync.dma_start(out=boutr[:], in_=bout_h))
                    for tg in range(NT):
                        S.wait(sync, "sy", "sP", (b["sP"] + tg - 1) if tg >= 2 else 0)
                        src_ap = ypark_h.rearrange("d p t -> p d t")[
                            :, :, tg * 128:(tg + 1) * 128]
                        S.inc("sQ", sync.dma_start(
                            out=yblk[tg % 2][:].rearrange("p (d t) -> p d t", d=ND),
                            in_=src_ap))

                @blk.gpsimd
                def _(g):
                    for dc in range(ND):
                        S.inc("sG", g.dma_start(out=wout_sb[:, dc * D:(dc + 1) * D],
                                                in_=wout_h[dc * 128:(dc + 1) * 128, :]))
                    S.wait(g, "gp", "sQ", 16 * wm_out_base)
                    for tg in range(NT):
                        S.wait(g, "gp", "sA", b["sA"] + tg + 1)
                        S.inc("sG", g.dma_start(out=out_h[tg * 128:(tg + 1) * 128, :],
                                                in_=outsb[tg % 2][:], accum_op=ALU.add))

                @blk.tensor
                def _(tensor):
                    S.wait(tensor, "te", "sG", 16 * (b["sG"] + ND))
                    S.wait(tensor, "te", "sQ", 16 * (b["sQ"] + 1))
                    for tg in range(NT):
                        S.wait(tensor, "te", "sQ", 16 * (b["sQ"] + 1 + tg + 1))
                        if tg >= 2:
                            S.wait(tensor, "te", "sA", b["sA"] + tg - 1)
                        last = None
                        for db in range(NDB):
                            for dc in range(ND):
                                tensor.matmul(
                                    psY[tg % 2][:, db * 512:(db + 1) * 512],
                                    yblk[tg % 2][:, dc * 128:(dc + 1) * 128],
                                    wout_sb[:, dc * D + db * 512: dc * D + db * 512 + 512],
                                    start=(dc == 0), stop=False)
                            last = tensor.matmul(
                                psY[tg % 2][:, db * 512:(db + 1) * 512],
                                ones[0:1, :], boutr[:, db * 512:(db + 1) * 512],
                                start=False, stop=True)
                        S.inc("sP", last)

                @blk.scalar
                def _(scalar):
                    for tg in range(NT):
                        S.wait(scalar, "sc", "sP", b["sP"] + tg + 1)
                        if tg >= 2:
                            S.wait(scalar, "sc", "sG", 16 * (b["sG"] + ND + tg - 1))
                        S.inc("sA", scalar.activation(outsb[tg % 2][:], psY[tg % 2][:],
                                                      AF.Identity))
        S.check(b, sQ=1 + NT, sG=ND + NT, sP=NT, sA=NT)

        rhb_stack = contextlib.ExitStack()
        rhb = rhb_stack.enter_context(nc.sbuf_tensor("rhb", [128, T], F32))
        gu_stack = contextlib.ExitStack()
        vgb = gu_stack.enter_context(nc.sbuf_tensor("vgb", [128, T], F32))
        vub = gu_stack.enter_context(nc.sbuf_tensor("vub", [128, T], F32))
        amaxfm = gu_stack.enter_context(nc.sbuf_tensor("amaxfm", [128, T], F32))
        amaxN = gu_stack.enter_context(nc.sbuf_tensor("amaxN", [128, T], F32))
        b = S.snap()
        with contextlib.ExitStack() as ph:
            vrow = ph.enter_context(nc.sbuf_tensor("vrow", [1, T], F32))
            sx_row = ph.enter_context(nc.sbuf_tensor("sx_row", [1, T], F32))
            psBC8 = ph.enter_context(nc.psum_tensor("psBC8", [128, T], F32))
            with nc.Block("ph7b") as blk:
                @blk.sync
                def _(sync):
                    S.inc("sQ", sync.dma_start(out=sx_row[:], in_=stat_st[0:NT, :]))

                @blk.vector
                def _(vector):
                    S.wait(vector, "ve", "sQ", 16 * (b["sQ"] + 1))
                    S.inc("sV", vector.tensor_scalar(vrow[:], sx_row[:], sw_b[0:1, 0:1],
                                                     None, ALU.mult))
                    S.wait(vector, "ve", "sP", b["sP"] + 1)
                    S.inc("sV", vector.tensor_copy(vgb[:], psBC8[:]))
                    S.inc("sV", vector.tensor_scalar(vrow[:], sx_row[:], sw_b[0:1, 1:2],
                                                     None, ALU.mult))
                    S.wait(vector, "ve", "sP", b["sP"] + 2)
                    S.inc("sV", vector.tensor_copy(vub[:], psBC8[:]))

                @blk.tensor
                def _(tensor):
                    for j in range(2):
                        S.wait(tensor, "te", "sV", b["sV"] + 1 + 2 * j)
                        last = None
                        for tb in range(NTB):
                            last = tensor.matmul(psBC8[:, tb * 512:(tb + 1) * 512],
                                                 ones[0:1, :],
                                                 vrow[:, tb * 512:(tb + 1) * 512],
                                                 start=True, stop=True)
                        S.inc("sP", last)
        S.check(b, sV=4, sP=2, sQ=1)

        qT_stack = contextlib.ExitStack()
        qT = qT_stack.enter_context(nc.sbuf_tensor("qT", [128, ND * T], BF16))
        b = S.snap()
        TH = T // 2
        NHB = max(1, TH // 512)
        HBW = min(TH, 512)
        vpos_tern = {}
        vpos = {}
        apos_tern = {}
        apos_sig = {}
        qpos = {}
        _v = b["sV"] + 1
        vpos_tern[0] = (_v + 1, _v + 2); _v += 2
        _a = b["sA"]
        apos_tern[0] = (_a + 1, _a + 2, _a + 3, _a + 4); _a += 4
        _q = b["sQ"] + 1
        qpos[("wg", 0)] = _q + 1
        qpos[("wu", 0)] = _q + 2
        _q += 2
        for oc in range(NO):
            if oc + 1 < NO:
                qpos[("wg", oc + 1)] = _q + 1
                qpos[("wu", oc + 1)] = _q + 2
                _q += 2
            qpos[("h", oc)] = _q + 1
            _q += 1
            if oc + 1 < NO:
                apos_tern[oc + 1] = (_a + 1, _a + 2, _a + 3, _a + 4); _a += 4
            apos_sig[(oc, 0)] = _a + 1
            apos_sig[(oc, 1)] = _a + 2
            _a += 2
            if oc + 1 < NO:
                vpos_tern[oc + 1] = (_v + 1, _v + 2); _v += 2
            for j, k in enumerate(("t1h0", "t2h0", "h0d", "t1h1", "t2h1", "h1d")):
                vpos[(oc, k)] = _v + 1 + j
            _v += 6
        pg8 = lambda oc, k: b["sP"] + 4 * oc + k
        with contextlib.ExitStack() as ph:
            wgs = [ph.enter_context(nc.sbuf_tensor(f"wgs{i}", [128, D], F32)) for i in range(2)]
            wus = [ph.enter_context(nc.sbuf_tensor(f"wus{i}", [128, D], F32)) for i in range(2)]
            teg = [ph.enter_context(nc.sbuf_tensor(f"teg{i}", [128, D], BF16)) for i in range(2)]
            teu = [ph.enter_context(nc.sbuf_tensor(f"teu{i}", [128, D], BF16)) for i in range(2)]
            zg0 = ph.enter_context(nc.sbuf_tensor("zg0", [128, D], F32))
            zg1 = ph.enter_context(nc.sbuf_tensor("zg1", [128, D], F32))
            hg = ph.enter_context(nc.sbuf_tensor("hg", [128, TH], F32))
            hs = ph.enter_context(nc.sbuf_tensor("hs", [128, TH], F32))
            hu = ph.enter_context(nc.sbuf_tensor("hu", [128, TH], F32))
            hsl = [ph.enter_context(nc.sbuf_tensor(f"hsl{i}", [128, T], F32)) for i in range(2)]
            psG = [ph.enter_context(nc.psum_tensor(f"psG{i}", [128, TH], F32)) for i in range(2)]
            psU = [ph.enter_context(nc.psum_tensor(f"psU{i}", [128, TH], F32)) for i in range(2)]

            def tern_acts(scalar, oc):
                S.wait(scalar, "sc", "sQ", 16 * qpos[("wg", oc)])
                if oc >= 1:
                    S.wait(scalar, "sc", "sV", vpos_tern[oc - 1][1])
                S.inc("sA", scalar.activation(zg0[:], wgs[oc % 2][:], AF.Sign,
                                              scale=rw_b[:, 0:1]))
                S.inc("sA", scalar.activation(zg1[:], wgs[oc % 2][:], AF.Square,
                                              scale=rw_b[:, 0:1]))
                S.wait(scalar, "sc", "sQ", 16 * qpos[("wu", oc)])
                S.wait(scalar, "sc", "sV", vpos_tern[oc][0])
                S.inc("sA", scalar.activation(zg0[:], wus[oc % 2][:], AF.Sign,
                                              scale=rw_b[:, 1:2]))
                S.inc("sA", scalar.activation(zg1[:], wus[oc % 2][:], AF.Square,
                                              scale=rw_b[:, 1:2]))

            def tern_dve(vector, oc):
                S.wait(vector, "ve", "sA", apos_tern[oc][1])
                if oc >= 2:
                    S.wait(vector, "ve", "sP", pg8(oc - 2, 4))
                vector.tensor_scalar(zg1[:], zg1[:], 0.25, None, ALU.is_ge)
                S.inc("sV", vector.tensor_tensor(teg[oc % 2][:], zg0[:], zg1[:], ALU.mult))
                S.wait(vector, "ve", "sA", apos_tern[oc][3])
                vector.tensor_scalar(zg1[:], zg1[:], 0.25, None, ALU.is_ge)
                S.inc("sV", vector.tensor_tensor(teu[oc % 2][:], zg0[:], zg1[:], ALU.mult))

            with nc.Block("ph8") as blk:
                @blk.sync
                def _(sync):
                    S.inc("sQ", sync.dma_start(
                        out=qT[:].rearrange("p (d t) -> p d t", d=ND),
                        in_=qpark_h.rearrange("d p t -> p d t")))
                    S.inc("sQ", sync.dma_start(out=wgs[0][:], in_=wg_h[0]))
                    S.inc("sQ", sync.dma_start(out=wus[0][:], in_=wu_h[0]))
                    for oc in range(NO):
                        if oc + 1 < NO:
                            S.wait(sync, "sy", "sA", apos_tern[oc - 1][1] if oc >= 1 else 0)
                            S.inc("sQ", sync.dma_start(out=wgs[(oc + 1) % 2][:],
                                                       in_=wg_h[oc + 1]))
                            S.wait(sync, "sy", "sA", apos_tern[oc - 1][3] if oc >= 1 else 0)
                            S.inc("sQ", sync.dma_start(out=wus[(oc + 1) % 2][:],
                                                       in_=wu_h[oc + 1]))
                        S.wait(sync, "sy", "sV", vpos[(oc, "h1d")])
                        S.inc("sQ", sync.dma_start(out=hscr_h[oc], in_=hsl[oc % 2][:]))

                @blk.tensor
                def _(tensor):
                    S.wait(tensor, "te", "sQ", 16 * (b["sQ"] + 1))
                    for oc in range(NO):
                        for (m, hf) in ((0, 0), (1, 0), (0, 1), (1, 1)):
                            tern = teg[oc % 2] if m == 0 else teu[oc % 2]
                            ps = (psG if m == 0 else psU)[hf]
                            S.wait(tensor, "te", "sV", vpos_tern[oc][m])
                            if oc >= 1:
                                rel = {(0, 0): "t1h0", (1, 0): "t2h0",
                                       (0, 1): "t1h1", (1, 1): "t2h1"}[(m, hf)]
                                S.wait(tensor, "te", "sV", vpos[(oc - 1, rel)])
                            last = None
                            for dc in range(ND):
                                for hb in range(NHB):
                                    off = dc * T + hf * TH + hb * HBW
                                    last = tensor.matmul(
                                        ps[:, hb * HBW:(hb + 1) * HBW],
                                        tern[:, dc * 128:(dc + 1) * 128],
                                        qT[:, off:off + HBW],
                                        start=(dc == 0), stop=(dc == ND - 1))
                            S.inc("sP", last)

                @blk.scalar
                def _(scalar):
                    _preload_tables(scalar, nc, AF.Sigmoid, AF.Identity, AF.Square, AF.Sign)
                    tern_acts(scalar, 0)
                    for oc in range(NO):
                        if oc + 1 < NO:
                            tern_acts(scalar, oc + 1)
                        S.wait(scalar, "sc", "sV", vpos[(oc, "t1h0")])
                        S.inc("sA", scalar.activation(hs[:], hg[:], AF.Sigmoid))
                        S.wait(scalar, "sc", "sV", vpos[(oc, "t1h1")])
                        S.inc("sA", scalar.activation(hs[:], hg[:], AF.Sigmoid))

                @blk.vector
                def _(vector):
                    vector.memset(amaxfm[:], 0.0)
                    S.inc("sV", vector.memset(amaxN[:], 0.0))
                    tern_dve(vector, 0)
                    for oc in range(NO):
                        if oc + 1 < NO:
                            tern_dve(vector, oc + 1)
                        for hf in range(2):
                            S.wait(vector, "ve", "sP", pg8(oc, 2 * hf + 1))
                            S.inc("sV", vector.tensor_tensor(
                                hg[:], psG[hf][:], vgb[:, hf * TH:(hf + 1) * TH], ALU.mult))
                            S.wait(vector, "ve", "sP", pg8(oc, 2 * hf + 2))
                            S.inc("sV", vector.tensor_tensor(
                                hu[:], psU[hf][:], vub[:, hf * TH:(hf + 1) * TH], ALU.mult))
                            S.wait(vector, "ve", "sA", apos_sig[(oc, hf)])
                            if oc >= 2:
                                S.wait(vector, "ve", "sQ", 16 * qpos[("h", oc - 2)])
                            sl = slice(hf * TH, (hf + 1) * TH)
                            vector.tensor_tensor(hsl[oc % 2][:, sl], hs[:], hu[:], ALU.mult)
                            vector.tensor_max(amaxfm[:, sl], amaxfm[:, sl],
                                              hsl[oc % 2][:, sl])
                            S.inc("sV", vector.tensor_tensor(
                                amaxN[:, sl], amaxN[:, sl], hsl[oc % 2][:, sl], ALU.min))
                    vector.tensor_scalar(amaxN[:], amaxN[:], -1.0, None, ALU.mult)
                    S.inc("sV", vector.tensor_max(amaxfm[:], amaxfm[:], amaxN[:]))
        S.check(b, sQ=1 + 3 * NO, sP=4 * NO, sA=6 * NO, sV=2 + 8 * NO)
        qT_stack.close()

        b = S.snap()
        with contextlib.ExitStack() as ph:
            ps9 = [ph.enter_context(nc.psum_tensor(f"ps9{i}", [128, 128], F32)) for i in range(2)]
            ps9b = ph.enter_context(nc.psum_tensor("ps9b", [NT, 128], F32))
            rh_row = ph.enter_context(nc.sbuf_tensor("rh_row", [1, T], F32))
            lh9 = ph.enter_context(nc.sbuf_tensor("lh9", [128, NT], F32))
            psBC = ph.enter_context(nc.psum_tensor("psBC", [128, T], F32))
            with nc.Block("ph9") as blk:
                @blk.tensor
                def _(tensor):
                    for tg in range(NT):
                        if tg >= 2:
                            S.wait(tensor, "te", "sV", b["sV"] + tg - 1)
                        S.inc("sP", tensor.transpose(ps9[tg % 2][:],
                                                     amaxfm[:, tg * 128:(tg + 1) * 128],
                                                     eyef[:]))
                    S.wait(tensor, "te", "sA", b["sA"] + 1)
                    S.inc("sP", tensor.transpose(ps9b[:], rh_tok[:], eyef[:]))
                    S.wait(tensor, "te", "sQ", 16 * (b["sQ"] + 1))
                    last = None
                    for tb in range(NTB):
                        last = tensor.matmul(psBC[:, tb * 512:(tb + 1) * 512], ones[0:1, :],
                                             rh_row[:, tb * 512:(tb + 1) * 512],
                                             start=True, stop=True)
                    S.inc("sP", last)

                @blk.scalar
                def _(scalar):
                    _preload_tables(scalar, nc, AF.Ln, AF.Exp, AF.Identity, AF.Square,
                                    AF.Sign)
                    S.wait(scalar, "sc", "sV", b["sV"] + NT + 1)
                    scalar.activation(lh9[:], sh_sc[:], AF.Ln)
                    S.inc("sA", scalar.activation(rh_tok[:], lh9[:], AF.Exp,
                                                  scale=cst[:, 2:3]))

                @blk.vector
                def _(vector):
                    for tg in range(NT):
                        S.wait(vector, "ve", "sP", b["sP"] + tg + 1)
                        S.inc("sV", vector.reduce_max(sh_sc[:, tg:tg + 1],
                                                      ps9[tg % 2][:], axis=AX.X))
                    S.fence(vector, "ve", vector.engine_nop())
                    if True:
                        pass
                    S.fence(vector, "ve", vector.tensor_scalar(
                        sh_sc[:], sh_sc[:], 1e-5, 1.0 / 127.0, ALU.max, ALU.mult))
                    S.inc("sV", vector.tensor_scalar(vd_tok[:], sh_sc[:], sw_b[:, 2:3], None,
                                                     ALU.mult))
                    S.wait(vector, "ve", "sP", b["sP"] + NT + 1)
                    S.inc("sV", vector.tensor_copy(stat_st[0:NT, :], ps9b[:]))
                    S.wait(vector, "ve", "sP", b["sP"] + NT + 2)
                    S.inc("sV", vector.tensor_copy(rhb[:], psBC[:]))

                @blk.sync
                def _(sync):
                    S.wait(sync, "sy", "sV", b["sV"] + NT + 2)
                    S.inc("sQ", sync.dma_start(out=rh_row[:], in_=stat_st[0:NT, :]))
        S.check(b, sP=NT + 2, sV=NT + 3, sQ=1, sA=1)
        gu_stack.close()

        b = S.snap()
        with contextlib.ExitStack() as ph:
            wdt = [[ph.enter_context(nc.sbuf_tensor(f"wdt{i}_{ol}", [128, D], BF16))
                    for ol in range(NOQ)] for i in range(2)]
            hread = [ph.enter_context(nc.sbuf_tensor("hread0", [128, T], F32))] * 2
            wds = [ph.enter_context(nc.sbuf_tensor("wds0", [128, D], F32))] * 2
            hqs = [ph.enter_context(nc.sbuf_tensor("hqs0", [128, T], BF16))] * 2
            hqb = [ph.enter_context(nc.sbuf_tensor(f"hqb{i}", [128, NOQ * 128], BF16))
                   for i in range(2)]
            BD = max(D, T)
            b1 = ph.enter_context(nc.sbuf_tensor("b1d", [128, BD], BF16))
            b2 = ph.enter_context(nc.sbuf_tensor("b2d", [128, BD], F32))
            q1 = ph.enter_context(nc.sbuf_tensor("q1d", [128, T // 2], F32))
            q2 = ph.enter_context(nc.sbuf_tensor("q2d", [128, T // 2], F32))
            hq32 = ph.enter_context(nc.sbuf_tensor("hq32", [128, T // 2],
                                                   mybir.dt.int32))
            outd = [ph.enter_context(nc.sbuf_tensor(f"outd{i}", [128, D], F32))
                    for i in range(2)]
            psD = [ph.enter_context(nc.psum_tensor(f"psD{i}", [128, D], F32)) for i in range(2)]

            pairs = [(q_, tg) for q_ in range(NQ) for tg in range(NT)]
            qv = {}
            _q = b["sQ"]
            for j in range(NO):
                _q += 1; qv[("h", j)] = _q
                if j >= 1:
                    _q += 1; qv[("st", j - 1)] = _q
            _q += 1; qv[("st", NO - 1)] = _q
            vv = {}
            _v = b["sV"]
            for j in range(NO):
                _v += 1; vv[("tern", j)] = _v
                _v += 1; vv[("quant", j)] = _v
            sched = []
            _cur = {"bi": 0, "ei": 0}

            def _drain_step(jq, max_emit):
                emitted = 0
                while emitted < max_emit:
                    b_el = (_cur["bi"] < len(pairs)
                            and pairs[_cur["bi"]][0] < jq)
                    e_el = (_cur["ei"] < len(pairs)
                            and pairs[_cur["ei"]][0] < jq)
                    if b_el and _cur["bi"] < _cur["ei"] + 3:
                        sched.append(("blk", _cur["bi"]))
                        _cur["bi"] += 1
                    elif e_el:
                        sched.append(("ev", _cur["ei"]))
                        _cur["ei"] += 1
                    else:
                        break
                    emitted += 1

            sched.append(("wd", 0))
            for j in range(NO):
                _drain_step(j // NOQ - 1, 10 ** 9)
                sched.append(("act", j))
                if j + 1 < NO:
                    sched.append(("wd", j + 1))
                _drain_step(j // NOQ, 2)
            _drain_step(NQ, 10 ** 9)
            av = {}
            _a = b["sA"]
            sblk = {}
            wdpos = {}
            _s = b["sS"]
            for kind, idx in sched:
                if kind == "act":
                    _a += 1; av[("sg", idx)] = _a
                    _a += 1; av[("sq", idx)] = _a
                elif kind == "ev":
                    _a += 1; av[("ev",) + pairs[idx]] = _a
                elif kind == "wd":
                    _s += 1; wdpos[idx] = _s
                else:
                    _s += 1; sblk[idx] = _s
            pv = {}
            _p = b["sP"]
            for bi, pr in enumerate(pairs):
                _p += 1; pv[pr] = _p
            gacc = {}
            _g = b["sG"]
            for bi in range(len(pairs)):
                _g += 1; gacc[bi] = _g

            with nc.Block("ph10") as blk:
                @blk.sync
                def _(sync):
                    for j in range(NO):
                        if j >= 1:
                            S.wait(sync, "sy", "sV", vv[("quant", j - 1)])
                        S.inc("sQ", sync.dma_start(out=hread[j % 2][:], in_=hscr_h[j]))
                        if j >= 1:
                            S.inc("sQ", sync.dma_start(out=hq_hbm[j - 1],
                                                       in_=hqs[(j - 1) % 2][:]))
                    S.wait(sync, "sy", "sV", vv[("quant", NO - 1)])
                    S.inc("sQ", sync.dma_start(out=hq_hbm[NO - 1],
                                               in_=hqs[(NO - 1) % 2][:]))

                @blk.gpsimd
                def _(g):
                    for bi, (q2, tg2) in enumerate(pairs):
                        S.wait(g, "gp", "sA", av[("ev", q2, tg2)])
                        if q2 >= 1:
                            S.wait(g, "gp", "sG", 16 * gacc[(q2 - 1) * NT + tg2])
                        elif bi == 0:
                            S.wait(g, "gp", "sG", 16 * b["sG"])
                        S.inc("sG", g.dma_start(out=out_h[tg2 * 128:(tg2 + 1) * 128, :],
                                                in_=outd[tg2 % 2][:], accum_op=ALU.add))
                    g.wait_ge(S.sems["sG"], 16 * S.cnt["sG"])

                @blk.scalar
                def _(scalar):
                    ei = 0
                    bi = 0

                    def emit_blk():
                        nonlocal bi
                        q3, tg3 = pairs[bi]
                        S.wait(scalar, "sc", "sQ",
                               16 * qv[("st", (q3 + 1) * NOQ - 1)])
                        if bi >= 2:
                            S.wait(scalar, "sc", "sP", pv[pairs[bi - 2]])
                        src_ap = hq_hbm.rearrange("o p t -> p o t")[
                            :, q3 * NOQ:(q3 + 1) * NOQ, tg3 * 128:(tg3 + 1) * 128]
                        S.inc("sS", scalar.dma_start(
                            out=hqb[bi % 2][:].rearrange("p (o t) -> p o t", o=NOQ),
                            in_=src_ap))
                        bi += 1

                    def emit_ev():
                        nonlocal ei
                        q2, tg2 = pairs[ei]
                        S.wait(scalar, "sc", "sP", pv[(q2, tg2)])
                        gi2 = q2 * NT + tg2
                        if gi2 >= 2:
                            S.wait(scalar, "sc", "sG", 16 * gacc[gi2 - 2])
                        S.inc("sA", scalar.activation(
                            outd[tg2 % 2][:], psD[tg2 % 2][:], AF.Identity,
                            scale=vd_tok[:, tg2:tg2 + 1]))
                        ei += 1

                    for kind, idx in sched:
                        if kind == "wd":
                            jj = idx
                            if jj >= 1:
                                S.wait(scalar, "sc", "sA", av[("sq", jj - 1)])
                            S.inc("sS", scalar.dma_start(out=wds[jj % 2][:],
                                                         in_=wd_h[jj]))
                        elif kind == "act":
                            j = idx
                            S.wait(scalar, "sc", "sS", 16 * wdpos[j])
                            if j >= 1:
                                S.wait(scalar, "sc", "sV",
                                       vv[("tern", j - 1)])
                            S.inc("sA", scalar.activation(b1[:, 0:D], wds[j % 2][:],
                                                          AF.Sign,
                                                          scale=rw_b[:, 2:3]))
                            S.inc("sA", scalar.activation(b2[:, 0:D], wds[j % 2][:],
                                                          AF.Square,
                                                          scale=rw_b[:, 2:3]))
                        elif kind == "blk":
                            assert idx == bi
                            emit_blk()
                        else:
                            assert idx == ei
                            emit_ev()

                @blk.vector
                def _(vector):
                    for j in range(NO):
                        q_ = j // NOQ
                        ol = j % NOQ
                        S.wait(vector, "ve", "sA", av[("sq", j)])
                        if q_ >= 2:
                            S.wait(vector, "ve", "sP", pv[(q_ - 2, NT - 1)])
                        vector.tensor_scalar(b2[:, 0:D], b2[:, 0:D], 0.25, None, ALU.is_ge)
                        S.inc("sV", vector.tensor_tensor(wdt[q_ % 2][ol][:], b1[:, 0:D],
                                                         b2[:, 0:D], ALU.mult))
                        S.wait(vector, "ve", "sQ", 16 * qv[("h", j)])
                        if j >= 1:
                            S.wait(vector, "ve", "sQ",
                                   16 * qv[("st", j - 1)])
                        TH2 = T // 2
                        last_q = None
                        for hh in range(2):
                            sl = slice(hh * TH2, (hh + 1) * TH2)
                            vector.tensor_tensor(q1[:], hread[j % 2][:, sl],
                                                 rhb[:, sl], ALU.mult)
                            vector.tensor_scalar(q1[:], q1[:], 256.0, None, ALU.add)
                            vector.tensor_copy(hq32[:], q1[:])
                            vector.tensor_copy(q2[:], hq32[:])
                            vector.tensor_tensor(q1[:], q1[:], q2[:], ALU.subtract)
                            vector.tensor_scalar(q1[:], q1[:], 0.5, None, ALU.is_ge)
                            last_q = vector.scalar_tensor_tensor(
                                hqs[j % 2][:, sl], q1[:], -256.0, q2[:],
                                ALU.add, ALU.add)
                        S.inc("sV", last_q)

                @blk.tensor
                def _(tensor):
                    for bi, (q_, tg) in enumerate(pairs):
                        S.wait(tensor, "te", "sV", vv[("tern", (q_ + 1) * NOQ - 1)])
                        S.wait(tensor, "te", "sS", 16 * sblk[bi])
                        gi2 = q_ * NT + tg
                        if gi2 >= 2:
                            S.wait(tensor, "te", "sA", av[("ev", *pairs[gi2 - 2])])
                        last = None
                        for ol in range(NOQ):
                            for db in range(NDB):
                                last = tensor.matmul(
                                    psD[tg % 2][:, db * 512:(db + 1) * 512],
                                    hqb[bi % 2][:, ol * 128:(ol + 1) * 128],
                                    wdt[q_ % 2][ol][:, db * 512:(db + 1) * 512],
                                    start=(ol == 0), stop=(ol == NOQ - 1))
                        S.inc("sP", last)
        S.check(b, sQ=2 * NO, sA=2 * NO + NQ * NT, sV=2 * NO, sP=NQ * NT,
                sG=NQ * NT, sS=NQ * NT + NO)
        rhb_stack.close()

    return nc


def _tile_slabs(w_t, nslab):
    rows, K = w_t.shape
    assert rows == nslab * 128 and K % 128 == 0
    nd = K // 128
    a = w_t.reshape(nslab, 128, nd, 128)
    return np.ascontiguousarray(a.transpose(0, 3, 2, 1).reshape(nslab, 128, K))


def _rope_tables(T, rot):
    try:
        import jax
        import jax.numpy as jnp
        with jax.default_device(jax.devices("cpu")[0]):
            inv = 1.0 / 10000.0 ** (jnp.arange(0, rot, 2, dtype=jnp.float32) / rot)
            freqs = jnp.arange(T, dtype=jnp.float32)[:, None] * inv[None, :]
            c = np.asarray(jnp.cos(freqs), dtype=np.float32)
            s = np.asarray(jnp.sin(freqs), dtype=np.float32)
        return c, s
    except Exception:
        a = np.arange(0, rot, 2, dtype=np.float32) / np.float32(rot)
        inv = (np.float32(1.0) / (np.float32(10000.0) ** a)).astype(np.float32)
        freqs = np.arange(T, dtype=np.float32)[:, None] * inv[None, :]
        return (np.cos(freqs).astype(np.float32), np.sin(freqs).astype(np.float32))


def _perm(D):
    rot = D // 2
    return np.concatenate([np.arange(0, rot, 2), np.arange(1, rot, 2),
                           np.arange(rot, D)])


def host_prep(c, inputs):
    D, T, DFF = c.D, c.T, c.DFF
    ND, NPC, NO = c.ND, c.NPC, c.NO
    W_in = np.asarray(inputs["L0_attn_W_in"], dtype=np.float32)
    b_in = np.asarray(inputs["L0_attn_b_in"], dtype=np.float32)
    W_out = np.asarray(inputs["L0_attn_W_out"], dtype=np.float32)
    b_out = np.asarray(inputs["L0_attn_b_out"], dtype=np.float32)
    w_gate = np.asarray(inputs["L0_ffn_w_gate"], dtype=np.float32)
    w_up = np.asarray(inputs["L0_ffn_w_up"], dtype=np.float32)
    w_down = np.asarray(inputs["L0_ffn_w_down"], dtype=np.float32)

    perm = _perm(D)
    Wq = W_in[0:D][perm]
    Wk = W_in[D:2 * D][perm]
    Wv = W_in[2 * D:3 * D][perm]
    bq = b_in[0:D][perm]
    bk = b_in[D:2 * D][perm]
    bv = b_in[2 * D:3 * D][perm]

    def kvq_rows(Wm):
        chunks = Wm.reshape(ND, 128, D)
        order = []
        for i in range(NPC):
            order.append(i)
            order.append(NPC + i)
        order += list(range(2 * NPC, ND))
        return chunks[order].reshape(D, D)

    def kvq_bias(bm):
        chunks = bm.reshape(ND, 128)
        order = []
        for i in range(NPC):
            order.append(i)
            order.append(NPC + i)
        order += list(range(2 * NPC, ND))
        return chunks[order].reshape(D)

    rows = np.concatenate([kvq_rows(Wk), Wv, kvq_rows(Wq)], axis=0)
    bias = np.concatenate([kvq_bias(bk), bv, kvq_bias(bq)])

    w_qkv_t = _tile_slabs(rows, 3 * ND)
    bqkv_t = np.ascontiguousarray(bias.reshape(3 * ND, 128).T)

    w_out_t = np.ascontiguousarray(W_out[:, perm].T)
    b_out_row = np.ascontiguousarray(b_out.reshape(1, D))

    w_gate_t = _tile_slabs(w_gate, NO)
    w_up_t = _tile_slabs(w_up, NO)
    w_down_t = np.ascontiguousarray(w_down.T.reshape(NO, 128, D))

    rot = D // 2
    cos32, sin32 = _rope_tables(T, rot)
    cos_t = np.ascontiguousarray(cos32.T.reshape(NPC, 128, T))
    sin_t = np.ascontiguousarray(sin32.T.reshape(NPC, 128, T))

    eyef = np.eye(128, dtype=np.float32)

    shared = dict(w_qkv_t=w_qkv_t, bqkv_t=bqkv_t, w_out_t=w_out_t,
                  b_out_row=b_out_row, w_gate_t=w_gate_t, w_up_t=w_up_t,
                  w_down_t=w_down_t, cos_t=cos_t, sin_t=sin_t, eyef=eyef)
    return shared


def kernel(L0_attn_X, L0_attn_W_in, L0_attn_b_in, L0_attn_W_out, L0_attn_b_out,
           L0_ffn_w_gate, L0_ffn_w_up, L0_ffn_w_down):
    X = np.ascontiguousarray(np.asarray(L0_attn_X, dtype=np.float32))
    B, T, D = X.shape
    DFF = L0_ffn_w_gate.shape[0]
    c = Cfg(T=T, D=D, DFF=DFF)
    shared = host_prep(c, dict(
        L0_attn_W_in=L0_attn_W_in, L0_attn_b_in=L0_attn_b_in,
        L0_attn_W_out=L0_attn_W_out, L0_attn_b_out=L0_attn_b_out,
        L0_ffn_w_gate=L0_ffn_w_gate, L0_ffn_w_up=L0_ffn_w_up,
        L0_ffn_w_down=L0_ffn_w_down))
    nc = build_program(c)
    in_maps = [dict(shared, x=np.ascontiguousarray(X[b_])) for b_ in range(B)]
    res = run_bass_kernel_spmd(nc, in_maps, core_ids=list(range(N_CORES)))
    return np.stack([np.asarray(res.results[b_]["out"]) for b_ in range(B)], axis=0)
